# version 14
# baseline (speedup 1.0000x reference)
import numpy as np
import ml_dtypes

import concourse.bass as bass
import concourse.mybir as mybir
from concourse.bass_utils import run_bass_kernel_spmd
from contextlib import ExitStack
from types import SimpleNamespace

BFNP = ml_dtypes.bfloat16
F32 = mybir.dt.float32
BF16 = mybir.dt.bfloat16
AF = mybir.ActivationFunctionType
ALU = mybir.AluOpType
AX = mybir.AxisListType

NCORES = 8
S = 4
L = 2048
D = 1024
TT = L // 128
IC = D // 128
DH = 4096
LN_EPS = 1e-5
HEAD_EPS = 1024.0 * 1e-6

ENGS = ("sync", "gpsimd", "tensor", "vector", "scalar")
SELF_SEM = {"tensor": "pe", "scalar": "sc", "vector": "ve", "gpsimd": "pl"}


class Trk:

    def __init__(self, nc):
        self.nc = nc
        self.q = {e: [] for e in ENGS}
        self.sem = {}
        self.val = {}
        self.mark = {}

    def add_sem(self, name, handle):
        self.sem[name] = handle
        self.val[name] = 0

    def op(self, eng, emit, deps=(), inc=None, inc_by=1):
        for dep in deps:
            if dep is None:
                continue
            s, v = dep
            if v <= 0 or s == SELF_SEM.get(eng):
                continue
            k = (eng, s)
            if self.mark.get(k, 0) < v:
                self.mark[k] = v
                self.q[eng].append(lambda e, h=self.sem[s], v=v: e.wait_ge(h, v))
        own = SELF_SEM.get(eng)
        if eng in ("vector", "scalar"):
            vb = self.val[own]
            if vb > 0:
                self.q[eng].append(lambda e, h=self.sem[own], v=vb: e.wait_ge(h, v))
            inc = own
        if inc is None:
            self.q[eng].append(emit)
            return None
        self.val[inc] += inc_by
        self.q[eng].append(lambda e, emit=emit, h=self.sem[inc], k=inc_by: emit(e).then_inc(h, k))
        return (inc, self.val[inc])

    def tok(self, name):
        return (name, self.val[name])

    def flush(self):
        qs, self.q = self.q, {e: [] for e in ENGS}
        live = [e for e in ENGS if qs[e]]
        if not live:
            return
        with self.nc.Block() as blk:
            for e in live:
                getattr(blk, e)(lambda eng, ops=qs[e]: [f(eng) for f in ops])
        for name in self.sem:
            if not name.startswith("d"):
                for eng in ENGS:
                    k = (eng, name)
                    self.mark[k] = max(self.mark.get(k, 0), self.val[name])


def new_ctx(nc, es, dstreams):
    C = SimpleNamespace()
    C.nc = nc
    C.t = Trk(nc)
    for s in ("pe", "sc", "ve", "pl"):
        C.t.add_sem(s, es.enter_context(nc.semaphore(s)))
    for s in dstreams:
        C.t.add_sem("d_" + s, es.enter_context(nc.semaphore("d_" + s)))
    C.war = {}
    C.x_tok = {}
    return C


def dma(C, eng, stream, out, in_, deps=()):
    return C.t.op(eng, lambda e, o=out, i=in_: e.dma_start(out=o, in_=i),
                  deps=deps, inc="d_" + stream, inc_by=16)


COMMON_STREAMS = ["x", "xs", "c", "pp32a", "pp32b", "pp32c", "st0", "st1", "st2"]



def alloc_common(C, es, nc):
    C.x_sb = es.enter_context(nc.sbuf_tensor("x_sb", [128, TT * 1024], F32))
    C.xn_tm = es.enter_context(nc.sbuf_tensor("xn_tm", [128, 2 * 1024], BF16))
    C.stats = es.enter_context(nc.sbuf_tensor("stats", [128, 80], F32))
    C.ident = es.enter_context(nc.sbuf_tensor("identsb", [128, 128], BF16))
    C.epsc = es.enter_context(nc.sbuf_tensor("epsc", [128, 2], F32))
    C.P = [es.enter_context(nc.psum_tensor(f"pb{i}", [128, 512], F32)) for i in range(8)]


def load_consts(C, d_ident, extra=()):
    dma(C, "sync", "c", C.ident[:], d_ident)
    for sb, dv in extra:
        dma(C, "sync", "c", sb, dv)
    C.const_tok = C.t.tok("d_c")
    C.ident_tok = C.const_tok
    C.t.op("vector", lambda e: e.memset(C.epsc[:, 0:1], LN_EPS))
    C.eps_tok = C.t.op("vector", lambda e: e.memset(C.epsc[:, 1:2], HEAD_EPS), inc="ve")


def load_x(C, d_x, s):
    dep = C.war.get(("x_store", 0))
    for tt in range(TT):
        dma(C, "sync", "x", C.x_sb[:, tt * 1024:(tt + 1) * 1024],
            d_x[s, tt * 128:(tt + 1) * 128, :], deps=[dep])
    wave = C.t.tok("d_x")
    for tt in range(TT):
        C.x_tok[tt] = wave


def store_x(C, d_xo, s):
    for tt in range(TT):
        dma(C, "gpsimd", "xs", d_xo[s, tt * 128:(tt + 1) * 128, :],
            C.x_sb[:, tt * 1024:(tt + 1) * 1024], deps=[C.x_tok[tt]])
    C.war[("x_store", 0)] = C.t.tok("d_xs")


def ln_to_fm(C, tts, dst, ncols, col_of, war_key=0):
    t = C.t
    n_t = len(tts)
    st = C.stats
    mean_all = st[:, 32:32 + n_t]
    var_all = st[:, 48:48 + n_t]
    nb_all = st[:, 64:64 + n_t]
    st_free = C.war.get(("stats", 0))
    for n, tt in enumerate(tts):
        b = n % 2
        xt = C.x_sb[:, tt * 1024:(tt + 1) * 1024]
        bn = st[:, b * 12:(b + 1) * 12]
        grp = bn.rearrange("p (c k) -> p k c", k=3)
        sm, sm2 = st[:, 24 + b: 25 + b], st[:, 26 + b: 27 + b]
        m4 = st[:, 28 + b * 2: 30 + b * 2] if False else st[:, 28:32]
        xdep = C.x_tok.get(tt)
        t.op("vector", lambda e, o=bn, i=xt: e.bn_stats(o[:, 0:6], i[:, 0:512]),
             deps=[xdep, st_free] if n == 0 else [xdep])
        t.op("vector", lambda e, o=bn, i=xt: e.bn_stats(o[:, 6:12], i[:, 512:1024]))
        t.op("vector", lambda e, o=sm, g=grp: e.reduce_sum(o, g[:, 1:2, :], axis=AX.X))
        t.op("vector", lambda e, o=st, g=grp: e.tensor_mul(o[:, 28:32], g[:, 1:2, :], g[:, 1:2, :]))
        t.op("vector", lambda e, o=sm2, i=st: e.reduce_sum(o, i[:, 28:32], axis=AX.X))
        t.op("vector", lambda e, o=mean_all, n=n, i=sm: e.tensor_scalar_mul(o[:, n:n + 1], i, 0.25))
        t.op("vector", lambda e, o=sm, g=grp: e.reduce_sum(o, g[:, 2:3, :], axis=AX.X))
        t.op("vector", lambda e, o=sm, i=sm: e.tensor_scalar_mul(o, i, 1.0 / 1024.0))
        t.op("vector", lambda e, o=sm2, i=sm2: e.tensor_scalar_mul(o, i, 0.25))
        t.op("vector", lambda e, o=sm, i=sm2: e.tensor_add(o, o, i))
        t.op("vector", lambda e, o=sm2, i=mean_all, n=n: e.tensor_mul(o, i[:, n:n + 1], i[:, n:n + 1]))
        t.op("vector", lambda e, o=var_all, n=n, a=sm, bb=sm2: e.tensor_sub(o[:, n:n + 1], a, bb))
    tve = t.op("vector", lambda e: e.engine_nop(), inc="ve")
    tsd = t.op("scalar", lambda e, o=var_all, i=var_all, ep=C.epsc[:, 0:1]:
               e.activation(o, i, AF.Sqrt, bias=ep),
               deps=[tve, C.eps_tok], inc="sc")
    t.op("vector", lambda e, o=var_all, i=var_all: e.reciprocal(o, i), deps=[tsd])
    tnb = t.op("vector", lambda e, o=nb_all, i=mean_all, r=var_all:
               e.tensor_tensor(o, i, r, ALU.mult), inc="ve")
    tnb = t.op("vector", lambda e, o=nb_all, i=nb_all: e.tensor_scalar_mul(o, i, -1.0), inc="ve")
    C.war[("stats", 0)] = tnb
    C.lncol = col_of
    C.lndst_war = C.war.get(("lnfm", war_key))
    dstv = dst[:].rearrange("p (c n) -> p c n", n=ncols)
    pend = None
    last = None
    for n, tt in enumerate(tts):
        b = n % 2
        xt = C.x_sb[:, tt * 1024:(tt + 1) * 1024]
        xn = C.xn_tm[:, b * 1024:(b + 1) * 1024]
        xn_free = C.war.get(("xn_tm", b))
        taf = t.op("scalar", lambda e, o=xn, i=xt, r=var_all, bb=nb_all, n=n:
                   e.activation(o, i, AF.Identity, scale=r[:, n:n + 1], bias=bb[:, n:n + 1]),
                   deps=[tnb, xn_free, C.x_tok.get(tt)], inc="sc")
        tpes = []
        for half in range(2):
            bank = 4 + b * 2 + half
            pb = C.P[bank]
            bank_free = C.war.get(("pbank", bank))
            tpe = None
            for j in range(4):
                ic = half * 4 + j
                tpe = t.op("tensor",
                           lambda e, p=pb, j=j, i=xn, ic=ic, idt=C.ident:
                           e.matmul(p[:, j * 128:(j + 1) * 128], i[:, ic * 128:(ic + 1) * 128], idt[:],
                                    start=True, stop=True),
                           deps=[taf, bank_free, C.ident_tok], inc="pe")
            tpes.append((bank, tpe))
        if pend is not None:
            last = _ln_evac(C, dstv, ncols, pend)
        pend = (tt, b, tpes)
    last = _ln_evac(C, dstv, ncols, pend)
    C.war[("stats", 0)] = last
    return last


def _ln_evac(C, dstv, ncols, pend):
    t = C.t
    tt, b, tpes = pend
    co_fn = pend
    last = None
    for half, (bank, tpe) in enumerate(tpes):
        pb = C.P[bank]
        tev = t.op("scalar",
                   lambda e, p=pb, dv=dstv, h=half, co=C.lncol(tt):
                   e.copy(dv[:, h * 4:(h + 1) * 4, co:co + 128],
                          p[:].rearrange("p (c n) -> p c n", n=128)),
                   deps=[tpe, C.lndst_war], inc="sc")
        C.war[("pbank", bank)] = tev
        C.war[("xn_tm", b)] = tev
        last = tev
    return last



def prepass_transpose(C, es, nc, jobs):
    t = C.t
    with ExitStack() as ps:
        maxI = max(j["I"] for j in jobs)
        wnat32 = [ps.enter_context(nc.sbuf_tensor(f"pp_w32{b}", [128, maxI], F32)) for b in range(3)]
        wnat = [ps.enter_context(nc.sbuf_tensor(f"pp_wbf{b}", [128, maxI], BF16)) for b in range(3)]
        asm = [ps.enter_context(nc.sbuf_tensor(f"pp_asm{b}", [128, maxI], BF16)) for b in range(3)]
        for job in jobs:
            O, I = job["O"], job["I"]
            src_, dst = job["src"], job["dst"]
            for oc in range(O // 128):
                b = oc % 3
                wb = wnat[b]
                w32 = wnat32[b]
                dma(C, "sync" if b != 1 else "gpsimd", "pp32%s" % "abc"[b], w32[:, 0:I],
                    src_[oc * 128:(oc + 1) * 128, :], deps=[C.war.get(("pp_w32", b))])
                tl = t.tok("d_pp32%s" % "abc"[b])
                wb_free = C.war.get(("pp_wbf", b))
                if job.get("sign"):
                    tc = t.op("scalar", lambda e, o=wb, i=w32, I=I: e.activation(o[:, 0:I], i[:, 0:I], AF.Sign),
                              deps=[tl, wb_free], inc="sc")
                elif job.get("scale") is not None:
                    scol = job["scale"][:, oc:oc + 1]
                    tc = t.op("scalar", lambda e, o=wb, i=w32, sl=scol, I=I:
                              e.activation(o[:, 0:I], i[:, 0:I], AF.Identity, scale=sl),
                              deps=[tl, wb_free, job.get("scale_tok")], inc="sc")
                else:
                    eng_c = "scalar" if oc % 2 == 0 else "vector"
                    tc = t.op(eng_c, lambda e, o=wb, i=w32, I=I:
                              (e.copy(o[:, 0:I], i[:, 0:I]) if hasattr(e, "copy")
                               else e.tensor_copy(o[:, 0:I], i[:, 0:I])),
                              deps=[tl, wb_free], inc="sc" if eng_c == "scalar" else "ve")
                C.war[("pp_w32", b)] = tc
                st_stream = "st%d" % (oc % 3)
                asm_free = C.war.get(("pp_asm", b))
                if job["kind"] == "cast":
                    dma(C, "sync", st_stream, dst[oc * 128:(oc + 1) * 128, :], wb[:, 0:I],
                        deps=[tc])
                    C.war[("pp_wbf", b)] = t.tok("d_" + st_stream)
                    continue
                tev = None
                nic = I // 128
                for g4 in range(0, nic, 4):
                    pbank = (g4 // 4) % 4
                    pb = C.P[pbank]
                    kk = min(4, nic - g4)
                    tpe = None
                    for j in range(kk):
                        ic = g4 + j
                        tpe = t.op("tensor",
                                   lambda e, p=pb, j=j, w=wb, ic=ic, idt=C.ident:
                                   e.matmul(p[:, j * 128:(j + 1) * 128], w[:, ic * 128:(ic + 1) * 128], idt[:],
                                            start=True, stop=True),
                                   deps=[tc, C.ident_tok, C.war.get(("pp_pb", pbank)), asm_free], inc="pe")
                    tev = t.op("vector", lambda e, o=asm[b], g4=g4, p=pb, kk=kk:
                               e.tensor_copy(o[:, g4 * 128: (g4 + kk) * 128], p[:, 0:kk * 128]),
                               deps=[tpe], inc="ve")
                    C.war[("pp_pb", pbank)] = tev
                C.war[("pp_wbf", b)] = tev
                if job["kind"] == "panel":
                    out_ap = dst[oc]
                    in_ap = asm[b][:, 0:I].rearrange("p (c o) -> p c o", o=128)
                else:
                    out_ap = dst[:, oc * 128:(oc + 1) * 128].rearrange("(c p) o -> p c o", p=128)
                    in_ap = asm[b][:, 0:I].rearrange("p (c o) -> p c o", o=128)
                dma(C, "sync", st_stream, out_ap, in_ap, deps=[tev])
                C.war[("pp_asm", b)] = t.tok("d_" + st_stream)
        done = [t.tok("d_st0"), t.tok("d_st1"), t.tok("d_st2")]
        t.flush()
    fence = done + [t.tok("d_pp32a"), t.tok("d_pp32b"), t.tok("d_pp32c")]
    for eng in ENGS:
        t.op(eng, lambda e: e.nop(), deps=fence)
    t.flush()
    C.prepass_done = done
    return done



def build_l1_ffn():
    nc = bass.Bass("TRN2", target_bir_lowering=False)
    d_x = nc.dram_tensor("x_in", [S, L, D], F32, kind="ExternalInput").ap()
    d_w = nc.dram_tensor("w", [D, D], F32, kind="ExternalInput").ap()
    d_ident = nc.dram_tensor("ident", [128, 128], BF16, kind="ExternalInput").ap()
    d_xo = nc.dram_tensor("x_out", [S, L, D], F32, kind="ExternalOutput").ap()
    d_swt = nc.dram_tensor("swt", [D, D], BF16, kind="Internal").ap()

    with ExitStack() as es:
        C = new_ctx(nc, es, COMMON_STREAMS + ["swt"])
        alloc_common(C, es, nc)
        load_consts(C, d_ident)
        pp = prepass_transpose(C, es, nc, [dict(
            src=d_w, dst=d_swt, kind="plain", O=D, I=D, sign=True)])

        with ExitStack() as ms:
            swt_sb = ms.enter_context(nc.sbuf_tensor("swt_sb", [128, IC * 1024], BF16))
            xn_g = ms.enter_context(nc.sbuf_tensor("xn_g", [128, IC * 512], BF16))
            xn_gB = ms.enter_context(nc.sbuf_tensor("xn_gB", [128, IC * 512], BF16))
            t = C.t
            swt_ld = None
            for s in range(S):
                load_x(C, d_x, s)
                if s == 0:
                    dma(C, "sync", "swt", swt_sb[:].rearrange("p (c o) -> p c o", o=1024),
                        d_swt.rearrange("(c p) o -> p c o", p=128), deps=pp)
                    swt_ld = t.tok("d_swt")
                C.war[("pB2dma", 0)] = swt_ld
                xgs = [xn_g, xn_gB]
                xn_next = ln_to_fm(C, [0, 1, 2, 3], xgs[0], 512, lambda tt: tt * 128, war_key=0)
                for tg in range(4):
                    xg = xgs[tg % 2]
                    xn_tok = xn_next
                    if tg < 3:
                        nts = list(range((tg + 1) * 4, (tg + 1) * 4 + 4))
                        xn_next = ln_to_fm(C, nts, xgs[(tg + 1) % 2], 512,
                                           lambda tt, tg=tg: (tt - (tg + 1) * 4) * 128,
                                           war_key=(tg + 1) % 2)
                    for tl in range(4):
                        tt = tg * 4 + tl
                        for og in range(2):
                            bank = tl * 2 + og
                            pb = C.P[bank]
                            bank_free = C.war.get(("pbank", bank))
                            tpe = None
                            for ic in range(IC):
                                tpe = t.op("tensor",
                                           lambda e, p=pb, ic=ic, tl=tl, og=og, xg=xg, w=swt_sb:
                                           e.matmul(p[:], xg[:, ic * 512 + tl * 128: ic * 512 + (tl + 1) * 128],
                                                    w[:, ic * 1024 + og * 512: ic * 1024 + (og + 1) * 512],
                                                    start=(ic == 0), stop=(ic == IC - 1)),
                                           deps=[xn_tok, swt_ld, bank_free] if ic == 0 else [],
                                           inc="pe" if ic == IC - 1 else None)
                            xslc = C.x_sb[:, tt * 1024 + og * 512: tt * 1024 + (og + 1) * 512]
                            tve = t.op("vector", lambda e, o=xslc, p=pb: e.tensor_add(o, o, p[:]),
                                       deps=[tpe, C.x_tok[tt]], inc="ve")
                            C.war[("pbank", bank)] = tve
                            C.x_tok[tt] = tve
                    C.war[("lnfm", tg % 2)] = t.tok("pe")
                for k_ in (("lnfm", 0), ("lnfm", 1), ("pB1", 0), ("pB2", 0), ("pWE", 0)):
                    C.war[k_] = t.tok("pe")
                store_x(C, d_xo, s)
                t.flush()
            t.op("sync", lambda e: e.wait_ge(C.t.sem["d_xs"], C.t.val["d_xs"]))
            t.flush()
    return nc



def build_l0_ffn():
    nc = bass.Bass("TRN2", target_bir_lowering=False)
    d_x = nc.dram_tensor("x_in", [S, L, D], F32, kind="ExternalInput").ap()
    d_w1 = nc.dram_tensor("w1", [DH, D], F32, kind="ExternalInput").ap()
    d_w2 = nc.dram_tensor("w2", [DH, D], F32, kind="ExternalInput").ap()
    d_h1 = nc.dram_tensor("h1", [DH, DH], F32, kind="ExternalInput").ap()
    d_h2 = nc.dram_tensor("h2", [DH, DH], F32, kind="ExternalInput").ap()
    d_w3 = nc.dram_tensor("w3", [D, DH], F32, kind="ExternalInput").ap()
    d_s1 = nc.dram_tensor("s1rs", [128, 32], F32, kind="ExternalInput").ap()
    d_s2 = nc.dram_tensor("s2rs", [128, 32], F32, kind="ExternalInput").ap()
    d_h1s = nc.dram_tensor("h1rs", [128, 32], F32, kind="ExternalInput").ap()
    d_h2s = nc.dram_tensor("h2rs", [128, 32], F32, kind="ExternalInput").ap()
    d_w3s = nc.dram_tensor("w3rs", [128, 8], F32, kind="ExternalInput").ap()
    d_ident = nc.dram_tensor("ident", [128, 128], BF16, kind="ExternalInput").ap()
    d_xo = nc.dram_tensor("x_out", [S, L, D], F32, kind="ExternalOutput").ap()
    d_w1t = nc.dram_tensor("w1t", [DH // 128, 128, D // 128, 128], BF16, kind="Internal").ap()
    d_w2t = nc.dram_tensor("w2t", [DH // 128, 128, D // 128, 128], BF16, kind="Internal").ap()
    d_h1t = nc.dram_tensor("h1t", [DH // 128, 128, DH // 128, 128], BF16, kind="Internal").ap()
    d_h2t = nc.dram_tensor("h2t", [DH // 128, 128, DH // 128, 128], BF16, kind="Internal").ap()
    d_w3t = nc.dram_tensor("w3t", [DH, D], BF16, kind="Internal").ap()

    with ExitStack() as es:
        C = new_ctx(nc, es, COMMON_STREAMS + ["w12a", "w12b", "ha", "hb", "w3a", "w3b"])
        alloc_common(C, es, nc)
        scol = es.enter_context(nc.sbuf_tensor("scol", [128, 4 * 32 + 8], F32))
        s1c, s2c, h1c, h2c = (scol[:, i * 32:(i + 1) * 32] for i in range(4))
        w3c = scol[:, 128:136]
        load_consts(C, d_ident, extra=[(s1c, d_s1), (s2c, d_s2), (h1c, d_h1s),
                                       (h2c, d_h2s), (w3c, d_w3s)])
        scol_tok = C.const_tok
        pp = prepass_transpose(C, es, nc, [
            dict(src=d_w1, dst=d_w1t, kind="panel", O=DH, I=D),
            dict(src=d_w2, dst=d_w2t, kind="panel", O=DH, I=D),
            dict(src=d_h1, dst=d_h1t, kind="panel", O=DH, I=DH),
            dict(src=d_h2, dst=d_h2t, kind="panel", O=DH, I=DH),
            dict(src=d_w3, dst=d_w3t, kind="plain", O=D, I=DH,
                 scale=w3c, scale_tok=scol_tok),
        ])

        with ExitStack() as ms:
            xn_g = ms.enter_context(nc.sbuf_tensor("xn_g", [128, IC * 512], BF16))
            xn_gB = ms.enter_context(nc.sbuf_tensor("xn_gB", [128, IC * 512], BF16))
            tA = ms.enter_context(nc.sbuf_tensor("tA", [128, 32 * 512], BF16))
            tB = ms.enter_context(nc.sbuf_tensor("tB", [128, 32 * 512], BF16))
            tsil = ms.enter_context(nc.sbuf_tensor("tsil", [128, 2 * 512], BF16))
            w12b = ms.enter_context(nc.sbuf_tensor("w12sb", [128, 4 * 1024], BF16))
            hbuf = ms.enter_context(nc.sbuf_tensor("hbsb", [128, 2 * 4096], BF16))
            t = C.t

            def h_stage(src, dst, panel_scratch, scale_cols, name, src_tok, dst_war=None, dst_war2=None):
                tsc = None
                for oc in range(32):
                    slot = oc % 2
                    pb_free = C.war.get(("hbuf", slot))
                    dma(C, "sync", "h%s" % "ab"[slot],
                        hbuf[:, slot * 4096:(slot + 1) * 4096].rearrange("p (c o) -> p c o", o=128),
                        panel_scratch[oc], deps=[pb_free, C.war.get(("pWE", 0))] + pp)
                    tl_ = t.tok("d_h%s" % "ab"[slot])
                    bank = oc % 4
                    pb = C.P[bank]
                    bank_free = C.war.get(("pbank", bank))
                    tpe = None
                    for ic in range(32):
                        tpe = t.op("tensor",
                                   lambda e, p=pb, sl=slot, ic=ic, x=src:
                                   e.matmul(p[:], hbuf[:, sl * 4096 + ic * 128: sl * 4096 + (ic + 1) * 128],
                                            x[:, ic * 512:(ic + 1) * 512],
                                            start=(ic == 0), stop=(ic == 31)),
                                   deps=([tl_, bank_free, src_tok]) if ic == 0 else [],
                                   inc="pe" if ic == 31 else None)
                    C.war[("hbuf", slot)] = tpe
                    tsc = t.op("scalar",
                               lambda e, p=pb, oc=oc, dd=dst, sc=scale_cols:
                               e.activation(dd[:, oc * 512:(oc + 1) * 512], p[:], AF.Gelu, scale=sc[:, oc:oc + 1]),
                               deps=[tpe, scol_tok, dst_war, dst_war2], inc="sc")
                    C.war[("pbank", bank)] = tsc
                return t.tok("sc")

            for s in range(S):
                load_x(C, d_x, s)
                xgs = [xn_g, xn_gB]
                xn_next = ln_to_fm(C, [0, 1, 2, 3], xgs[0], 512, lambda tt: tt * 128, war_key=0)
                for tg in range(4):
                    xg = xgs[tg % 2]
                    xn_tok = xn_next
                    for oc in range(32):
                        slot = oc % 2
                        w_free = C.war.get(("w12sb", slot))
                        dma(C, "sync", "w12%s" % "ab"[slot],
                            w12b[:, slot * 2048: slot * 2048 + 1024].rearrange("p (c o) -> p c o", o=128),
                            d_w1t[oc], deps=[w_free, C.war.get(("pWE", 0))] + pp)
                        dma(C, "sync", "w12%s" % "ab"[slot],
                            w12b[:, slot * 2048 + 1024: slot * 2048 + 2048].rearrange("p (c o) -> p c o", o=128),
                            d_w2t[oc])
                        tlw = t.tok("d_w12%s" % "ab"[slot])
                        ba, bb_ = (oc % 2) * 2, (oc % 2) * 2 + 1
                        pa, pb_ = C.P[ba], C.P[bb_]
                        fa, fb = C.war.get(("pbank", ba)), C.war.get(("pbank", bb_))
                        for ic in range(IC):
                            t.op("tensor",
                                 lambda e, p=pa, sl=slot, ic=ic:
                                 e.matmul(p[:], w12b[:, sl * 2048 + ic * 128: sl * 2048 + (ic + 1) * 128],
                                          xg[:, ic * 512:(ic + 1) * 512], start=(ic == 0), stop=(ic == IC - 1)),
                                 deps=[tlw, fa, xn_tok] if ic == 0 else [],
                                 inc="pe" if ic == IC - 1 else None)
                        tpa = t.tok("pe")
                        for ic in range(IC):
                            t.op("tensor",
                                 lambda e, p=pb_, sl=slot, ic=ic:
                                 e.matmul(p[:], w12b[:, sl * 2048 + 1024 + ic * 128: sl * 2048 + 1024 + (ic + 1) * 128],
                                          xg[:, ic * 512:(ic + 1) * 512], start=(ic == 0), stop=(ic == IC - 1)),
                                 deps=[fb] if ic == 0 else [],
                                 inc="pe" if ic == IC - 1 else None)
                        tpb = t.tok("pe")
                        C.war[("w12sb", slot)] = tpb
                        sl_ap = tsil[:, slot * 512:(slot + 1) * 512]
                        sl_free = C.war.get(("tsil", slot))
                        tss = t.op("scalar",
                                   lambda e, o=sl_ap, p=pa, oc=oc:
                                   e.activation(o, p[:], AF.Silu, scale=s1c[:, oc:oc + 1]),
                                   deps=[tpa, scol_tok, sl_free], inc="sc")
                        C.war[("pbank", ba)] = tss
                        tvv = t.op("vector",
                                   lambda e, o=tA, oc=oc, p=pb_, sl=sl_ap:
                                   e.scalar_tensor_tensor(o[:, oc * 512:(oc + 1) * 512], p[:],
                                                          s2c[:, oc:oc + 1], sl, ALU.mult, ALU.mult),
                                   deps=[tpb, tss, C.war.get(("tAkey", 0)), C.war.get(("pB1", 0))], inc="ve")
                        C.war[("pbank", bb_)] = tvv
                        C.war[("tsil", slot)] = tvv
                    tA_tok = t.tok("ve")
                    C.war[("lnfm", tg % 2)] = t.tok("pe")
                    if tg < 3:
                        nts = list(range((tg + 1) * 4, (tg + 1) * 4 + 4))
                        xn_next = ln_to_fm(C, nts, xgs[(tg + 1) % 2], 512,
                                           lambda tt, tg=tg: (tt - (tg + 1) * 4) * 128,
                                           war_key=(tg + 1) % 2)
                    g1_tok = h_stage(tA, tB, d_h1t, h1c, "h1", tA_tok,
                                     dst_war=C.war.get(("w3tb", 0)), dst_war2=C.war.get(("pB2", 0)))
                    g2_tok = h_stage(tB, tA, d_h2t, h2c, "h2", g1_tok)
                    w3read = None
                    for og in range(2):
                        dma(C, "sync", "w3%s" % "ab"[og],
                            tB[:].rearrange("p (c o) -> p c o", o=512),
                            d_w3t[:, og * 512:(og + 1) * 512].rearrange("(c p) o -> p c o", p=128),
                            deps=[g2_tok, w3read] + pp)
                        tl3 = t.tok("d_w3%s" % "ab"[og])
                        for tl_i in range(4):
                            bank = og * 4 + tl_i
                            pb = C.P[bank]
                            bank_free = C.war.get(("pbank", bank))
                            tpe = None
                            for ic in range(32):
                                tpe = t.op("tensor",
                                           lambda e, p=pb, ic=ic, tl_i=tl_i:
                                           e.matmul(p[:], tA[:, ic * 512 + tl_i * 128: ic * 512 + (tl_i + 1) * 128],
                                                    tB[:, ic * 512:(ic + 1) * 512],
                                                    start=(ic == 0), stop=(ic == 31)),
                                           deps=[tl3, bank_free, g2_tok] if ic == 0 else [],
                                           inc="pe" if ic == 31 else None)
                            tt = tg * 4 + tl_i
                            xslc = C.x_sb[:, tt * 1024 + og * 512: tt * 1024 + (og + 1) * 512]
                            tve = t.op("vector", lambda e, o=xslc, p=pb: e.tensor_add(o, o, p[:]),
                                       deps=[tpe, C.x_tok[tt]], inc="ve")
                            C.war[("pbank", bank)] = tve
                            C.x_tok[tt] = tve
                        w3read = t.tok("pe")
                    C.war[("tAkey", 0)] = t.tok("pe")
                    C.war[("w3tb", 0)] = t.tok("pe")
                    t.flush()
                for k_ in (("lnfm", 0), ("lnfm", 1), ("pB1", 0), ("pB2", 0), ("pWE", 0)):
                    C.war[k_] = t.tok("pe")
                store_x(C, d_xo, s)
                t.flush()
            t.op("sync", lambda e: e.wait_ge(C.t.sem["d_xs"], C.t.val["d_xs"]))
            t.flush()
    return nc



def build_l0_attn():
    nc = bass.Bass("TRN2", target_bir_lowering=False)
    d_x = nc.dram_tensor("x_in", [S, L, D], F32, kind="ExternalInput").ap()
    d_wqk = nc.dram_tensor("wqk", [D, 2048], F32, kind="ExternalInput").ap()
    d_wv = nc.dram_tensor("wv", [D, D], F32, kind="ExternalInput").ap()
    d_wo = nc.dram_tensor("wo", [D, D], F32, kind="ExternalInput").ap()
    d_aff = nc.dram_tensor("aff", [128, 4], F32, kind="ExternalInput").ap()
    d_ident = nc.dram_tensor("ident", [128, 128], BF16, kind="ExternalInput").ap()
    d_masks = nc.dram_tensor("masks", [4, 128, 512], BF16, kind="ExternalInput").ap()
    d_xo = nc.dram_tensor("x_out", [S, L, D], F32, kind="ExternalOutput").ap()
    d_wqk16 = nc.dram_tensor("wqk16", [D, 2048], BF16, kind="Internal").ap()
    d_wv16 = nc.dram_tensor("wv16", [D, D], BF16, kind="Internal").ap()
    d_wo16 = nc.dram_tensor("wo16", [D, D], BF16, kind="Internal").ap()

    with ExitStack() as es:
        C = new_ctx(nc, es, COMMON_STREAMS + ["wqk", "wv", "wo"])
        alloc_common(C, es, nc)
        masks = es.enter_context(nc.sbuf_tensor("maskssb", [128, 4 * 512], BF16))
        affc = es.enter_context(nc.sbuf_tensor("affc", [128, 4], F32))
        c32 = es.enter_context(nc.sbuf_tensor("c32", [1, 64], BF16))
        t = C.t
        load_consts(C, d_ident,
                    extra=[(masks[:, j * 512:(j + 1) * 512], d_masks[j]) for j in range(4)]
                          + [(affc[:], d_aff)])
        const_tok = C.const_tok
        pp = prepass_transpose(C, es, nc, [
            dict(src=d_wqk, dst=d_wqk16, kind="cast", O=D, I=2048),
            dict(src=d_wv, dst=d_wv16, kind="cast", O=D, I=D),
            dict(src=d_wo, dst=d_wo16, kind="cast", O=D, I=D),
        ])
        ms = es.enter_context(ExitStack())
        xn_fm = ms.enter_context(nc.sbuf_tensor("xn_fm", [128, IC * L], BF16))
        qk_fm = ms.enter_context(nc.sbuf_tensor("qk_fm", [128, 4 * L], BF16))
        v1 = ms.enter_context(nc.sbuf_tensor("v1", [128, TT * 260], BF16))
        aout = ms.enter_context(nc.sbuf_tensor("aout", [128, 8 * L], BF16))
        qn0 = ms.enter_context(nc.sbuf_tensor("qn0", [128, 4 * 512], BF16))
        et = ms.enter_context(nc.sbuf_tensor("et", [128, 8 * 512], BF16))
        rb = ms.enter_context(nc.sbuf_tensor("rb", [128, 512], F32))
        rr = ms.enter_context(nc.sbuf_tensor("rr", [1, 1024], BF16))
        wqk_sb = ms.enter_context(nc.sbuf_tensor("wqk_sb", [128, IC * 512], BF16))
        wv_sb = ms.enter_context(nc.sbuf_tensor("wv_sb", [128, IC * 256], BF16))
        wo_sb = ms.enter_context(nc.sbuf_tensor("wo_sb", [128, IC * 512], BF16))
        hstat = ms.enter_context(nc.sbuf_tensor("hstat", [128, 2 * 96], F32))
        t.op("vector", lambda e: e.memset(c32[:], 1.0 / 32.0))
        c32_tok = t.op("vector", lambda e: e.memset(v1[:], 1.0), inc="ve")

        for s in range(S):
            load_x(C, d_x, s)
            xn_tok = ln_to_fm(C, list(range(TT)), xn_fm, L, lambda tt: tt * 128)
            v1_rd = C.war.get(("v1", 0))
            for hg in range(4):
                wqk_free = C.war.get(("wqk_sb", 0))
                for ic in range(IC):
                    dma(C, "sync", "wqk", wqk_sb[:, ic * 512:(ic + 1) * 512],
                        d_wqk16[ic * 128:(ic + 1) * 128, hg * 512:(hg + 1) * 512],
                        deps=[wqk_free] + pp)
                wqk_tok = t.tok("d_wqk")
                wv_free = C.war.get(("wv_sb", 0))
                for ic in range(IC):
                    dma(C, "sync", "wv", wv_sb[:, ic * 256:(ic + 1) * 256],
                        d_wv16[ic * 128:(ic + 1) * 128, hg * 256:(hg + 1) * 256],
                        deps=[wv_free] + pp)
                wv_tok = t.tok("d_wv")
                def emit_qtrans(p_):
                    tt_, q4_, tseg_ = p_
                    tb = 3 + tt_ % 2
                    pb2 = C.P[tb]
                    b2_free = C.war.get(("pbank", tb))
                    qn = qn0[:, q4_ * 512:(q4_ + 1) * 512]
                    tpt = None
                    for j in range(4):
                        tpt = t.op("tensor",
                                   lambda e, p=pb2, j=j, i=qn:
                                   e.matmul(p[:, j * 128:(j + 1) * 128], i[:, j * 128:(j + 1) * 128], C.ident[:],
                                            start=True, stop=True),
                                   deps=[tseg_, b2_free, C.ident_tok], inc="pe")
                    qv = qk_fm[:].rearrange("p (c n) -> p c n", n=L)
                    pv3 = pb2[:].rearrange("p (c n) -> p c n", n=128)
                    qfm_free = C.war.get(("pB2dma", 0))
                    tqe = None
                    for cix in range(4):
                        isq = (cix % 2 == 0)
                        tqe = t.op("scalar",
                                   lambda e, qv=qv, pv3=pv3, cix=cix, tt=tt_, isq=isq:
                                   e.activation(qv[:, cix, tt * 128:(tt + 1) * 128],
                                                pv3[:, cix, :], AF.Identity,
                                                scale=affc[:, 0:1] if isq else affc[:, 1:2],
                                                bias=affc[:, 2:3] if isq else affc[:, 3:4]),
                                   deps=[tpt, const_tok, qfm_free], inc="sc")
                    C.war[("pbank", tb)] = tqe
                    C.war[("qn0", q4_)] = tqe
                    return tqe

                pend_q = []
                pend2 = None

                def emit_segs(p2):
                    tt_, b_, q4_, pb_ = p2["tt"], p2["b"], p2["q4"], p2["pb"]
                    st_ = hstat[:, b_ * 96:(b_ + 1) * 96]
                    mean8_ = st_[:, 48:56]; var8_ = st_[:, 80:88]; r8_ = st_[:, 88:96]
                    trc = t.op("vector", lambda e, o=r8_, i=var8_: e.reciprocal(o, i), deps=[p2["tsd"]], inc="ve")
                    qn = qn0[:, q4_ * 512:(q4_ + 1) * 512]
                    qn_free = C.war.get(("qn0", q4_))
                    tseg = None
                    for sg in range(8):
                        tseg = t.op("vector",
                                    lambda e, o=qn, i=pb_, sg=sg, m=mean8_, r=r8_:
                                    e.tensor_scalar(o[:, sg * 64:(sg + 1) * 64], i[:, sg * 64:(sg + 1) * 64],
                                                    m[:, sg:sg + 1], r[:, sg:sg + 1], ALU.subtract, ALU.mult),
                                    deps=[trc, qn_free, C.war.get(("pB2dma", 0))] if sg == 0 else [],
                                    inc="ve" if sg == 7 else None)
                    C.war[("hstat", b_)] = tseg
                    C.war[("pbank", p2["b3"])] = tseg
                    pend_q.append((tt_, q4_, tseg))
                    if len(pend_q) > 2:
                        emit_qtrans(pend_q.pop(0))

                for tt in range(TT):
                    b3 = tt % 3
                    b = tt % 2
                    q4 = tt % 4
                    pb = C.P[b3]
                    bank_free = C.war.get(("pbank", b3))
                    tpe = None
                    for ic in range(IC):
                        tpe = t.op("tensor",
                                   lambda e, p=pb, ic=ic, tt=tt:
                                   e.matmul(p[:], xn_fm[:, ic * L + tt * 128: ic * L + (tt + 1) * 128],
                                            wqk_sb[:, ic * 512:(ic + 1) * 512],
                                            start=(ic == 0), stop=(ic == IC - 1)),
                                   deps=[xn_tok, wqk_tok, bank_free] if ic == 0 else [],
                                   inc="pe" if ic == IC - 1 else None)
                    st = hstat[:, b * 96:(b + 1) * 96]
                    v6 = st[:, 0:48].rearrange("p (c k) -> p k c", k=6)
                    mean8 = st[:, 48:56]; sM28 = st[:, 56:64]
                    t1 = st[:, 64:72]; t2 = st[:, 72:80]
                    var8 = st[:, 80:88]
                    st_free = C.war.get(("hstat", b))
                    for sg8 in range(8):
                        t.op("vector", lambda e, o=st, i=pb, sg8=sg8:
                             e.bn_stats(o[:, sg8 * 6: sg8 * 6 + 6], i[:, sg8 * 64:(sg8 + 1) * 64]),
                             deps=[tpe, st_free] if sg8 == 0 else [])
                    t.op("vector", lambda e, o=mean8, v=v6: e.tensor_add(o, v[:, 1:2, :], v[:, 4:5, :]))
                    t.op("vector", lambda e, o=mean8, i=mean8: e.tensor_scalar_mul(o, i, 0.5))
                    t.op("vector", lambda e, o=t1, v=v6: e.tensor_mul(o, v[:, 1:2, :], v[:, 1:2, :]))
                    t.op("vector", lambda e, o=t2, v=v6: e.tensor_mul(o, v[:, 4:5, :], v[:, 4:5, :]))
                    t.op("vector", lambda e, o=t1, i=t2: e.tensor_add(o, o, i))
                    t.op("vector", lambda e, o=t1, i=t1: e.tensor_scalar_mul(o, i, 0.5))
                    t.op("vector", lambda e, o=sM28, v=v6: e.tensor_add(o, v[:, 2:3, :], v[:, 5:6, :]))
                    t.op("vector", lambda e, o=sM28, i=sM28: e.tensor_scalar_mul(o, i, 1.0 / 64.0))
                    t.op("vector", lambda e, o=sM28, i=t1: e.tensor_add(o, o, i))
                    t.op("vector", lambda e, o=t2, i=mean8: e.tensor_mul(o, i, i))
                    tvv = t.op("vector", lambda e, o=var8, a=sM28, bb=t2: e.tensor_sub(o, a, bb), inc="ve")
                    tsd = t.op("scalar", lambda e, o=var8, i=var8:
                               e.activation(o, i, AF.Sqrt, bias=C.epsc[:, 1:2]),
                               deps=[tvv, C.eps_tok], inc="sc")
                    if pend2 is not None:
                        emit_segs(pend2)
                    pend2 = dict(tt=tt, b=b, q4=q4, pb=pb, b3=b3, tsd=tsd)
                    vbank = 5 + tt % 3
                    vpb = C.P[vbank]
                    vbank_free = C.war.get(("pbank", vbank))
                    tpeV = None
                    for ic in range(IC):
                        tpeV = t.op("tensor",
                                    lambda e, p=vpb, ic=ic, tt=tt:
                                    e.matmul(p[:, 0:256], xn_fm[:, ic * L + tt * 128: ic * L + (tt + 1) * 128],
                                             wv_sb[:, ic * 256:(ic + 1) * 256],
                                             start=(ic == 0), stop=(ic == IC - 1)),
                                    deps=[xn_tok, wv_tok, vbank_free, c32_tok, v1_rd] if ic == 0 else [],
                                    inc="pe" if ic == IC - 1 else None)
                    tve = t.op("scalar",
                               lambda e, p=vpb, tt=tt:
                               e.activation(v1[:, tt * 260: tt * 260 + 260].rearrange("p (h dd) -> p h dd", dd=65)[:, :, 0:64],
                                            p[:, 0:256].rearrange("p (h dd) -> p h dd", dd=64), AF.Identity, scale=1.0 / 32.0),
                               deps=[tpeV], inc="sc")
                    C.war[("pbank", vbank)] = tve
                emit_segs(pend2)
                pend2 = None
                for p_ in pend_q:
                    emit_qtrans(p_)
                pend_q = []
                qk_ready = t.tok("sc")
                v_ready = qk_ready
                def emit_norm(p_):
                    tbc = t.op("tensor", lambda e, p=C.P[4], sl=p_["slot"]: e.matmul(p[0:64, :], c32[:, 0:64],
                                                                                    rr[0:1, sl * 512:(sl + 1) * 512],
                                                                                    start=True, stop=True),
                               deps=[p_["trr"], C.war.get(("pbank", 4)), c32_tok], inc="pe")
                    C.war[("rr", p_["slot"])] = tbc
                    trb = t.op("scalar", lambda e, p=C.P[4]: e.copy(rb[0:64, :], p[0:64, :]),
                               deps=[tbc, C.war.get(("rb", 0))], inc="sc")
                    C.war[("pbank", 4)] = trb
                    h_, qg_ = p_["h"], p_["qg"]
                    hp_, hs_ = h_ // 2, h_ % 2
                    tao = t.op("vector",
                               lambda e, p=p_["pvb"], hp=hp_, hs2=hs_, qg=qg_:
                               e.tensor_mul(aout[hs2 * 64:(hs2 + 1) * 64, hp * L + qg * 512: hp * L + (qg + 1) * 512],
                                            p[0:64, :], rb[0:64, :]),
                               deps=[trb, C.war.get(("aoutw", 0))], inc="ve")
                    C.war[("pbank", p_["bank"])] = tao
                    C.war[("rb", 0)] = tao

                pends = []
                for pr in range(2):
                    qch, kch = 2 * pr, 2 * pr + 1
                    for qg in range(4):
                        pvbank = [6, 7]
                        pvb = [C.P[6], C.P[7]]
                        pv_free = None
                        n_kt = 4 * qg + 4
                        tpv = [None, None]
                        for b0 in range(0, n_kt, 4):
                            kts = range(b0, min(b0 + 4, n_kt))
                            etoks = {}
                            for kt in kts:
                                for hsub in range(2):
                                    sbank = (0, 1, 2, 3)[(2 * kt + hsub) % 4]
                                    psb = C.P[sbank]
                                    ps_free = C.war.get(("pbank", sbank))
                                    tps = t.op("tensor",
                                               lambda e, p=psb, kch=kch, qch=qch, hsub=hsub, kt=kt, qg=qg:
                                               e.matmul(p[:], qk_fm[hsub * 64:(hsub + 1) * 64, kch * L + kt * 128: kch * L + (kt + 1) * 128],
                                                        qk_fm[hsub * 64:(hsub + 1) * 64, qch * L + qg * 512: qch * L + (qg + 1) * 512],
                                                        start=True, stop=True),
                                               deps=[qk_ready, ps_free], inc="pe")
                                    eslot = (2 * kt + hsub) % 8
                                    e_free = C.war.get(("et", eslot))
                                    tex = t.op("scalar",
                                               lambda e, p=psb, es_=eslot:
                                               e.activation(et[:, es_ * 512:(es_ + 1) * 512], p[:], AF.Exp, scale=1.0 / 64.0),
                                               deps=[tps, e_free], inc="sc")
                                    C.war[("pbank", sbank)] = tex
                                    etok = tex
                                    if kt >= 4 * qg:
                                        j = kt - 4 * qg
                                        etok = t.op("vector",
                                                    lambda e, es_=eslot, j=j:
                                                    e.tensor_mul(et[:, es_ * 512:(es_ + 1) * 512],
                                                                 et[:, es_ * 512:(es_ + 1) * 512],
                                                                 masks[:, j * 512:(j + 1) * 512]),
                                                    deps=[tex, const_tok], inc="ve")
                                    etoks[(kt, hsub)] = etok
                            if pends:
                                for p_ in pends:
                                    emit_norm(p_)
                                pends = []
                            if pv_free is None:
                                pv_free = [C.war.get(("pbank", 6)), C.war.get(("pbank", 7))]
                            for kt in kts:
                                for hsub in range(2):
                                    hl = 2 * pr + hsub
                                    eslot = (2 * kt + hsub) % 8
                                    tpv[hsub] = t.op("tensor",
                                               lambda e, p=pvb[hsub], kt=kt, hl=hl, u=eslot, n_kt=n_kt:
                                               e.matmul(p[0:65, :], v1[:, kt * 260 + hl * 65: kt * 260 + hl * 65 + 65],
                                                        et[:, u * 512:(u + 1) * 512], start=(kt == 0), stop=(kt == n_kt - 1)),
                                               deps=[etoks[(kt, hsub)], v_ready] + ([pv_free[hsub]] if kt == 0 else []),
                                               inc="pe")
                                    C.war[("et", eslot)] = tpv[hsub]
                        for hsub in range(2):
                            def _recip(e, p=pvb[hsub], sl=hsub):
                                with C.nc.allow_low_precision(reason="bf16 softmax recip operand"):
                                    return e.reciprocal(rr[0:1, sl * 512:(sl + 1) * 512], p[64:65, :])
                            trr = t.op("vector", _recip, deps=[tpv[hsub], C.war.get(("rr", hsub))], inc="ve")
                            pends.append(dict(trr=trr, pvb=pvb[hsub], bank=pvbank[hsub], slot=hsub,
                                              h=hg * 4 + 2 * pr + hsub, qg=qg))
                for p_ in pends:
                    emit_norm(p_)
                pends = []
                C.war[("wqk_sb", 0)] = t.tok("pe")
                C.war[("wv_sb", 0)] = t.tok("pe")
            aout_tok = t.tok("ve")
            C.war[("v1", 0)] = t.tok("pe")
            for og in range(2):
                wo_free = C.war.get(("wo_sb", 0))
                for hp in range(IC):
                    dma(C, "sync", "wo", wo_sb[:, hp * 512:(hp + 1) * 512],
                        d_wo16[hp * 128:(hp + 1) * 128, og * 512:(og + 1) * 512],
                        deps=[wo_free] + pp)
                wo_tok = t.tok("d_wo")
                for tt in range(TT):
                    bank = tt % 4
                    pb = C.P[bank]
                    bank_free = C.war.get(("pbank", bank))
                    tpe = None
                    for hp in range(IC):
                        tpe = t.op("tensor",
                                   lambda e, p=pb, hp=hp, tt=tt:
                                   e.matmul(p[:], aout[:, hp * L + tt * 128: hp * L + (tt + 1) * 128],
                                            wo_sb[:, hp * 512:(hp + 1) * 512],
                                            start=(hp == 0), stop=(hp == IC - 1)),
                                   deps=[aout_tok, wo_tok, bank_free] if hp == 0 else [],
                                   inc="pe" if hp == IC - 1 else None)
                    xslc = C.x_sb[:, tt * 1024 + og * 512: tt * 1024 + (og + 1) * 512]
                    tve = t.op("vector", lambda e, o=xslc, p=pb: e.tensor_add(o, o, p[:]),
                               deps=[tpe, C.x_tok[tt]], inc="ve")
                    C.war[("pbank", bank)] = tve
                    C.x_tok[tt] = tve
                C.war[("wo_sb", 0)] = t.tok("pe")
            C.war[("aoutw", 0)] = t.tok("pe")
            for k_ in (("lnfm", 0), ("lnfm", 1), ("pB1", 0), ("pB2", 0), ("pWE", 0)):
                C.war[k_] = t.tok("pe")
            store_x(C, d_xo, s)
            t.flush()
        t.op("sync", lambda e: e.wait_ge(C.t.sem["d_xs"], C.t.val["d_xs"]))
        t.flush()
    return nc



def build_l1_attn():
    H1 = 8
    nc = bass.Bass("TRN2", target_bir_lowering=False)
    d_x = nc.dram_tensor("x_in", [S, L, D], F32, kind="ExternalInput").ap()
    d_wq = nc.dram_tensor("wq", [512, D], F32, kind="ExternalInput").ap()
    d_wk = nc.dram_tensor("wk", [512, D], F32, kind="ExternalInput").ap()
    d_wv = nc.dram_tensor("wv", [8192, D], F32, kind="ExternalInput").ap()
    d_ident = nc.dram_tensor("ident", [128, 128], BF16, kind="ExternalInput").ap()
    d_xo = nc.dram_tensor("x_out", [S, L, D], F32, kind="ExternalOutput").ap()
    d_wqt = nc.dram_tensor("wqt", [4, 128, IC, 128], BF16, kind="Internal").ap()
    d_wkt = nc.dram_tensor("wkt", [4, 128, IC, 128], BF16, kind="Internal").ap()
    d_wvt = nc.dram_tensor("wvt", [D, 8192], BF16, kind="Internal").ap()

    with ExitStack() as es:
        C = new_ctx(nc, es, COMMON_STREAMS + ["qka", "qkb", "wvb"])
        alloc_common(C, es, nc)
        load_consts(C, d_ident)
        pp = prepass_transpose(C, es, nc, [
            dict(src=d_wq, dst=d_wqt, kind="panel", O=512, I=D),
            dict(src=d_wk, dst=d_wkt, kind="panel", O=512, I=D),
            dict(src=d_wv, dst=d_wvt, kind="plain", O=8192, I=D),
        ])
        with ExitStack() as ms:
            xn_fm = ms.enter_context(nc.sbuf_tensor("xn_fm", [128, IC * L], BF16))
            q_fm = ms.enter_context(nc.sbuf_tensor("q_fm", [128, 4 * L], BF16))
            k_fm = ms.enter_context(nc.sbuf_tensor("k_fm", [128, 4 * L], BF16))
            v_h = ms.enter_context(nc.sbuf_tensor("v_h", [128, TT * 1024], BF16))
            et = ms.enter_context(nc.sbuf_tensor("et", [128, TT * 512], BF16))
            wqkb = ms.enter_context(nc.sbuf_tensor("wqkb", [128, 2 * 1024], BF16))
            wvb = ms.enter_context(nc.sbuf_tensor("wvb", [128, IC * 512], BF16))
            ones = ms.enter_context(nc.sbuf_tensor("ones", [128, 1], BF16))
            rc = ms.enter_context(nc.sbuf_tensor("rc", [128, 1], F32))
            t = C.t
            ones_tok = t.op("vector", lambda e: e.memset(ones[:], 1.0), inc="ve")

            for s in range(S):
                load_x(C, d_x, s)
                xn_tok = ln_to_fm(C, list(range(TT)), xn_fm, L, lambda tt: tt * 128)
                for which, dscr, dstb in (("q", d_wqt, q_fm), ("k", d_wkt, k_fm)):
                    for oc in range(4):
                        slot = oc % 2
                        w_free = C.war.get(("wqkb", slot))
                        dma(C, "sync", "qk%s" % "ab"[slot],
                            wqkb[:, slot * 1024:(slot + 1) * 1024].rearrange("p (c o) -> p c o", o=128),
                            dscr[oc], deps=[w_free, C.war.get(("pWE", 0))] + pp)
                        twl = t.tok("d_qk%s" % "ab"[slot])
                        for ng in range(4):
                            bank = (oc * 4 + ng) % 2
                            pb = C.P[bank]
                            bank_free = C.war.get(("pbank", bank))
                            tpe = None
                            for ic in range(IC):
                                tpe = t.op("tensor",
                                           lambda e, p=pb, sl=slot, ic=ic, ng=ng:
                                           e.matmul(p[:], wqkb[:, sl * 1024 + ic * 128: sl * 1024 + (ic + 1) * 128],
                                                    xn_fm[:, ic * L + ng * 512: ic * L + (ng + 1) * 512],
                                                    start=(ic == 0), stop=(ic == IC - 1)),
                                           deps=[twl, bank_free, xn_tok] if ic == 0 else [],
                                           inc="pe" if ic == IC - 1 else None)
                            tev = t.op("vector",
                                       lambda e, p=pb, dd=dstb, oc=oc, ng=ng:
                                       e.tensor_copy(dd[:, oc * L + ng * 512: oc * L + (ng + 1) * 512], p[:]),
                                       deps=[tpe, C.war.get((which + "_fm", 0)), C.war.get(("pB2", 0))], inc="ve")
                            C.war[("pbank", bank)] = tev
                        C.war[("wqkb", slot)] = t.tok("pe")
                qk_ready = t.tok("ve")
                for h in range(H1):
                    hp, hsub = h // 2, h % 2
                    for dvg in range(2):
                        wv_free = C.war.get(("wvb", 0))
                        dma(C, "sync", "wvb", wvb[:].rearrange("p (c o) -> p c o", o=512),
                            d_wvt[:, h * 1024 + dvg * 512: h * 1024 + (dvg + 1) * 512]
                            .rearrange("(c p) o -> p c o", p=128),
                            deps=[wv_free, C.war.get(("pWE", 0))] + pp)
                        twv = t.tok("d_wvb")
                        for tt in range(TT):
                            bank = (2, 3, 6, 7)[tt % 4]
                            pb = C.P[bank]
                            bank_free = C.war.get(("pbank", bank))
                            tpe = None
                            for ic in range(IC):
                                tpe = t.op("tensor",
                                           lambda e, p=pb, ic=ic, tt=tt:
                                           e.matmul(p[:], xn_fm[:, ic * L + tt * 128: ic * L + (tt + 1) * 128],
                                                    wvb[:, ic * 512:(ic + 1) * 512],
                                                    start=(ic == 0), stop=(ic == IC - 1)),
                                           deps=[twv, bank_free, xn_tok] if ic == 0 else [],
                                           inc="pe" if ic == IC - 1 else None)
                            tev = t.op("vector",
                                       lambda e, p=pb, tt=tt, dvg=dvg:
                                       e.tensor_copy(v_h[:, tt * 1024 + dvg * 512: tt * 1024 + (dvg + 1) * 512], p[:]),
                                       deps=[tpe, C.war.get(("v_h", 0)), C.war.get(("pB1", 0))], inc="ve")
                            C.war[("pbank", bank)] = tev
                        C.war[("wvb", 0)] = t.tok("pe")
                    vh_ready = t.tok("ve")
                    for ng in range(4):
                        et_rd = C.war.get(("et", 0))
                        exp_toks = []
                        for mt in range(TT):
                            bank = mt % 4
                            pb = C.P[bank]
                            bank_free = C.war.get(("pbank", bank))
                            tps = t.op("tensor",
                                       lambda e, p=pb, hp=hp, hsub=hsub, mt=mt, ng=ng:
                                       e.matmul(p[:], k_fm[hsub * 64:(hsub + 1) * 64, hp * L + mt * 128: hp * L + (mt + 1) * 128],
                                                q_fm[hsub * 64:(hsub + 1) * 64, hp * L + ng * 512: hp * L + (ng + 1) * 512],
                                                start=True, stop=True),
                                       deps=[qk_ready, bank_free, et_rd], inc="pe")
                            tex = t.op("scalar",
                                       lambda e, p=pb, mt=mt: e.activation(et[:, mt * 512:(mt + 1) * 512], p[:], AF.Exp),
                                       deps=[tps, C.war.get(("pWE", 0))], inc="sc")
                            exp_toks.append(tex)
                            C.war[("pbank", bank)] = tex
                        for nl in range(4):
                            nt = ng * 4 + nl
                            sb_ = C.P[4]
                            s_free = C.war.get(("pbank", 4))
                            tsm = None
                            for mt in range(TT):
                                tsm = t.op("tensor",
                                           lambda e, p=sb_, mt=mt, nl=nl:
                                           e.matmul(p[:, 0:1], et[:, mt * 512 + nl * 128: mt * 512 + (nl + 1) * 128],
                                                    ones[:], start=(mt == 0), stop=(mt == TT - 1)),
                                           deps=([s_free, ones_tok] if mt == 0 else []) + [exp_toks[mt]],
                                           inc="pe" if mt == TT - 1 else None)
                            trc = t.op("vector", lambda e, p=sb_: e.reciprocal(rc[:], p[:, 0:1]),
                                       deps=[tsm, C.war.get(("rc", 0))], inc="ve")
                            C.war[("pbank", 4)] = trc
                            tve = None
                            for dg in range(2):
                                bank = 5 + dg
                                pb = C.P[bank]
                                bank_free = C.war.get(("pbank", bank))
                                tpv = None
                                for mt in range(TT):
                                    tpv = t.op("tensor",
                                               lambda e, p=pb, mt=mt, nl=nl, dg=dg:
                                               e.matmul(p[:], et[:, mt * 512 + nl * 128: mt * 512 + (nl + 1) * 128],
                                                        v_h[:, mt * 1024 + dg * 512: mt * 1024 + (dg + 1) * 512],
                                                        start=(mt == 0), stop=(mt == TT - 1)),
                                               deps=([vh_ready, bank_free] if mt == 0 else []) + [exp_toks[mt]],
                                               inc="pe" if mt == TT - 1 else None)
                                xslc = C.x_sb[:, nt * 1024 + dg * 512: nt * 1024 + (dg + 1) * 512]
                                tve = t.op("vector",
                                           lambda e, o=xslc, p=pb:
                                           e.scalar_tensor_tensor(o, p[:], rc[:, 0:1], o, ALU.mult, ALU.add),
                                           deps=[tpv, trc, C.x_tok[nt]], inc="ve")
                                C.war[("pbank", bank)] = tve
                                C.x_tok[nt] = tve
                            C.war[("rc", 0)] = tve
                        C.war[("et", 0)] = t.tok("pe")
                    C.war[("v_h", 0)] = t.tok("pe")
                for k_ in (("lnfm", 0), ("lnfm", 1), ("pB1", 0), ("pB2", 0), ("pWE", 0)):
                    C.war[k_] = t.tok("pe")
                store_x(C, d_xo, s)
                t.flush()
            t.op("sync", lambda e: e.wait_ge(C.t.sem["d_xs"], C.t.val["d_xs"]))
            t.flush()
    return nc



def _consts():
    ident = np.eye(128, dtype=np.float32).astype(BFNP)
    masks = np.zeros((4, 128, 512), np.float32)
    for j in range(4):
        for kp in range(128):
            masks[j, kp, j * 128 + kp:] = 1.0
    return ident, masks.astype(BFNP)


def _run(nc, per_core_maps):
    return run_bass_kernel_spmd(nc, per_core_maps, core_ids=list(range(NCORES)))


def _aff_cols(inp):
    f32 = np.float32
    qs, ks = np.asarray(inp["L0_attn_q_scale"], f32), np.asarray(inp["L0_attn_k_scale"], f32)
    qb, kb = np.asarray(inp["L0_attn_q_bias"], f32), np.asarray(inp["L0_attn_k_bias"], f32)
    return np.ascontiguousarray(np.stack(
        [np.concatenate([qs, qs]), np.concatenate([ks, ks]),
         np.concatenate([qb, qb]), np.concatenate([kb, kb])], axis=1).astype(f32))


def _pair_wqk(w):
    w = np.asarray(w, np.float32).reshape(1024, 4, 2, 2, 2, 64)
    return np.ascontiguousarray(w.transpose(0, 1, 2, 4, 3, 5).reshape(1024, 2048))


def kernel(**inp):
    ident, masks = _consts()
    f32 = np.float32
    x = np.ascontiguousarray(inp["L0_attn_inputs"], dtype=f32)
    xs = [np.ascontiguousarray(v) for v in np.split(x, NCORES, axis=0)]
    rs = lambda v, n: np.ascontiguousarray(np.asarray(v, f32).reshape(n, 128).T)

    nc = build_fused()
    shared = dict(wqk=_pair_wqk(inp["L0_attn_W_qk"]),
                  wv0=np.ascontiguousarray(inp["L0_attn_W_v"], f32),
                  wo=np.ascontiguousarray(inp["L0_attn_W_out"], f32),
                  aff=_aff_cols(inp), masks=masks, ident=ident,
                  w1=np.ascontiguousarray(inp["L0_ffn_w1_w"], f32),
                  w2=np.ascontiguousarray(inp["L0_ffn_w2_w"], f32),
                  h1=np.ascontiguousarray(inp["L0_ffn_h1_w"], f32),
                  h2=np.ascontiguousarray(inp["L0_ffn_h2_w"], f32),
                  w3=np.ascontiguousarray(inp["L0_ffn_w3_w"], f32),
                  s1rs=rs(inp["L0_ffn_w1_s"], 32), s2rs=rs(inp["L0_ffn_w2_s"], 32),
                  h1rs=rs(inp["L0_ffn_h1_s"], 32), h2rs=rs(inp["L0_ffn_h2_s"], 32),
                  w3rs=rs(inp["L0_ffn_w3_s"], 8),
                  wq=np.ascontiguousarray(inp["L1_attn_Wq"], f32),
                  wk=np.ascontiguousarray(inp["L1_attn_Wk"], f32),
                  wv1=np.ascontiguousarray(inp["L1_attn_Wv"], f32),
                  wl1f=np.ascontiguousarray(inp["L1_ffn_weight"], f32))
    maps = [dict(x_in=xs[c], **shared) for c in range(NCORES)]
    res = _run(nc, maps)
    return np.concatenate([np.asarray(r["x_out"]) for r in res.results], axis=0)























def build_fused():
    H1 = 8
    nc = bass.Bass("TRN2", target_bir_lowering=False)
    d_x = nc.dram_tensor("x_in", [S, L, D], F32, kind="ExternalInput").ap()
    d_wqk = nc.dram_tensor("wqk", [D, 2048], F32, kind="ExternalInput").ap()
    d_wv0 = nc.dram_tensor("wv0", [D, D], F32, kind="ExternalInput").ap()
    d_wo = nc.dram_tensor("wo", [D, D], F32, kind="ExternalInput").ap()
    d_aff = nc.dram_tensor("aff", [128, 4], F32, kind="ExternalInput").ap()
    d_masks = nc.dram_tensor("masks", [4, 128, 512], BF16, kind="ExternalInput").ap()
    d_w1 = nc.dram_tensor("w1", [DH, D], F32, kind="ExternalInput").ap()
    d_w2 = nc.dram_tensor("w2", [DH, D], F32, kind="ExternalInput").ap()
    d_h1 = nc.dram_tensor("h1", [DH, DH], F32, kind="ExternalInput").ap()
    d_h2 = nc.dram_tensor("h2", [DH, DH], F32, kind="ExternalInput").ap()
    d_w3 = nc.dram_tensor("w3", [D, DH], F32, kind="ExternalInput").ap()
    d_s1 = nc.dram_tensor("s1rs", [128, 32], F32, kind="ExternalInput").ap()
    d_s2 = nc.dram_tensor("s2rs", [128, 32], F32, kind="ExternalInput").ap()
    d_h1s = nc.dram_tensor("h1rs", [128, 32], F32, kind="ExternalInput").ap()
    d_h2s = nc.dram_tensor("h2rs", [128, 32], F32, kind="ExternalInput").ap()
    d_w3s = nc.dram_tensor("w3rs", [128, 8], F32, kind="ExternalInput").ap()
    d_wq = nc.dram_tensor("wq", [512, D], F32, kind="ExternalInput").ap()
    d_wk = nc.dram_tensor("wk", [512, D], F32, kind="ExternalInput").ap()
    d_wv1 = nc.dram_tensor("wv1", [8192, D], F32, kind="ExternalInput").ap()
    d_wl1f = nc.dram_tensor("wl1f", [D, D], F32, kind="ExternalInput").ap()
    d_ident = nc.dram_tensor("ident", [128, 128], BF16, kind="ExternalInput").ap()
    d_xo = nc.dram_tensor("x_out", [S, L, D], F32, kind="ExternalOutput").ap()
    d_wqk16 = nc.dram_tensor("wqk16", [D, 2048], BF16, kind="Internal").ap()
    d_wv16 = nc.dram_tensor("wv16", [D, D], BF16, kind="Internal").ap()
    d_wo16 = nc.dram_tensor("wo16", [D, D], BF16, kind="Internal").ap()
    d_w1t = nc.dram_tensor("w1t", [DH // 128, 128, D // 128, 128], BF16, kind="Internal").ap()
    d_w2t = nc.dram_tensor("w2t", [DH // 128, 128, D // 128, 128], BF16, kind="Internal").ap()
    d_h1t = nc.dram_tensor("h1t", [DH // 128, 128, DH // 128, 128], BF16, kind="Internal").ap()
    d_h2t = nc.dram_tensor("h2t", [DH // 128, 128, DH // 128, 128], BF16, kind="Internal").ap()
    d_w3t = nc.dram_tensor("w3t", [DH, D], BF16, kind="Internal").ap()
    d_wqt = nc.dram_tensor("wqt", [4, 128, IC, 128], BF16, kind="Internal").ap()
    d_wkt = nc.dram_tensor("wkt", [4, 128, IC, 128], BF16, kind="Internal").ap()
    d_wvt = nc.dram_tensor("wvt", [D, 8192], BF16, kind="Internal").ap()
    d_swt = nc.dram_tensor("swt", [D, D], BF16, kind="Internal").ap()

    streams = COMMON_STREAMS + ["wqk", "wv", "wo", "w12a", "w12b", "ha", "hb",
                                "w3a", "w3b", "qka", "qkb", "wvb", "swt"]
    with ExitStack() as es:
        C = new_ctx(nc, es, streams)
        alloc_common(C, es, nc)
        masks = es.enter_context(nc.sbuf_tensor("maskssb", [128, 4 * 512], BF16))
        affc = es.enter_context(nc.sbuf_tensor("affc", [128, 4], F32))
        c32 = es.enter_context(nc.sbuf_tensor("c32", [1, 64], BF16))
        scol = es.enter_context(nc.sbuf_tensor("scol", [128, 4 * 32 + 8], F32))
        s1c, s2c, h1c, h2c = (scol[:, i * 32:(i + 1) * 32] for i in range(4))
        w3c = scol[:, 128:136]
        t = C.t
        load_consts(C, d_ident,
                    extra=[(masks[:, j * 512:(j + 1) * 512], d_masks[j]) for j in range(4)]
                          + [(affc[:], d_aff), (s1c, d_s1), (s2c, d_s2),
                             (h1c, d_h1s), (h2c, d_h2s), (w3c, d_w3s)])
        const_tok = C.const_tok
        scol_tok = C.const_tok
        pp = prepass_transpose(C, es, nc, [
            dict(src=d_wqk, dst=d_wqk16, kind="cast", O=D, I=2048),
            dict(src=d_wv0, dst=d_wv16, kind="cast", O=D, I=D),
            dict(src=d_wo, dst=d_wo16, kind="cast", O=D, I=D),
            dict(src=d_w1, dst=d_w1t, kind="panel", O=DH, I=D),
            dict(src=d_w2, dst=d_w2t, kind="panel", O=DH, I=D),
            dict(src=d_h1, dst=d_h1t, kind="panel", O=DH, I=DH),
            dict(src=d_h2, dst=d_h2t, kind="panel", O=DH, I=DH),
            dict(src=d_w3, dst=d_w3t, kind="plain", O=D, I=DH,
                 scale=w3c, scale_tok=scol_tok),
            dict(src=d_wq, dst=d_wqt, kind="panel", O=512, I=D),
            dict(src=d_wk, dst=d_wkt, kind="panel", O=512, I=D),
            dict(src=d_wv1, dst=d_wvt, kind="plain", O=8192, I=D),
            dict(src=d_wl1f, dst=d_swt, kind="plain", O=D, I=D, sign=True),
        ])
        XN = es.enter_context(nc.sbuf_tensor("poolXN", [128, 16384], BF16)).ap()
        B1 = es.enter_context(nc.sbuf_tensor("poolB1", [128, 16384], BF16)).ap()
        B2 = es.enter_context(nc.sbuf_tensor("poolB2", [128, 16384], BF16)).ap()
        WE = es.enter_context(nc.sbuf_tensor("poolWE", [128, 14336], BF16)).ap()
        rb = es.enter_context(nc.sbuf_tensor("rb", [128, 512], F32))
        rr = es.enter_context(nc.sbuf_tensor("rr", [1, 1024], BF16))
        hstat = es.enter_context(nc.sbuf_tensor("hstat", [128, 2 * 96], F32))
        rc = es.enter_context(nc.sbuf_tensor("rc", [128, 1], F32))
        ones = es.enter_context(nc.sbuf_tensor("ones", [128, 1], BF16))
        tsil = es.enter_context(nc.sbuf_tensor("tsil", [128, 2 * 512], BF16))
        t.op("vector", lambda e: e.memset(c32[:], 1.0 / 32.0))
        ones_tok = t.op("vector", lambda e: e.memset(ones[:], 1.0), inc="ve")
        w12b = WE[:, 0:4096]
        hbuf = WE[:, 4096:12288]
        def h_stage(src, dst, panel_scratch, scale_cols, name, src_tok, dst_war=None, dst_war2=None):
            tsc = None
            for oc in range(32):
                slot = oc % 2
                pb_free = C.war.get(("hbuf", slot))
                dma(C, "sync", "h%s" % "ab"[slot],
                    hbuf[:, slot * 4096:(slot + 1) * 4096].rearrange("p (c o) -> p c o", o=128),
                    panel_scratch[oc], deps=[pb_free, C.war.get(("pWE", 0))] + pp)
                tl_ = t.tok("d_h%s" % "ab"[slot])
                bank = oc % 4
                pb = C.P[bank]
                bank_free = C.war.get(("pbank", bank))
                tpe = None
                for ic in range(32):
                    tpe = t.op("tensor",
                               lambda e, p=pb, sl=slot, ic=ic, x=src:
                               e.matmul(p[:], hbuf[:, sl * 4096 + ic * 128: sl * 4096 + (ic + 1) * 128],
                                        x[:, ic * 512:(ic + 1) * 512],
                                        start=(ic == 0), stop=(ic == 31)),
                               deps=([tl_, bank_free, src_tok]) if ic == 0 else [],
                               inc="pe" if ic == 31 else None)
                C.war[("hbuf", slot)] = tpe
                tsc = t.op("scalar",
                           lambda e, p=pb, oc=oc, dd=dst, sc=scale_cols:
                           e.activation(dd[:, oc * 512:(oc + 1) * 512], p[:], AF.Gelu, scale=sc[:, oc:oc + 1]),
                           deps=[tpe, scol_tok, dst_war, dst_war2], inc="sc")
                C.war[("pbank", bank)] = tsc
            return t.tok("sc")


        for s in range(S):
            load_x(C, d_x, s)
            xn_fm = XN
            qk_fm = B2[:, 0:8192]
            v1 = B2[:, 8192:12352]
            qn0 = B2[:, 12352:14400]
            aout = B1
            wqk_sb = WE[:, 0:4096]
            wv_sb = WE[:, 4096:6144]
            wo_sb = WE[:, 6144:10240]
            et = WE[:, 10240:14336]
            c32_tok = t.op("vector", lambda e: e.memset(v1[:], 1.0), deps=[C.war.get(("pB2dma", 0))], inc="ve")
            xn_tok = ln_to_fm(C, list(range(TT)), xn_fm, L, lambda tt: tt * 128)
            v1_rd = C.war.get(("v1", 0))
            for hg in range(4):
                wqk_free = C.war.get(("wqk_sb", 0))
                for ic in range(IC):
                    dma(C, "sync", "wqk", wqk_sb[:, ic * 512:(ic + 1) * 512],
                        d_wqk16[ic * 128:(ic + 1) * 128, hg * 512:(hg + 1) * 512],
                        deps=[wqk_free] + pp)
                wqk_tok = t.tok("d_wqk")
                wv_free = C.war.get(("wv_sb", 0))
                for ic in range(IC):
                    dma(C, "sync", "wv", wv_sb[:, ic * 256:(ic + 1) * 256],
                        d_wv16[ic * 128:(ic + 1) * 128, hg * 256:(hg + 1) * 256],
                        deps=[wv_free] + pp)
                wv_tok = t.tok("d_wv")
                def emit_qtrans(p_):
                    tt_, q4_, tseg_ = p_
                    tb = 3 + tt_ % 2
                    pb2 = C.P[tb]
                    b2_free = C.war.get(("pbank", tb))
                    qn = qn0[:, q4_ * 512:(q4_ + 1) * 512]
                    tpt = None
                    for j in range(4):
                        tpt = t.op("tensor",
                                   lambda e, p=pb2, j=j, i=qn:
                                   e.matmul(p[:, j * 128:(j + 1) * 128], i[:, j * 128:(j + 1) * 128], C.ident[:],
                                            start=True, stop=True),
                                   deps=[tseg_, b2_free, C.ident_tok], inc="pe")
                    qv = qk_fm[:].rearrange("p (c n) -> p c n", n=L)
                    pv3 = pb2[:].rearrange("p (c n) -> p c n", n=128)
                    qfm_free = C.war.get(("pB2dma", 0))
                    tqe = None
                    for cix in range(4):
                        isq = (cix % 2 == 0)
                        tqe = t.op("scalar",
                                   lambda e, qv=qv, pv3=pv3, cix=cix, tt=tt_, isq=isq:
                                   e.activation(qv[:, cix, tt * 128:(tt + 1) * 128],
                                                pv3[:, cix, :], AF.Identity,
                                                scale=affc[:, 0:1] if isq else affc[:, 1:2],
                                                bias=affc[:, 2:3] if isq else affc[:, 3:4]),
                                   deps=[tpt, const_tok, qfm_free], inc="sc")
                    C.war[("pbank", tb)] = tqe
                    C.war[("qn0", q4_)] = tqe
                    return tqe

                pend_q = []
                pend2 = None

                def emit_segs(p2):
                    tt_, b_, q4_, pb_ = p2["tt"], p2["b"], p2["q4"], p2["pb"]
                    st_ = hstat[:, b_ * 96:(b_ + 1) * 96]
                    mean8_ = st_[:, 48:56]; var8_ = st_[:, 80:88]; r8_ = st_[:, 88:96]
                    trc = t.op("vector", lambda e, o=r8_, i=var8_: e.reciprocal(o, i), deps=[p2["tsd"]], inc="ve")
                    qn = qn0[:, q4_ * 512:(q4_ + 1) * 512]
                    qn_free = C.war.get(("qn0", q4_))
                    tseg = None
                    for sg in range(8):
                        tseg = t.op("vector",
                                    lambda e, o=qn, i=pb_, sg=sg, m=mean8_, r=r8_:
                                    e.tensor_scalar(o[:, sg * 64:(sg + 1) * 64], i[:, sg * 64:(sg + 1) * 64],
                                                    m[:, sg:sg + 1], r[:, sg:sg + 1], ALU.subtract, ALU.mult),
                                    deps=[trc, qn_free, C.war.get(("pB2dma", 0))] if sg == 0 else [],
                                    inc="ve" if sg == 7 else None)
                    C.war[("hstat", b_)] = tseg
                    C.war[("pbank", p2["b3"])] = tseg
                    pend_q.append((tt_, q4_, tseg))
                    if len(pend_q) > 2:
                        emit_qtrans(pend_q.pop(0))

                for tt in range(TT):
                    b3 = tt % 3
                    b = tt % 2
                    q4 = tt % 4
                    pb = C.P[b3]
                    bank_free = C.war.get(("pbank", b3))
                    tpe = None
                    for ic in range(IC):
                        tpe = t.op("tensor",
                                   lambda e, p=pb, ic=ic, tt=tt:
                                   e.matmul(p[:], xn_fm[:, ic * L + tt * 128: ic * L + (tt + 1) * 128],
                                            wqk_sb[:, ic * 512:(ic + 1) * 512],
                                            start=(ic == 0), stop=(ic == IC - 1)),
                                   deps=[xn_tok, wqk_tok, bank_free] if ic == 0 else [],
                                   inc="pe" if ic == IC - 1 else None)
                    st = hstat[:, b * 96:(b + 1) * 96]
                    v6 = st[:, 0:48].rearrange("p (c k) -> p k c", k=6)
                    mean8 = st[:, 48:56]; sM28 = st[:, 56:64]
                    t1 = st[:, 64:72]; t2 = st[:, 72:80]
                    var8 = st[:, 80:88]
                    st_free = C.war.get(("hstat", b))
                    for sg8 in range(8):
                        t.op("vector", lambda e, o=st, i=pb, sg8=sg8:
                             e.bn_stats(o[:, sg8 * 6: sg8 * 6 + 6], i[:, sg8 * 64:(sg8 + 1) * 64]),
                             deps=[tpe, st_free] if sg8 == 0 else [])
                    t.op("vector", lambda e, o=mean8, v=v6: e.tensor_add(o, v[:, 1:2, :], v[:, 4:5, :]))
                    t.op("vector", lambda e, o=mean8, i=mean8: e.tensor_scalar_mul(o, i, 0.5))
                    t.op("vector", lambda e, o=t1, v=v6: e.tensor_mul(o, v[:, 1:2, :], v[:, 1:2, :]))
                    t.op("vector", lambda e, o=t2, v=v6: e.tensor_mul(o, v[:, 4:5, :], v[:, 4:5, :]))
                    t.op("vector", lambda e, o=t1, i=t2: e.tensor_add(o, o, i))
                    t.op("vector", lambda e, o=t1, i=t1: e.tensor_scalar_mul(o, i, 0.5))
                    t.op("vector", lambda e, o=sM28, v=v6: e.tensor_add(o, v[:, 2:3, :], v[:, 5:6, :]))
                    t.op("vector", lambda e, o=sM28, i=sM28: e.tensor_scalar_mul(o, i, 1.0 / 64.0))
                    t.op("vector", lambda e, o=sM28, i=t1: e.tensor_add(o, o, i))
                    t.op("vector", lambda e, o=t2, i=mean8: e.tensor_mul(o, i, i))
                    tvv = t.op("vector", lambda e, o=var8, a=sM28, bb=t2: e.tensor_sub(o, a, bb), inc="ve")
                    tsd = t.op("scalar", lambda e, o=var8, i=var8:
                               e.activation(o, i, AF.Sqrt, bias=C.epsc[:, 1:2]),
                               deps=[tvv, C.eps_tok], inc="sc")
                    if pend2 is not None:
                        emit_segs(pend2)
                    pend2 = dict(tt=tt, b=b, q4=q4, pb=pb, b3=b3, tsd=tsd)
                    vbank = 5 + tt % 3
                    vpb = C.P[vbank]
                    vbank_free = C.war.get(("pbank", vbank))
                    tpeV = None
                    for ic in range(IC):
                        tpeV = t.op("tensor",
                                    lambda e, p=vpb, ic=ic, tt=tt:
                                    e.matmul(p[:, 0:256], xn_fm[:, ic * L + tt * 128: ic * L + (tt + 1) * 128],
                                             wv_sb[:, ic * 256:(ic + 1) * 256],
                                             start=(ic == 0), stop=(ic == IC - 1)),
                                    deps=[xn_tok, wv_tok, vbank_free, c32_tok, v1_rd] if ic == 0 else [],
                                    inc="pe" if ic == IC - 1 else None)
                    tve = t.op("scalar",
                               lambda e, p=vpb, tt=tt:
                               e.activation(v1[:, tt * 260: tt * 260 + 260].rearrange("p (h dd) -> p h dd", dd=65)[:, :, 0:64],
                                            p[:, 0:256].rearrange("p (h dd) -> p h dd", dd=64), AF.Identity, scale=1.0 / 32.0),
                               deps=[tpeV], inc="sc")
                    C.war[("pbank", vbank)] = tve
                emit_segs(pend2)
                pend2 = None
                for p_ in pend_q:
                    emit_qtrans(p_)
                pend_q = []
                qk_ready = t.tok("sc")
                v_ready = qk_ready
                def emit_norm(p_):
                    tbc = t.op("tensor", lambda e, p=C.P[4], sl=p_["slot"]: e.matmul(p[0:64, :], c32[:, 0:64],
                                                                                    rr[0:1, sl * 512:(sl + 1) * 512],
                                                                                    start=True, stop=True),
                               deps=[p_["trr"], C.war.get(("pbank", 4)), c32_tok], inc="pe")
                    C.war[("rr", p_["slot"])] = tbc
                    trb = t.op("scalar", lambda e, p=C.P[4]: e.copy(rb[0:64, :], p[0:64, :]),
                               deps=[tbc, C.war.get(("rb", 0))], inc="sc")
                    C.war[("pbank", 4)] = trb
                    h_, qg_ = p_["h"], p_["qg"]
                    hp_, hs_ = h_ // 2, h_ % 2
                    tao = t.op("vector",
                               lambda e, p=p_["pvb"], hp=hp_, hs2=hs_, qg=qg_:
                               e.tensor_mul(aout[hs2 * 64:(hs2 + 1) * 64, hp * L + qg * 512: hp * L + (qg + 1) * 512],
                                            p[0:64, :], rb[0:64, :]),
                               deps=[trb, C.war.get(("aoutw", 0))], inc="ve")
                    C.war[("pbank", p_["bank"])] = tao
                    C.war[("rb", 0)] = tao

                pends = []
                for pr in range(2):
                    qch, kch = 2 * pr, 2 * pr + 1
                    for qg in range(4):
                        pvbank = [6, 7]
                        pvb = [C.P[6], C.P[7]]
                        pv_free = None
                        n_kt = 4 * qg + 4
                        tpv = [None, None]
                        for b0 in range(0, n_kt, 4):
                            kts = range(b0, min(b0 + 4, n_kt))
                            etoks = {}
                            for kt in kts:
                                for hsub in range(2):
                                    sbank = (0, 1, 2, 3)[(2 * kt + hsub) % 4]
                                    psb = C.P[sbank]
                                    ps_free = C.war.get(("pbank", sbank))
                                    tps = t.op("tensor",
                                               lambda e, p=psb, kch=kch, qch=qch, hsub=hsub, kt=kt, qg=qg:
                                               e.matmul(p[:], qk_fm[hsub * 64:(hsub + 1) * 64, kch * L + kt * 128: kch * L + (kt + 1) * 128],
                                                        qk_fm[hsub * 64:(hsub + 1) * 64, qch * L + qg * 512: qch * L + (qg + 1) * 512],
                                                        start=True, stop=True),
                                               deps=[qk_ready, ps_free], inc="pe")
                                    eslot = (2 * kt + hsub) % 8
                                    e_free = C.war.get(("et", eslot))
                                    tex = t.op("scalar",
                                               lambda e, p=psb, es_=eslot:
                                               e.activation(et[:, es_ * 512:(es_ + 1) * 512], p[:], AF.Exp, scale=1.0 / 64.0),
                                               deps=[tps, e_free], inc="sc")
                                    C.war[("pbank", sbank)] = tex
                                    etok = tex
                                    if kt >= 4 * qg:
                                        j = kt - 4 * qg
                                        etok = t.op("vector",
                                                    lambda e, es_=eslot, j=j:
                                                    e.tensor_mul(et[:, es_ * 512:(es_ + 1) * 512],
                                                                 et[:, es_ * 512:(es_ + 1) * 512],
                                                                 masks[:, j * 512:(j + 1) * 512]),
                                                    deps=[tex, const_tok], inc="ve")
                                    etoks[(kt, hsub)] = etok
                            if pends:
                                for p_ in pends:
                                    emit_norm(p_)
                                pends = []
                            if pv_free is None:
                                pv_free = [C.war.get(("pbank", 6)), C.war.get(("pbank", 7))]
                            for kt in kts:
                                for hsub in range(2):
                                    hl = 2 * pr + hsub
                                    eslot = (2 * kt + hsub) % 8
                                    tpv[hsub] = t.op("tensor",
                                               lambda e, p=pvb[hsub], kt=kt, hl=hl, u=eslot, n_kt=n_kt:
                                               e.matmul(p[0:65, :], v1[:, kt * 260 + hl * 65: kt * 260 + hl * 65 + 65],
                                                        et[:, u * 512:(u + 1) * 512], start=(kt == 0), stop=(kt == n_kt - 1)),
                                               deps=[etoks[(kt, hsub)], v_ready] + ([pv_free[hsub]] if kt == 0 else []),
                                               inc="pe")
                                    C.war[("et", eslot)] = tpv[hsub]
                        for hsub in range(2):
                            def _recip(e, p=pvb[hsub], sl=hsub):
                                with C.nc.allow_low_precision(reason="bf16 softmax recip operand"):
                                    return e.reciprocal(rr[0:1, sl * 512:(sl + 1) * 512], p[64:65, :])
                            trr = t.op("vector", _recip, deps=[tpv[hsub], C.war.get(("rr", hsub))], inc="ve")
                            pends.append(dict(trr=trr, pvb=pvb[hsub], bank=pvbank[hsub], slot=hsub,
                                              h=hg * 4 + 2 * pr + hsub, qg=qg))
                for p_ in pends:
                    emit_norm(p_)
                pends = []
                C.war[("wqk_sb", 0)] = t.tok("pe")
                C.war[("wv_sb", 0)] = t.tok("pe")
            aout_tok = t.tok("ve")
            C.war[("v1", 0)] = t.tok("pe")
            for og in range(2):
                wo_free = C.war.get(("wo_sb", 0))
                for hp in range(IC):
                    dma(C, "sync", "wo", wo_sb[:, hp * 512:(hp + 1) * 512],
                        d_wo16[hp * 128:(hp + 1) * 128, og * 512:(og + 1) * 512],
                        deps=[wo_free] + pp)
                wo_tok = t.tok("d_wo")
                for tt in range(TT):
                    bank = tt % 4
                    pb = C.P[bank]
                    bank_free = C.war.get(("pbank", bank))
                    tpe = None
                    for hp in range(IC):
                        tpe = t.op("tensor",
                                   lambda e, p=pb, hp=hp, tt=tt:
                                   e.matmul(p[:], aout[:, hp * L + tt * 128: hp * L + (tt + 1) * 128],
                                            wo_sb[:, hp * 512:(hp + 1) * 512],
                                            start=(hp == 0), stop=(hp == IC - 1)),
                                   deps=[aout_tok, wo_tok, bank_free] if hp == 0 else [],
                                   inc="pe" if hp == IC - 1 else None)
                    xslc = C.x_sb[:, tt * 1024 + og * 512: tt * 1024 + (og + 1) * 512]
                    tve = t.op("vector", lambda e, o=xslc, p=pb: e.tensor_add(o, o, p[:]),
                               deps=[tpe, C.x_tok[tt]], inc="ve")
                    C.war[("pbank", bank)] = tve
                    C.x_tok[tt] = tve
                C.war[("wo_sb", 0)] = t.tok("pe")
            C.war[("aoutw", 0)] = t.tok("pe")
            for k_ in (("lnfm", 0), ("lnfm", 1), ("pB1", 0), ("pB2", 0), ("pWE", 0)):
                C.war[k_] = t.tok("pe")
            xn_g = XN[:, 0:4096]
            xn_gB = XN[:, 4096:8192]
            tA = B1
            tB = B2
            xgs = [xn_g, xn_gB]
            xn_next = ln_to_fm(C, [0, 1, 2, 3], xgs[0], 512, lambda tt: tt * 128, war_key=0)
            for tg in range(4):
                xg = xgs[tg % 2]
                xn_tok = xn_next
                for oc in range(32):
                    slot = oc % 2
                    w_free = C.war.get(("w12sb", slot))
                    dma(C, "sync", "w12%s" % "ab"[slot],
                        w12b[:, slot * 2048: slot * 2048 + 1024].rearrange("p (c o) -> p c o", o=128),
                        d_w1t[oc], deps=[w_free, C.war.get(("pWE", 0))] + pp)
                    dma(C, "sync", "w12%s" % "ab"[slot],
                        w12b[:, slot * 2048 + 1024: slot * 2048 + 2048].rearrange("p (c o) -> p c o", o=128),
                        d_w2t[oc])
                    tlw = t.tok("d_w12%s" % "ab"[slot])
                    ba, bb_ = (oc % 2) * 2, (oc % 2) * 2 + 1
                    pa, pb_ = C.P[ba], C.P[bb_]
                    fa, fb = C.war.get(("pbank", ba)), C.war.get(("pbank", bb_))
                    for ic in range(IC):
                        t.op("tensor",
                             lambda e, p=pa, sl=slot, ic=ic:
                             e.matmul(p[:], w12b[:, sl * 2048 + ic * 128: sl * 2048 + (ic + 1) * 128],
                                      xg[:, ic * 512:(ic + 1) * 512], start=(ic == 0), stop=(ic == IC - 1)),
                             deps=[tlw, fa, xn_tok] if ic == 0 else [],
                             inc="pe" if ic == IC - 1 else None)
                    tpa = t.tok("pe")
                    for ic in range(IC):
                        t.op("tensor",
                             lambda e, p=pb_, sl=slot, ic=ic:
                             e.matmul(p[:], w12b[:, sl * 2048 + 1024 + ic * 128: sl * 2048 + 1024 + (ic + 1) * 128],
                                      xg[:, ic * 512:(ic + 1) * 512], start=(ic == 0), stop=(ic == IC - 1)),
                             deps=[fb] if ic == 0 else [],
                             inc="pe" if ic == IC - 1 else None)
                    tpb = t.tok("pe")
                    C.war[("w12sb", slot)] = tpb
                    sl_ap = tsil[:, slot * 512:(slot + 1) * 512]
                    sl_free = C.war.get(("tsil", slot))
                    tss = t.op("scalar",
                               lambda e, o=sl_ap, p=pa, oc=oc:
                               e.activation(o, p[:], AF.Silu, scale=s1c[:, oc:oc + 1]),
                               deps=[tpa, scol_tok, sl_free], inc="sc")
                    C.war[("pbank", ba)] = tss
                    tvv = t.op("vector",
                               lambda e, o=tA, oc=oc, p=pb_, sl=sl_ap:
                               e.scalar_tensor_tensor(o[:, oc * 512:(oc + 1) * 512], p[:],
                                                      s2c[:, oc:oc + 1], sl, ALU.mult, ALU.mult),
                               deps=[tpb, tss, C.war.get(("tAkey", 0)), C.war.get(("pB1", 0))], inc="ve")
                    C.war[("pbank", bb_)] = tvv
                    C.war[("tsil", slot)] = tvv
                tA_tok = t.tok("ve")
                C.war[("lnfm", tg % 2)] = t.tok("pe")
                if tg < 3:
                    nts = list(range((tg + 1) * 4, (tg + 1) * 4 + 4))
                    xn_next = ln_to_fm(C, nts, xgs[(tg + 1) % 2], 512,
                                       lambda tt, tg=tg: (tt - (tg + 1) * 4) * 128,
                                       war_key=(tg + 1) % 2)
                g1_tok = h_stage(tA, tB, d_h1t, h1c, "h1", tA_tok,
                                 dst_war=C.war.get(("w3tb", 0)), dst_war2=C.war.get(("pB2", 0)))
                g2_tok = h_stage(tB, tA, d_h2t, h2c, "h2", g1_tok)
                w3read = None
                for og in range(2):
                    dma(C, "sync", "w3%s" % "ab"[og],
                        tB[:].rearrange("p (c o) -> p c o", o=512),
                        d_w3t[:, og * 512:(og + 1) * 512].rearrange("(c p) o -> p c o", p=128),
                        deps=[g2_tok, w3read] + pp)
                    tl3 = t.tok("d_w3%s" % "ab"[og])
                    for tl_i in range(4):
                        bank = og * 4 + tl_i
                        pb = C.P[bank]
                        bank_free = C.war.get(("pbank", bank))
                        tpe = None
                        for ic in range(32):
                            tpe = t.op("tensor",
                                       lambda e, p=pb, ic=ic, tl_i=tl_i:
                                       e.matmul(p[:], tA[:, ic * 512 + tl_i * 128: ic * 512 + (tl_i + 1) * 128],
                                                tB[:, ic * 512:(ic + 1) * 512],
                                                start=(ic == 0), stop=(ic == 31)),
                                       deps=[tl3, bank_free, g2_tok] if ic == 0 else [],
                                       inc="pe" if ic == 31 else None)
                        tt = tg * 4 + tl_i
                        xslc = C.x_sb[:, tt * 1024 + og * 512: tt * 1024 + (og + 1) * 512]
                        tve = t.op("vector", lambda e, o=xslc, p=pb: e.tensor_add(o, o, p[:]),
                                   deps=[tpe, C.x_tok[tt]], inc="ve")
                        C.war[("pbank", bank)] = tve
                        C.x_tok[tt] = tve
                    w3read = t.tok("pe")
                C.war[("tAkey", 0)] = t.tok("pe")
                C.war[("w3tb", 0)] = t.tok("pe")
                t.flush()
            for k_ in (("lnfm", 0), ("lnfm", 1), ("pB1", 0), ("pB2", 0), ("pWE", 0)):
                C.war[k_] = t.tok("pe")
                pass
            xn_fm = XN
            q_fm = B2[:, 0:8192]
            k_fm = B2[:, 8192:16384]
            v_h = B1
            et = WE[:, 0:8192]
            wqkb = WE[:, 8192:10240]
            wvb = WE[:, 10240:14336]
            xn_tok = ln_to_fm(C, list(range(TT)), xn_fm, L, lambda tt: tt * 128)
            for which, dscr, dstb in (("q", d_wqt, q_fm), ("k", d_wkt, k_fm)):
                for oc in range(4):
                    slot = oc % 2
                    w_free = C.war.get(("wqkb", slot))
                    dma(C, "sync", "qk%s" % "ab"[slot],
                        wqkb[:, slot * 1024:(slot + 1) * 1024].rearrange("p (c o) -> p c o", o=128),
                        dscr[oc], deps=[w_free, C.war.get(("pWE", 0))] + pp)
                    twl = t.tok("d_qk%s" % "ab"[slot])
                    for ng in range(4):
                        bank = (oc * 4 + ng) % 2
                        pb = C.P[bank]
                        bank_free = C.war.get(("pbank", bank))
                        tpe = None
                        for ic in range(IC):
                            tpe = t.op("tensor",
                                       lambda e, p=pb, sl=slot, ic=ic, ng=ng:
                                       e.matmul(p[:], wqkb[:, sl * 1024 + ic * 128: sl * 1024 + (ic + 1) * 128],
                                                xn_fm[:, ic * L + ng * 512: ic * L + (ng + 1) * 512],
                                                start=(ic == 0), stop=(ic == IC - 1)),
                                       deps=[twl, bank_free, xn_tok] if ic == 0 else [],
                                       inc="pe" if ic == IC - 1 else None)
                        tev = t.op("vector",
                                   lambda e, p=pb, dd=dstb, oc=oc, ng=ng:
                                   e.tensor_copy(dd[:, oc * L + ng * 512: oc * L + (ng + 1) * 512], p[:]),
                                   deps=[tpe, C.war.get((which + "_fm", 0)), C.war.get(("pB2", 0))], inc="ve")
                        C.war[("pbank", bank)] = tev
                    C.war[("wqkb", slot)] = t.tok("pe")
            qk_ready = t.tok("ve")
            for h in range(H1):
                hp, hsub = h // 2, h % 2
                for dvg in range(2):
                    wv_free = C.war.get(("wvb", 0))
                    dma(C, "sync", "wvb", wvb[:].rearrange("p (c o) -> p c o", o=512),
                        d_wvt[:, h * 1024 + dvg * 512: h * 1024 + (dvg + 1) * 512]
                        .rearrange("(c p) o -> p c o", p=128),
                        deps=[wv_free, C.war.get(("pWE", 0))] + pp)
                    twv = t.tok("d_wvb")
                    for tt in range(TT):
                        bank = (2, 3, 6, 7)[tt % 4]
                        pb = C.P[bank]
                        bank_free = C.war.get(("pbank", bank))
                        tpe = None
                        for ic in range(IC):
                            tpe = t.op("tensor",
                                       lambda e, p=pb, ic=ic, tt=tt:
                                       e.matmul(p[:], xn_fm[:, ic * L + tt * 128: ic * L + (tt + 1) * 128],
                                                wvb[:, ic * 512:(ic + 1) * 512],
                                                start=(ic == 0), stop=(ic == IC - 1)),
                                       deps=[twv, bank_free, xn_tok] if ic == 0 else [],
                                       inc="pe" if ic == IC - 1 else None)
                        tev = t.op("vector",
                                   lambda e, p=pb, tt=tt, dvg=dvg:
                                   e.tensor_copy(v_h[:, tt * 1024 + dvg * 512: tt * 1024 + (dvg + 1) * 512], p[:]),
                                   deps=[tpe, C.war.get(("v_h", 0)), C.war.get(("pB1", 0))], inc="ve")
                        C.war[("pbank", bank)] = tev
                    C.war[("wvb", 0)] = t.tok("pe")
                vh_ready = t.tok("ve")
                for ng in range(4):
                    et_rd = C.war.get(("et", 0))
                    exp_toks = []
                    for mt in range(TT):
                        bank = mt % 4
                        pb = C.P[bank]
                        bank_free = C.war.get(("pbank", bank))
                        tps = t.op("tensor",
                                   lambda e, p=pb, hp=hp, hsub=hsub, mt=mt, ng=ng:
                                   e.matmul(p[:], k_fm[hsub * 64:(hsub + 1) * 64, hp * L + mt * 128: hp * L + (mt + 1) * 128],
                                            q_fm[hsub * 64:(hsub + 1) * 64, hp * L + ng * 512: hp * L + (ng + 1) * 512],
                                            start=True, stop=True),
                                   deps=[qk_ready, bank_free, et_rd], inc="pe")
                        tex = t.op("scalar",
                                   lambda e, p=pb, mt=mt: e.activation(et[:, mt * 512:(mt + 1) * 512], p[:], AF.Exp),
                                   deps=[tps, C.war.get(("pWE", 0))], inc="sc")
                        exp_toks.append(tex)
                        C.war[("pbank", bank)] = tex
                    for nl in range(4):
                        nt = ng * 4 + nl
                        sb_ = C.P[4]
                        s_free = C.war.get(("pbank", 4))
                        tsm = None
                        for mt in range(TT):
                            tsm = t.op("tensor",
                                       lambda e, p=sb_, mt=mt, nl=nl:
                                       e.matmul(p[:, 0:1], et[:, mt * 512 + nl * 128: mt * 512 + (nl + 1) * 128],
                                                ones[:], start=(mt == 0), stop=(mt == TT - 1)),
                                       deps=([s_free, ones_tok] if mt == 0 else []) + [exp_toks[mt]],
                                       inc="pe" if mt == TT - 1 else None)
                        trc = t.op("vector", lambda e, p=sb_: e.reciprocal(rc[:], p[:, 0:1]),
                                   deps=[tsm, C.war.get(("rc", 0))], inc="ve")
                        C.war[("pbank", 4)] = trc
                        tve = None
                        for dg in range(2):
                            bank = 5 + dg
                            pb = C.P[bank]
                            bank_free = C.war.get(("pbank", bank))
                            tpv = None
                            for mt in range(TT):
                                tpv = t.op("tensor",
                                           lambda e, p=pb, mt=mt, nl=nl, dg=dg:
                                           e.matmul(p[:], et[:, mt * 512 + nl * 128: mt * 512 + (nl + 1) * 128],
                                                    v_h[:, mt * 1024 + dg * 512: mt * 1024 + (dg + 1) * 512],
                                                    start=(mt == 0), stop=(mt == TT - 1)),
                                           deps=([vh_ready, bank_free] if mt == 0 else []) + [exp_toks[mt]],
                                           inc="pe" if mt == TT - 1 else None)
                            xslc = C.x_sb[:, nt * 1024 + dg * 512: nt * 1024 + (dg + 1) * 512]
                            tve = t.op("vector",
                                       lambda e, o=xslc, p=pb:
                                       e.scalar_tensor_tensor(o, p[:], rc[:, 0:1], o, ALU.mult, ALU.add),
                                       deps=[tpv, trc, C.x_tok[nt]], inc="ve")
                            C.war[("pbank", bank)] = tve
                            C.x_tok[nt] = tve
                        C.war[("rc", 0)] = tve
                    C.war[("et", 0)] = t.tok("pe")
                C.war[("v_h", 0)] = t.tok("pe")
            for k_ in (("lnfm", 0), ("lnfm", 1), ("pB1", 0), ("pB2", 0), ("pWE", 0)):
                C.war[k_] = t.tok("pe")
            swt_sb = B2[:, 0:8192]
            xn_g = XN[:, 0:4096]
            xn_gB = XN[:, 4096:8192]
            dma(C, "sync", "swt", swt_sb[:].rearrange("p (c o) -> p c o", o=1024),
                d_swt.rearrange("(c p) o -> p c o", p=128), deps=pp + [C.war.get(("swtsb", 0)), C.war.get(("pB2", 0))])
            swt_ld = t.tok("d_swt")
            C.war[("pB2dma", 0)] = swt_ld
            xgs = [xn_g, xn_gB]
            xn_next = ln_to_fm(C, [0, 1, 2, 3], xgs[0], 512, lambda tt: tt * 128, war_key=0)
            for tg in range(4):
                xg = xgs[tg % 2]
                xn_tok = xn_next
                if tg < 3:
                    nts = list(range((tg + 1) * 4, (tg + 1) * 4 + 4))
                    xn_next = ln_to_fm(C, nts, xgs[(tg + 1) % 2], 512,
                                       lambda tt, tg=tg: (tt - (tg + 1) * 4) * 128,
                                       war_key=(tg + 1) % 2)
                for tl in range(4):
                    tt = tg * 4 + tl
                    for og in range(2):
                        bank = tl * 2 + og
                        pb = C.P[bank]
                        bank_free = C.war.get(("pbank", bank))
                        tpe = None
                        for ic in range(IC):
                            tpe = t.op("tensor",
                                       lambda e, p=pb, ic=ic, tl=tl, og=og, xg=xg, w=swt_sb:
                                       e.matmul(p[:], xg[:, ic * 512 + tl * 128: ic * 512 + (tl + 1) * 128],
                                                w[:, ic * 1024 + og * 512: ic * 1024 + (og + 1) * 512],
                                                start=(ic == 0), stop=(ic == IC - 1)),
                                       deps=[xn_tok, swt_ld, bank_free] if ic == 0 else [],
                                       inc="pe" if ic == IC - 1 else None)
                        xslc = C.x_sb[:, tt * 1024 + og * 512: tt * 1024 + (og + 1) * 512]
                        tve = t.op("vector", lambda e, o=xslc, p=pb: e.tensor_add(o, o, p[:]),
                                   deps=[tpe, C.x_tok[tt]], inc="ve")
                        C.war[("pbank", bank)] = tve
                        C.x_tok[tt] = tve
                C.war[("lnfm", tg % 2)] = t.tok("pe")
            for k_ in (("lnfm", 0), ("lnfm", 1), ("pB1", 0), ("pB2", 0), ("pWE", 0)):
                C.war[k_] = t.tok("pe")
            C.war[("swtsb", 0)] = t.tok("pe")
            store_x(C, d_xo, s)
            t.flush()
        t.op("sync", lambda e: e.wait_ge(C.t.sem["d_xs"], C.t.val["d_xs"]))
        t.flush()
    return nc


# version 15
# speedup vs baseline: 1.0659x; 1.0659x over previous
import numpy as np
import ml_dtypes

import concourse.bass as bass
import concourse.mybir as mybir
from concourse.bass_utils import run_bass_kernel_spmd
from contextlib import ExitStack
from types import SimpleNamespace

BFNP = ml_dtypes.bfloat16
F32 = mybir.dt.float32
BF16 = mybir.dt.bfloat16
AF = mybir.ActivationFunctionType
ALU = mybir.AluOpType
AX = mybir.AxisListType

NCORES = 8
S = 4
L = 2048
D = 1024
TT = L // 128
IC = D // 128
DH = 4096
LN_EPS = 1e-5
HEAD_EPS = 1024.0 * 1e-6

ENGS = ("sync", "gpsimd", "tensor", "vector", "scalar")
SELF_SEM = {"tensor": "pe", "scalar": "sc", "vector": "ve", "gpsimd": "pl"}


class Trk:

    def __init__(self, nc):
        self.nc = nc
        self.q = {e: [] for e in ENGS}
        self.sem = {}
        self.val = {}
        self.mark = {}

    def add_sem(self, name, handle):
        self.sem[name] = handle
        self.val[name] = 0

    def op(self, eng, emit, deps=(), inc=None, inc_by=1):
        for dep in deps:
            if dep is None:
                continue
            s, v = dep
            if v <= 0 or s == SELF_SEM.get(eng):
                continue
            k = (eng, s)
            if self.mark.get(k, 0) < v:
                self.mark[k] = v
                self.q[eng].append(lambda e, h=self.sem[s], v=v: e.wait_ge(h, v))
        own = SELF_SEM.get(eng)
        if eng in ("vector", "scalar"):
            vb = self.val[own]
            if vb > 0:
                self.q[eng].append(lambda e, h=self.sem[own], v=vb: e.wait_ge(h, v))
            inc = own
        if inc is None:
            self.q[eng].append(emit)
            return None
        self.val[inc] += inc_by
        self.q[eng].append(lambda e, emit=emit, h=self.sem[inc], k=inc_by: emit(e).then_inc(h, k))
        return (inc, self.val[inc])

    def tok(self, name):
        return (name, self.val[name])

    def flush(self):
        qs, self.q = self.q, {e: [] for e in ENGS}
        live = [e for e in ENGS if qs[e]]
        if not live:
            return
        with self.nc.Block() as blk:
            for e in live:
                getattr(blk, e)(lambda eng, ops=qs[e]: [f(eng) for f in ops])
        for name in self.sem:
            if not name.startswith("d"):
                for eng in ENGS:
                    k = (eng, name)
                    self.mark[k] = max(self.mark.get(k, 0), self.val[name])


def new_ctx(nc, es, dstreams):
    C = SimpleNamespace()
    C.nc = nc
    C.t = Trk(nc)
    for s in ("pe", "sc", "ve", "pl"):
        C.t.add_sem(s, es.enter_context(nc.semaphore(s)))
    for s in dstreams:
        C.t.add_sem("d_" + s, es.enter_context(nc.semaphore("d_" + s)))
    C.war = {}
    C.x_tok = {}
    return C


def dma(C, eng, stream, out, in_, deps=()):
    return C.t.op(eng, lambda e, o=out, i=in_: e.dma_start(out=o, in_=i),
                  deps=deps, inc="d_" + stream, inc_by=16)


COMMON_STREAMS = ["x", "xs", "c", "pp32a", "pp32b", "pp32c", "st0", "st1", "st2"]



def alloc_common(C, es, nc):
    C.x_sb = es.enter_context(nc.sbuf_tensor("x_sb", [128, TT * 1024], F32))
    C.xn_tm = es.enter_context(nc.sbuf_tensor("xn_tm", [128, 2 * 1024], BF16))
    C.stats = es.enter_context(nc.sbuf_tensor("stats", [128, 80], F32))
    C.ident = es.enter_context(nc.sbuf_tensor("identsb", [128, 128], BF16))
    C.epsc = es.enter_context(nc.sbuf_tensor("epsc", [128, 2], F32))
    C.P = [es.enter_context(nc.psum_tensor(f"pb{i}", [128, 512], F32)) for i in range(8)]


def load_consts(C, d_ident, extra=()):
    dma(C, "sync", "c", C.ident[:], d_ident)
    for sb, dv in extra:
        dma(C, "sync", "c", sb, dv)
    C.const_tok = C.t.tok("d_c")
    C.ident_tok = C.const_tok
    C.t.op("vector", lambda e: e.memset(C.epsc[:, 0:1], LN_EPS))
    C.eps_tok = C.t.op("vector", lambda e: e.memset(C.epsc[:, 1:2], HEAD_EPS), inc="ve")


def load_x(C, d_x, s):
    dep = C.war.get(("x_store", 0))
    for tt in range(TT):
        dma(C, "sync", "x", C.x_sb[:, tt * 1024:(tt + 1) * 1024],
            d_x[s, tt * 128:(tt + 1) * 128, :], deps=[dep])
    wave = C.t.tok("d_x")
    for tt in range(TT):
        C.x_tok[tt] = wave


def store_x(C, d_xo, s):
    for tt in range(TT):
        dma(C, "gpsimd", "xs", d_xo[s, tt * 128:(tt + 1) * 128, :],
            C.x_sb[:, tt * 1024:(tt + 1) * 1024], deps=[C.x_tok[tt]])
    C.war[("x_store", 0)] = C.t.tok("d_xs")


def ln_to_fm(C, tts, dst, ncols, col_of, war_key=0):
    t = C.t
    n_t = len(tts)
    st = C.stats
    mean_all = st[:, 32:32 + n_t]
    var_all = st[:, 48:48 + n_t]
    nb_all = st[:, 64:64 + n_t]
    st_free = C.war.get(("stats", 0))
    for n, tt in enumerate(tts):
        b = n % 2
        xt = C.x_sb[:, tt * 1024:(tt + 1) * 1024]
        bn = st[:, b * 12:(b + 1) * 12]
        grp = bn.rearrange("p (c k) -> p k c", k=3)
        sm, sm2 = st[:, 24 + b: 25 + b], st[:, 26 + b: 27 + b]
        m4 = st[:, 28 + b * 2: 30 + b * 2] if False else st[:, 28:32]
        xdep = C.x_tok.get(tt)
        t.op("vector", lambda e, o=bn, i=xt: e.bn_stats(o[:, 0:6], i[:, 0:512]),
             deps=[xdep, st_free] if n == 0 else [xdep])
        t.op("vector", lambda e, o=bn, i=xt: e.bn_stats(o[:, 6:12], i[:, 512:1024]))
        t.op("vector", lambda e, o=sm, g=grp: e.reduce_sum(o, g[:, 1:2, :], axis=AX.X))
        t.op("vector", lambda e, o=st, g=grp: e.tensor_mul(o[:, 28:32], g[:, 1:2, :], g[:, 1:2, :]))
        t.op("vector", lambda e, o=sm2, i=st: e.reduce_sum(o, i[:, 28:32], axis=AX.X))
        t.op("vector", lambda e, o=mean_all, n=n, i=sm: e.tensor_scalar_mul(o[:, n:n + 1], i, 0.25))
        t.op("vector", lambda e, o=sm, g=grp: e.reduce_sum(o, g[:, 2:3, :], axis=AX.X))
        t.op("vector", lambda e, o=sm, i=sm: e.tensor_scalar_mul(o, i, 1.0 / 1024.0))
        t.op("vector", lambda e, o=sm2, i=sm2: e.tensor_scalar_mul(o, i, 0.25))
        t.op("vector", lambda e, o=sm, i=sm2: e.tensor_add(o, o, i))
        t.op("vector", lambda e, o=sm2, i=mean_all, n=n: e.tensor_mul(o, i[:, n:n + 1], i[:, n:n + 1]))
        t.op("vector", lambda e, o=var_all, n=n, a=sm, bb=sm2: e.tensor_sub(o[:, n:n + 1], a, bb))
    tve = t.op("vector", lambda e: e.engine_nop(), inc="ve")
    tsd = t.op("scalar", lambda e, o=var_all, i=var_all, ep=C.epsc[:, 0:1]:
               e.activation(o, i, AF.Sqrt, bias=ep),
               deps=[tve, C.eps_tok], inc="sc")
    t.op("vector", lambda e, o=var_all, i=var_all: e.reciprocal(o, i), deps=[tsd])
    tnb = t.op("vector", lambda e, o=nb_all, i=mean_all, r=var_all:
               e.tensor_tensor(o, i, r, ALU.mult), inc="ve")
    tnb = t.op("vector", lambda e, o=nb_all, i=nb_all: e.tensor_scalar_mul(o, i, -1.0), inc="ve")
    C.war[("stats", 0)] = tnb
    C.lncol = col_of
    C.lndst_war = C.war.get(("lnfm", war_key))
    dstv = dst[:].rearrange("p (c n) -> p c n", n=ncols)
    pend = None
    last = None
    for n, tt in enumerate(tts):
        b = n % 2
        xt = C.x_sb[:, tt * 1024:(tt + 1) * 1024]
        xn = C.xn_tm[:, b * 1024:(b + 1) * 1024]
        xn_free = C.war.get(("xn_tm", b))
        taf = t.op("scalar", lambda e, o=xn, i=xt, r=var_all, bb=nb_all, n=n:
                   e.activation(o, i, AF.Identity, scale=r[:, n:n + 1], bias=bb[:, n:n + 1]),
                   deps=[tnb, xn_free, C.x_tok.get(tt)], inc="sc")
        tpes = []
        for half in range(2):
            bank = 4 + b * 2 + half
            pb = C.P[bank]
            bank_free = C.war.get(("pbank", bank))
            tpe = None
            for j in range(4):
                ic = half * 4 + j
                tpe = t.op("tensor",
                           lambda e, p=pb, j=j, i=xn, ic=ic, idt=C.ident:
                           e.matmul(p[:, j * 128:(j + 1) * 128], i[:, ic * 128:(ic + 1) * 128], idt[:],
                                    start=True, stop=True),
                           deps=[taf, bank_free, C.ident_tok], inc="pe")
            tpes.append((bank, tpe))
        if pend is not None:
            last = _ln_evac(C, dstv, ncols, pend)
        pend = (tt, b, tpes)
    last = _ln_evac(C, dstv, ncols, pend)
    C.war[("stats", 0)] = last
    return last


def _ln_evac(C, dstv, ncols, pend):
    t = C.t
    tt, b, tpes = pend
    co_fn = pend
    last = None
    for half, (bank, tpe) in enumerate(tpes):
        pb = C.P[bank]
        tev = t.op("scalar",
                   lambda e, p=pb, dv=dstv, h=half, co=C.lncol(tt):
                   e.copy(dv[:, h * 4:(h + 1) * 4, co:co + 128],
                          p[:].rearrange("p (c n) -> p c n", n=128)),
                   deps=[tpe, C.lndst_war], inc="sc")
        C.war[("pbank", bank)] = tev
        C.war[("xn_tm", b)] = tev
        last = tev
    return last



def prepass_transpose(C, es, nc, jobs):
    t = C.t
    with ExitStack() as ps:
        maxI = max(j["I"] for j in jobs)
        wnat32 = [ps.enter_context(nc.sbuf_tensor(f"pp_w32{b}", [128, maxI], F32)) for b in range(3)]
        wnat = [ps.enter_context(nc.sbuf_tensor(f"pp_wbf{b}", [128, maxI], BF16)) for b in range(3)]
        asm = [ps.enter_context(nc.sbuf_tensor(f"pp_asm{b}", [128, maxI], BF16)) for b in range(3)]
        for job in jobs:
            O, I = job["O"], job["I"]
            src_, dst = job["src"], job["dst"]
            for oc in range(O // 128):
                b = oc % 3
                wb = wnat[b]
                w32 = wnat32[b]
                dma(C, "sync" if b != 1 else "gpsimd", "pp32%s" % "abc"[b], w32[:, 0:I],
                    src_[oc * 128:(oc + 1) * 128, :], deps=[C.war.get(("pp_w32", b))])
                tl = t.tok("d_pp32%s" % "abc"[b])
                wb_free = C.war.get(("pp_wbf", b))
                if job.get("sign"):
                    tc = t.op("scalar", lambda e, o=wb, i=w32, I=I: e.activation(o[:, 0:I], i[:, 0:I], AF.Sign),
                              deps=[tl, wb_free], inc="sc")
                elif job.get("scale") is not None:
                    scol = job["scale"][:, oc:oc + 1]
                    tc = t.op("scalar", lambda e, o=wb, i=w32, sl=scol, I=I:
                              e.activation(o[:, 0:I], i[:, 0:I], AF.Identity, scale=sl),
                              deps=[tl, wb_free, job.get("scale_tok")], inc="sc")
                else:
                    eng_c = "scalar" if oc % 2 == 0 else "vector"
                    tc = t.op(eng_c, lambda e, o=wb, i=w32, I=I:
                              (e.copy(o[:, 0:I], i[:, 0:I]) if hasattr(e, "copy")
                               else e.tensor_copy(o[:, 0:I], i[:, 0:I])),
                              deps=[tl, wb_free], inc="sc" if eng_c == "scalar" else "ve")
                C.war[("pp_w32", b)] = tc
                st_stream = "st%d" % (oc % 3)
                asm_free = C.war.get(("pp_asm", b))
                if job["kind"] == "cast":
                    dma(C, "sync", st_stream, dst[oc * 128:(oc + 1) * 128, :], wb[:, 0:I],
                        deps=[tc])
                    C.war[("pp_wbf", b)] = t.tok("d_" + st_stream)
                    continue
                tev = None
                nic = I // 128
                for g4 in range(0, nic, 4):
                    pbank = (g4 // 4) % 4
                    pb = C.P[pbank]
                    kk = min(4, nic - g4)
                    tpe = None
                    for j in range(kk):
                        ic = g4 + j
                        tpe = t.op("tensor",
                                   lambda e, p=pb, j=j, w=wb, ic=ic, idt=C.ident:
                                   e.matmul(p[:, j * 128:(j + 1) * 128], w[:, ic * 128:(ic + 1) * 128], idt[:],
                                            start=True, stop=True),
                                   deps=[tc, C.ident_tok, C.war.get(("pp_pb", pbank)), asm_free], inc="pe")
                    tev = t.op("vector", lambda e, o=asm[b], g4=g4, p=pb, kk=kk:
                               e.tensor_copy(o[:, g4 * 128: (g4 + kk) * 128], p[:, 0:kk * 128]),
                               deps=[tpe], inc="ve")
                    C.war[("pp_pb", pbank)] = tev
                C.war[("pp_wbf", b)] = tev
                if job["kind"] == "panel":
                    out_ap = dst[oc]
                    in_ap = asm[b][:, 0:I].rearrange("p (c o) -> p c o", o=128)
                else:
                    out_ap = dst[:, oc * 128:(oc + 1) * 128].rearrange("(c p) o -> p c o", p=128)
                    in_ap = asm[b][:, 0:I].rearrange("p (c o) -> p c o", o=128)
                dma(C, "sync", st_stream, out_ap, in_ap, deps=[tev])
                C.war[("pp_asm", b)] = t.tok("d_" + st_stream)
        done = [t.tok("d_st0"), t.tok("d_st1"), t.tok("d_st2")]
        t.flush()
    fence = done + [t.tok("d_pp32a"), t.tok("d_pp32b"), t.tok("d_pp32c")]
    for eng in ENGS:
        t.op(eng, lambda e: e.nop(), deps=fence)
    t.flush()
    C.prepass_done = done
    return done



def build_l1_ffn():
    nc = bass.Bass("TRN2", target_bir_lowering=False)
    d_x = nc.dram_tensor("x_in", [S, L, D], F32, kind="ExternalInput").ap()
    d_w = nc.dram_tensor("w", [D, D], F32, kind="ExternalInput").ap()
    d_ident = nc.dram_tensor("ident", [128, 128], BF16, kind="ExternalInput").ap()
    d_xo = nc.dram_tensor("x_out", [S, L, D], F32, kind="ExternalOutput").ap()
    d_swt = nc.dram_tensor("swt", [D, D], BF16, kind="Internal").ap()

    with ExitStack() as es:
        C = new_ctx(nc, es, COMMON_STREAMS + ["swt"])
        alloc_common(C, es, nc)
        load_consts(C, d_ident)
        pp = prepass_transpose(C, es, nc, [dict(
            src=d_w, dst=d_swt, kind="plain", O=D, I=D, sign=True)])

        with ExitStack() as ms:
            swt_sb = ms.enter_context(nc.sbuf_tensor("swt_sb", [128, IC * 1024], BF16))
            xn_g = ms.enter_context(nc.sbuf_tensor("xn_g", [128, IC * 512], BF16))
            xn_gB = ms.enter_context(nc.sbuf_tensor("xn_gB", [128, IC * 512], BF16))
            t = C.t
            swt_ld = None
            for s in range(S):
                load_x(C, d_x, s)
                if s == 0:
                    dma(C, "sync", "swt", swt_sb[:].rearrange("p (c o) -> p c o", o=1024),
                        d_swt.rearrange("(c p) o -> p c o", p=128), deps=pp)
                    swt_ld = t.tok("d_swt")
                C.war[("pB2dma", 0)] = swt_ld
                xgs = [xn_g, xn_gB]
                xn_next = ln_to_fm(C, [0, 1, 2, 3], xgs[0], 512, lambda tt: tt * 128, war_key=0)
                for tg in range(4):
                    xg = xgs[tg % 2]
                    xn_tok = xn_next
                    if tg < 3:
                        nts = list(range((tg + 1) * 4, (tg + 1) * 4 + 4))
                        xn_next = ln_to_fm(C, nts, xgs[(tg + 1) % 2], 512,
                                           lambda tt, tg=tg: (tt - (tg + 1) * 4) * 128,
                                           war_key=(tg + 1) % 2)
                    for tl in range(4):
                        tt = tg * 4 + tl
                        for og in range(2):
                            bank = tl * 2 + og
                            pb = C.P[bank]
                            bank_free = C.war.get(("pbank", bank))
                            tpe = None
                            for ic in range(IC):
                                tpe = t.op("tensor",
                                           lambda e, p=pb, ic=ic, tl=tl, og=og, xg=xg, w=swt_sb:
                                           e.matmul(p[:], xg[:, ic * 512 + tl * 128: ic * 512 + (tl + 1) * 128],
                                                    w[:, ic * 1024 + og * 512: ic * 1024 + (og + 1) * 512],
                                                    start=(ic == 0), stop=(ic == IC - 1)),
                                           deps=[xn_tok, swt_ld, bank_free] if ic == 0 else [],
                                           inc="pe" if ic == IC - 1 else None)
                            xslc = C.x_sb[:, tt * 1024 + og * 512: tt * 1024 + (og + 1) * 512]
                            tve = t.op("vector", lambda e, o=xslc, p=pb: e.tensor_add(o, o, p[:]),
                                       deps=[tpe, C.x_tok[tt]], inc="ve")
                            C.war[("pbank", bank)] = tve
                            C.x_tok[tt] = tve
                    C.war[("lnfm", tg % 2)] = t.tok("pe")
                for k_ in (("lnfm", 0), ("lnfm", 1), ("pB1", 0), ("pB2", 0), ("pWE", 0)):
                    C.war[k_] = t.tok("pe")
                store_x(C, d_xo, s)
                t.flush()
            t.op("sync", lambda e: e.wait_ge(C.t.sem["d_xs"], C.t.val["d_xs"]))
            t.flush()
    return nc



def build_l0_ffn():
    nc = bass.Bass("TRN2", target_bir_lowering=False)
    d_x = nc.dram_tensor("x_in", [S, L, D], F32, kind="ExternalInput").ap()
    d_w1 = nc.dram_tensor("w1", [DH, D], F32, kind="ExternalInput").ap()
    d_w2 = nc.dram_tensor("w2", [DH, D], F32, kind="ExternalInput").ap()
    d_h1 = nc.dram_tensor("h1", [DH, DH], F32, kind="ExternalInput").ap()
    d_h2 = nc.dram_tensor("h2", [DH, DH], F32, kind="ExternalInput").ap()
    d_w3 = nc.dram_tensor("w3", [D, DH], F32, kind="ExternalInput").ap()
    d_s1 = nc.dram_tensor("s1rs", [128, 32], F32, kind="ExternalInput").ap()
    d_s2 = nc.dram_tensor("s2rs", [128, 32], F32, kind="ExternalInput").ap()
    d_h1s = nc.dram_tensor("h1rs", [128, 32], F32, kind="ExternalInput").ap()
    d_h2s = nc.dram_tensor("h2rs", [128, 32], F32, kind="ExternalInput").ap()
    d_w3s = nc.dram_tensor("w3rs", [128, 8], F32, kind="ExternalInput").ap()
    d_ident = nc.dram_tensor("ident", [128, 128], BF16, kind="ExternalInput").ap()
    d_xo = nc.dram_tensor("x_out", [S, L, D], F32, kind="ExternalOutput").ap()
    d_w1t = nc.dram_tensor("w1t", [DH // 128, 128, D // 128, 128], BF16, kind="Internal").ap()
    d_w2t = nc.dram_tensor("w2t", [DH // 128, 128, D // 128, 128], BF16, kind="Internal").ap()
    d_h1t = nc.dram_tensor("h1t", [DH // 128, 128, DH // 128, 128], BF16, kind="Internal").ap()
    d_h2t = nc.dram_tensor("h2t", [DH // 128, 128, DH // 128, 128], BF16, kind="Internal").ap()
    d_w3t = nc.dram_tensor("w3t", [DH, D], BF16, kind="Internal").ap()

    with ExitStack() as es:
        C = new_ctx(nc, es, COMMON_STREAMS + ["w12a", "w12b", "ha", "hb", "w3a", "w3b"])
        alloc_common(C, es, nc)
        scol = es.enter_context(nc.sbuf_tensor("scol", [128, 4 * 32 + 8], F32))
        s1c, s2c, h1c, h2c = (scol[:, i * 32:(i + 1) * 32] for i in range(4))
        w3c = scol[:, 128:136]
        load_consts(C, d_ident, extra=[(s1c, d_s1), (s2c, d_s2), (h1c, d_h1s),
                                       (h2c, d_h2s), (w3c, d_w3s)])
        scol_tok = C.const_tok
        pp = prepass_transpose(C, es, nc, [
            dict(src=d_w1, dst=d_w1t, kind="panel", O=DH, I=D),
            dict(src=d_w2, dst=d_w2t, kind="panel", O=DH, I=D),
            dict(src=d_h1, dst=d_h1t, kind="panel", O=DH, I=DH),
            dict(src=d_h2, dst=d_h2t, kind="panel", O=DH, I=DH),
            dict(src=d_w3, dst=d_w3t, kind="plain", O=D, I=DH,
                 scale=w3c, scale_tok=scol_tok),
        ])

        with ExitStack() as ms:
            xn_g = ms.enter_context(nc.sbuf_tensor("xn_g", [128, IC * 512], BF16))
            xn_gB = ms.enter_context(nc.sbuf_tensor("xn_gB", [128, IC * 512], BF16))
            tA = ms.enter_context(nc.sbuf_tensor("tA", [128, 32 * 512], BF16))
            tB = ms.enter_context(nc.sbuf_tensor("tB", [128, 32 * 512], BF16))
            tsil = ms.enter_context(nc.sbuf_tensor("tsil", [128, 2 * 512], BF16))
            w12b = ms.enter_context(nc.sbuf_tensor("w12sb", [128, 4 * 1024], BF16))
            hbuf = ms.enter_context(nc.sbuf_tensor("hbsb", [128, 2 * 4096], BF16))
            t = C.t

            def h_stage(src, dst, panel_scratch, scale_cols, name, src_tok, dst_war=None, dst_war2=None):
                tsc = None
                for oc in range(32):
                    slot = oc % 2
                    pb_free = C.war.get(("hbuf", slot))
                    dma(C, "sync", "h%s" % "ab"[slot],
                        hbuf[:, slot * 4096:(slot + 1) * 4096].rearrange("p (c o) -> p c o", o=128),
                        panel_scratch[oc], deps=[pb_free, C.war.get(("pWE", 0))] + pp)
                    tl_ = t.tok("d_h%s" % "ab"[slot])
                    bank = oc % 4
                    pb = C.P[bank]
                    bank_free = C.war.get(("pbank", bank))
                    tpe = None
                    for ic in range(32):
                        tpe = t.op("tensor",
                                   lambda e, p=pb, sl=slot, ic=ic, x=src:
                                   e.matmul(p[:], hbuf[:, sl * 4096 + ic * 128: sl * 4096 + (ic + 1) * 128],
                                            x[:, ic * 512:(ic + 1) * 512],
                                            start=(ic == 0), stop=(ic == 31)),
                                   deps=([tl_, bank_free, src_tok]) if ic == 0 else [],
                                   inc="pe" if ic == 31 else None)
                    C.war[("hbuf", slot)] = tpe
                    tsc = t.op("scalar",
                               lambda e, p=pb, oc=oc, dd=dst, sc=scale_cols:
                               e.activation(dd[:, oc * 512:(oc + 1) * 512], p[:], AF.Gelu, scale=sc[:, oc:oc + 1]),
                               deps=[tpe, scol_tok, dst_war, dst_war2], inc="sc")
                    C.war[("pbank", bank)] = tsc
                return t.tok("sc")

            for s in range(S):
                load_x(C, d_x, s)
                xgs = [xn_g, xn_gB]
                xn_next = ln_to_fm(C, [0, 1, 2, 3], xgs[0], 512, lambda tt: tt * 128, war_key=0)
                for tg in range(4):
                    xg = xgs[tg % 2]
                    xn_tok = xn_next
                    for oc in range(32):
                        slot = oc % 2
                        w_free = C.war.get(("w12sb", slot))
                        dma(C, "sync", "w12%s" % "ab"[slot],
                            w12b[:, slot * 2048: slot * 2048 + 1024].rearrange("p (c o) -> p c o", o=128),
                            d_w1t[oc], deps=[w_free, C.war.get(("pWE", 0))] + pp)
                        dma(C, "sync", "w12%s" % "ab"[slot],
                            w12b[:, slot * 2048 + 1024: slot * 2048 + 2048].rearrange("p (c o) -> p c o", o=128),
                            d_w2t[oc])
                        tlw = t.tok("d_w12%s" % "ab"[slot])
                        ba, bb_ = (oc % 2) * 2, (oc % 2) * 2 + 1
                        pa, pb_ = C.P[ba], C.P[bb_]
                        fa, fb = C.war.get(("pbank", ba)), C.war.get(("pbank", bb_))
                        for ic in range(IC):
                            t.op("tensor",
                                 lambda e, p=pa, sl=slot, ic=ic:
                                 e.matmul(p[:], w12b[:, sl * 2048 + ic * 128: sl * 2048 + (ic + 1) * 128],
                                          xg[:, ic * 512:(ic + 1) * 512], start=(ic == 0), stop=(ic == IC - 1)),
                                 deps=[tlw, fa, xn_tok] if ic == 0 else [],
                                 inc="pe" if ic == IC - 1 else None)
                        tpa = t.tok("pe")
                        for ic in range(IC):
                            t.op("tensor",
                                 lambda e, p=pb_, sl=slot, ic=ic:
                                 e.matmul(p[:], w12b[:, sl * 2048 + 1024 + ic * 128: sl * 2048 + 1024 + (ic + 1) * 128],
                                          xg[:, ic * 512:(ic + 1) * 512], start=(ic == 0), stop=(ic == IC - 1)),
                                 deps=[fb] if ic == 0 else [],
                                 inc="pe" if ic == IC - 1 else None)
                        tpb = t.tok("pe")
                        C.war[("w12sb", slot)] = tpb
                        sl_ap = tsil[:, slot * 512:(slot + 1) * 512]
                        sl_free = C.war.get(("tsil", slot))
                        tss = t.op("scalar",
                                   lambda e, o=sl_ap, p=pa, oc=oc:
                                   e.activation(o, p[:], AF.Silu, scale=s1c[:, oc:oc + 1]),
                                   deps=[tpa, scol_tok, sl_free], inc="sc")
                        C.war[("pbank", ba)] = tss
                        tvv = t.op("vector",
                                   lambda e, o=tA, oc=oc, p=pb_, sl=sl_ap:
                                   e.scalar_tensor_tensor(o[:, oc * 512:(oc + 1) * 512], p[:],
                                                          s2c[:, oc:oc + 1], sl, ALU.mult, ALU.mult),
                                   deps=[tpb, tss, C.war.get(("tAkey", 0)), C.war.get(("pB1", 0))], inc="ve")
                        C.war[("pbank", bb_)] = tvv
                        C.war[("tsil", slot)] = tvv
                    tA_tok = t.tok("ve")
                    C.war[("lnfm", tg % 2)] = t.tok("pe")
                    if tg < 3:
                        nts = list(range((tg + 1) * 4, (tg + 1) * 4 + 4))
                        xn_next = ln_to_fm(C, nts, xgs[(tg + 1) % 2], 512,
                                           lambda tt, tg=tg: (tt - (tg + 1) * 4) * 128,
                                           war_key=(tg + 1) % 2)
                    g1_tok = h_stage(tA, tB, d_h1t, h1c, "h1", tA_tok,
                                     dst_war=C.war.get(("w3tb", 0)), dst_war2=C.war.get(("pB2", 0)))
                    g2_tok = h_stage(tB, tA, d_h2t, h2c, "h2", g1_tok)
                    w3read = None
                    for og in range(2):
                        dma(C, "sync", "w3%s" % "ab"[og],
                            tB[:].rearrange("p (c o) -> p c o", o=512),
                            d_w3t[:, og * 512:(og + 1) * 512].rearrange("(c p) o -> p c o", p=128),
                            deps=[g2_tok, w3read] + pp)
                        tl3 = t.tok("d_w3%s" % "ab"[og])
                        for tl_i in range(4):
                            bank = og * 4 + tl_i
                            pb = C.P[bank]
                            bank_free = C.war.get(("pbank", bank))
                            tpe = None
                            for ic in range(32):
                                tpe = t.op("tensor",
                                           lambda e, p=pb, ic=ic, tl_i=tl_i:
                                           e.matmul(p[:], tA[:, ic * 512 + tl_i * 128: ic * 512 + (tl_i + 1) * 128],
                                                    tB[:, ic * 512:(ic + 1) * 512],
                                                    start=(ic == 0), stop=(ic == 31)),
                                           deps=[tl3, bank_free, g2_tok] if ic == 0 else [],
                                           inc="pe" if ic == 31 else None)
                            tt = tg * 4 + tl_i
                            xslc = C.x_sb[:, tt * 1024 + og * 512: tt * 1024 + (og + 1) * 512]
                            tve = t.op("vector", lambda e, o=xslc, p=pb: e.tensor_add(o, o, p[:]),
                                       deps=[tpe, C.x_tok[tt]], inc="ve")
                            C.war[("pbank", bank)] = tve
                            C.x_tok[tt] = tve
                        w3read = t.tok("pe")
                    C.war[("tAkey", 0)] = t.tok("pe")
                    C.war[("w3tb", 0)] = t.tok("pe")
                    t.flush()
                for k_ in (("lnfm", 0), ("lnfm", 1), ("pB1", 0), ("pB2", 0), ("pWE", 0)):
                    C.war[k_] = t.tok("pe")
                store_x(C, d_xo, s)
                t.flush()
            t.op("sync", lambda e: e.wait_ge(C.t.sem["d_xs"], C.t.val["d_xs"]))
            t.flush()
    return nc



def build_l0_attn():
    nc = bass.Bass("TRN2", target_bir_lowering=False)
    d_x = nc.dram_tensor("x_in", [S, L, D], F32, kind="ExternalInput").ap()
    d_wqk = nc.dram_tensor("wqk", [D, 2048], F32, kind="ExternalInput").ap()
    d_wv = nc.dram_tensor("wv", [D, D], F32, kind="ExternalInput").ap()
    d_wo = nc.dram_tensor("wo", [D, D], F32, kind="ExternalInput").ap()
    d_aff = nc.dram_tensor("aff", [128, 4], F32, kind="ExternalInput").ap()
    d_ident = nc.dram_tensor("ident", [128, 128], BF16, kind="ExternalInput").ap()
    d_masks = nc.dram_tensor("masks", [4, 128, 512], BF16, kind="ExternalInput").ap()
    d_xo = nc.dram_tensor("x_out", [S, L, D], F32, kind="ExternalOutput").ap()
    d_wqk16 = nc.dram_tensor("wqk16", [D, 2048], BF16, kind="Internal").ap()
    d_wv16 = nc.dram_tensor("wv16", [D, D], BF16, kind="Internal").ap()
    d_wo16 = nc.dram_tensor("wo16", [D, D], BF16, kind="Internal").ap()

    with ExitStack() as es:
        C = new_ctx(nc, es, COMMON_STREAMS + ["wqk", "wv", "wo"])
        alloc_common(C, es, nc)
        masks = es.enter_context(nc.sbuf_tensor("maskssb", [128, 4 * 512], BF16))
        affc = es.enter_context(nc.sbuf_tensor("affc", [128, 4], F32))
        c32 = es.enter_context(nc.sbuf_tensor("c32", [1, 64], BF16))
        t = C.t
        load_consts(C, d_ident,
                    extra=[(masks[:, j * 512:(j + 1) * 512], d_masks[j]) for j in range(4)]
                          + [(affc[:], d_aff)])
        const_tok = C.const_tok
        pp = prepass_transpose(C, es, nc, [
            dict(src=d_wqk, dst=d_wqk16, kind="cast", O=D, I=2048),
            dict(src=d_wv, dst=d_wv16, kind="cast", O=D, I=D),
            dict(src=d_wo, dst=d_wo16, kind="cast", O=D, I=D),
        ])
        ms = es.enter_context(ExitStack())
        xn_fm = ms.enter_context(nc.sbuf_tensor("xn_fm", [128, IC * L], BF16))
        qk_fm = ms.enter_context(nc.sbuf_tensor("qk_fm", [128, 4 * L], BF16))
        v1 = ms.enter_context(nc.sbuf_tensor("v1", [128, TT * 260], BF16))
        aout = ms.enter_context(nc.sbuf_tensor("aout", [128, 8 * L], BF16))
        qn0 = ms.enter_context(nc.sbuf_tensor("qn0", [128, 4 * 512], BF16))
        et = ms.enter_context(nc.sbuf_tensor("et", [128, 8 * 512], BF16))
        rb = ms.enter_context(nc.sbuf_tensor("rb", [128, 512], F32))
        rr = ms.enter_context(nc.sbuf_tensor("rr", [1, 1024], BF16))
        wqk_sb = ms.enter_context(nc.sbuf_tensor("wqk_sb", [128, IC * 512], BF16))
        wv_sb = ms.enter_context(nc.sbuf_tensor("wv_sb", [128, IC * 256], BF16))
        wo_sb = ms.enter_context(nc.sbuf_tensor("wo_sb", [128, IC * 512], BF16))
        hstat = ms.enter_context(nc.sbuf_tensor("hstat", [128, 2 * 96], F32))
        t.op("vector", lambda e: e.memset(c32[:], 1.0 / 32.0))
        c32_tok = t.op("vector", lambda e: e.memset(v1[:], 1.0), inc="ve")

        for s in range(S):
            load_x(C, d_x, s)
            xn_tok = ln_to_fm(C, list(range(TT)), xn_fm, L, lambda tt: tt * 128)
            v1_rd = C.war.get(("v1", 0))
            for hg in range(4):
                wqk_free = C.war.get(("wqk_sb", 0))
                for ic in range(IC):
                    dma(C, "sync", "wqk", wqk_sb[:, ic * 512:(ic + 1) * 512],
                        d_wqk16[ic * 128:(ic + 1) * 128, hg * 512:(hg + 1) * 512],
                        deps=[wqk_free] + pp)
                wqk_tok = t.tok("d_wqk")
                wv_free = C.war.get(("wv_sb", 0))
                for ic in range(IC):
                    dma(C, "sync", "wv", wv_sb[:, ic * 256:(ic + 1) * 256],
                        d_wv16[ic * 128:(ic + 1) * 128, hg * 256:(hg + 1) * 256],
                        deps=[wv_free] + pp)
                wv_tok = t.tok("d_wv")
                def emit_qtrans(p_):
                    tt_, q4_, tseg_ = p_
                    tb = 3 + tt_ % 2
                    pb2 = C.P[tb]
                    b2_free = C.war.get(("pbank", tb))
                    qn = qn0[:, q4_ * 512:(q4_ + 1) * 512]
                    tpt = None
                    for j in range(4):
                        tpt = t.op("tensor",
                                   lambda e, p=pb2, j=j, i=qn:
                                   e.matmul(p[:, j * 128:(j + 1) * 128], i[:, j * 128:(j + 1) * 128], C.ident[:],
                                            start=True, stop=True),
                                   deps=[tseg_, b2_free, C.ident_tok], inc="pe")
                    qv = qk_fm[:].rearrange("p (c n) -> p c n", n=L)
                    pv3 = pb2[:].rearrange("p (c n) -> p c n", n=128)
                    qfm_free = C.war.get(("pB2dma", 0))
                    tqe = None
                    for cix in range(4):
                        isq = (cix % 2 == 0)
                        tqe = t.op("scalar",
                                   lambda e, qv=qv, pv3=pv3, cix=cix, tt=tt_, isq=isq:
                                   e.activation(qv[:, cix, tt * 128:(tt + 1) * 128],
                                                pv3[:, cix, :], AF.Identity,
                                                scale=affc[:, 0:1] if isq else affc[:, 1:2],
                                                bias=affc[:, 2:3] if isq else affc[:, 3:4]),
                                   deps=[tpt, const_tok, qfm_free], inc="sc")
                    C.war[("pbank", tb)] = tqe
                    C.war[("qn0", q4_)] = tqe
                    return tqe

                pend_q = []
                pend2 = None

                def emit_segs(p2):
                    tt_, b_, q4_, pb_ = p2["tt"], p2["b"], p2["q4"], p2["pb"]
                    st_ = hstat[:, b_ * 96:(b_ + 1) * 96]
                    mean8_ = st_[:, 48:56]; var8_ = st_[:, 80:88]; r8_ = st_[:, 88:96]
                    trc = t.op("vector", lambda e, o=r8_, i=var8_: e.reciprocal(o, i), deps=[p2["tsd"]], inc="ve")
                    qn = qn0[:, q4_ * 512:(q4_ + 1) * 512]
                    qn_free = C.war.get(("qn0", q4_))
                    tseg = None
                    for sg in range(8):
                        tseg = t.op("vector",
                                    lambda e, o=qn, i=pb_, sg=sg, m=mean8_, r=r8_:
                                    e.tensor_scalar(o[:, sg * 64:(sg + 1) * 64], i[:, sg * 64:(sg + 1) * 64],
                                                    m[:, sg:sg + 1], r[:, sg:sg + 1], ALU.subtract, ALU.mult),
                                    deps=[trc, qn_free, C.war.get(("pB2dma", 0))] if sg == 0 else [],
                                    inc="ve" if sg == 7 else None)
                    C.war[("hstat", b_)] = tseg
                    C.war[("pbank", p2["b3"])] = tseg
                    pend_q.append((tt_, q4_, tseg))
                    if len(pend_q) > 2:
                        emit_qtrans(pend_q.pop(0))

                for tt in range(TT):
                    b3 = tt % 3
                    b = tt % 2
                    q4 = tt % 4
                    pb = C.P[b3]
                    bank_free = C.war.get(("pbank", b3))
                    tpe = None
                    for ic in range(IC):
                        tpe = t.op("tensor",
                                   lambda e, p=pb, ic=ic, tt=tt:
                                   e.matmul(p[:], xn_fm[:, ic * L + tt * 128: ic * L + (tt + 1) * 128],
                                            wqk_sb[:, ic * 512:(ic + 1) * 512],
                                            start=(ic == 0), stop=(ic == IC - 1)),
                                   deps=[xn_tok, wqk_tok, bank_free] if ic == 0 else [],
                                   inc="pe" if ic == IC - 1 else None)
                    st = hstat[:, b * 96:(b + 1) * 96]
                    v6 = st[:, 0:48].rearrange("p (c k) -> p k c", k=6)
                    mean8 = st[:, 48:56]; sM28 = st[:, 56:64]
                    t1 = st[:, 64:72]; t2 = st[:, 72:80]
                    var8 = st[:, 80:88]
                    st_free = C.war.get(("hstat", b))
                    for sg8 in range(8):
                        t.op("vector", lambda e, o=st, i=pb, sg8=sg8:
                             e.bn_stats(o[:, sg8 * 6: sg8 * 6 + 6], i[:, sg8 * 64:(sg8 + 1) * 64]),
                             deps=[tpe, st_free] if sg8 == 0 else [])
                    t.op("vector", lambda e, o=mean8, v=v6: e.tensor_add(o, v[:, 1:2, :], v[:, 4:5, :]))
                    t.op("vector", lambda e, o=mean8, i=mean8: e.tensor_scalar_mul(o, i, 0.5))
                    t.op("vector", lambda e, o=t1, v=v6: e.tensor_mul(o, v[:, 1:2, :], v[:, 1:2, :]))
                    t.op("vector", lambda e, o=t2, v=v6: e.tensor_mul(o, v[:, 4:5, :], v[:, 4:5, :]))
                    t.op("vector", lambda e, o=t1, i=t2: e.tensor_add(o, o, i))
                    t.op("vector", lambda e, o=t1, i=t1: e.tensor_scalar_mul(o, i, 0.5))
                    t.op("vector", lambda e, o=sM28, v=v6: e.tensor_add(o, v[:, 2:3, :], v[:, 5:6, :]))
                    t.op("vector", lambda e, o=sM28, i=sM28: e.tensor_scalar_mul(o, i, 1.0 / 64.0))
                    t.op("vector", lambda e, o=sM28, i=t1: e.tensor_add(o, o, i))
                    t.op("vector", lambda e, o=t2, i=mean8: e.tensor_mul(o, i, i))
                    tvv = t.op("vector", lambda e, o=var8, a=sM28, bb=t2: e.tensor_sub(o, a, bb), inc="ve")
                    tsd = t.op("scalar", lambda e, o=var8, i=var8:
                               e.activation(o, i, AF.Sqrt, bias=C.epsc[:, 1:2]),
                               deps=[tvv, C.eps_tok], inc="sc")
                    if pend2 is not None:
                        emit_segs(pend2)
                    pend2 = dict(tt=tt, b=b, q4=q4, pb=pb, b3=b3, tsd=tsd)
                    vbank = 5 + tt % 3
                    vpb = C.P[vbank]
                    vbank_free = C.war.get(("pbank", vbank))
                    tpeV = None
                    for ic in range(IC):
                        tpeV = t.op("tensor",
                                    lambda e, p=vpb, ic=ic, tt=tt:
                                    e.matmul(p[:, 0:256], xn_fm[:, ic * L + tt * 128: ic * L + (tt + 1) * 128],
                                             wv_sb[:, ic * 256:(ic + 1) * 256],
                                             start=(ic == 0), stop=(ic == IC - 1)),
                                    deps=[xn_tok, wv_tok, vbank_free, c32_tok, v1_rd] if ic == 0 else [],
                                    inc="pe" if ic == IC - 1 else None)
                    tve = t.op("scalar",
                               lambda e, p=vpb, tt=tt:
                               e.activation(v1[:, tt * 260: tt * 260 + 260].rearrange("p (h dd) -> p h dd", dd=65)[:, :, 0:64],
                                            p[:, 0:256].rearrange("p (h dd) -> p h dd", dd=64), AF.Identity, scale=1.0 / 32.0),
                               deps=[tpeV], inc="sc")
                    C.war[("pbank", vbank)] = tve
                emit_segs(pend2)
                pend2 = None
                for p_ in pend_q:
                    emit_qtrans(p_)
                pend_q = []
                qk_ready = t.tok("sc")
                v_ready = qk_ready
                def emit_norm(p_):
                    tbc = t.op("tensor", lambda e, p=C.P[4], sl=p_["slot"]: e.matmul(p[0:64, :], c32[:, 0:64],
                                                                                    rr[0:1, sl * 512:(sl + 1) * 512],
                                                                                    start=True, stop=True),
                               deps=[p_["trr"], C.war.get(("pbank", 4)), c32_tok], inc="pe")
                    C.war[("rr", p_["slot"])] = tbc
                    trb = t.op("scalar", lambda e, p=C.P[4]: e.copy(rb[0:64, :], p[0:64, :]),
                               deps=[tbc, C.war.get(("rb", 0))], inc="sc")
                    C.war[("pbank", 4)] = trb
                    h_, qg_ = p_["h"], p_["qg"]
                    hp_, hs_ = h_ // 2, h_ % 2
                    tao = t.op("vector",
                               lambda e, p=p_["pvb"], hp=hp_, hs2=hs_, qg=qg_:
                               e.tensor_mul(aout[hs2 * 64:(hs2 + 1) * 64, hp * L + qg * 512: hp * L + (qg + 1) * 512],
                                            p[0:64, :], rb[0:64, :]),
                               deps=[trb, C.war.get(("aoutw", 0))], inc="ve")
                    C.war[("pbank", p_["bank"])] = tao
                    C.war[("rb", 0)] = tao

                pends = []
                for pr in range(2):
                    qch, kch = 2 * pr, 2 * pr + 1
                    for qg in range(4):
                        pvbank = [6, 7]
                        pvb = [C.P[6], C.P[7]]
                        pv_free = None
                        n_kt = 4 * qg + 4
                        tpv = [None, None]
                        for b0 in range(0, n_kt, 4):
                            kts = range(b0, min(b0 + 4, n_kt))
                            etoks = {}
                            for kt in kts:
                                for hsub in range(2):
                                    sbank = (0, 1, 2, 3)[(2 * kt + hsub) % 4]
                                    psb = C.P[sbank]
                                    ps_free = C.war.get(("pbank", sbank))
                                    tps = t.op("tensor",
                                               lambda e, p=psb, kch=kch, qch=qch, hsub=hsub, kt=kt, qg=qg:
                                               e.matmul(p[:], qk_fm[hsub * 64:(hsub + 1) * 64, kch * L + kt * 128: kch * L + (kt + 1) * 128],
                                                        qk_fm[hsub * 64:(hsub + 1) * 64, qch * L + qg * 512: qch * L + (qg + 1) * 512],
                                                        start=True, stop=True),
                                               deps=[qk_ready, ps_free], inc="pe")
                                    eslot = (2 * kt + hsub) % 8
                                    e_free = C.war.get(("et", eslot))
                                    tex = t.op("scalar",
                                               lambda e, p=psb, es_=eslot:
                                               e.activation(et[:, es_ * 512:(es_ + 1) * 512], p[:], AF.Exp, scale=1.0 / 64.0),
                                               deps=[tps, e_free], inc="sc")
                                    C.war[("pbank", sbank)] = tex
                                    etok = tex
                                    if kt >= 4 * qg:
                                        j = kt - 4 * qg
                                        etok = t.op("vector",
                                                    lambda e, es_=eslot, j=j:
                                                    e.tensor_mul(et[:, es_ * 512:(es_ + 1) * 512],
                                                                 et[:, es_ * 512:(es_ + 1) * 512],
                                                                 masks[:, j * 512:(j + 1) * 512]),
                                                    deps=[tex, const_tok], inc="ve")
                                    etoks[(kt, hsub)] = etok
                            if pends:
                                for p_ in pends:
                                    emit_norm(p_)
                                pends = []
                            if pv_free is None:
                                pv_free = [C.war.get(("pbank", 6)), C.war.get(("pbank", 7))]
                            for kt in kts:
                                for hsub in range(2):
                                    hl = 2 * pr + hsub
                                    eslot = (2 * kt + hsub) % 8
                                    tpv[hsub] = t.op("tensor",
                                               lambda e, p=pvb[hsub], kt=kt, hl=hl, u=eslot, n_kt=n_kt:
                                               e.matmul(p[0:65, :], v1[:, kt * 260 + hl * 65: kt * 260 + hl * 65 + 65],
                                                        et[:, u * 512:(u + 1) * 512], start=(kt == 0), stop=(kt == n_kt - 1)),
                                               deps=[etoks[(kt, hsub)], v_ready] + ([pv_free[hsub]] if kt == 0 else []),
                                               inc="pe")
                                    C.war[("et", eslot)] = tpv[hsub]
                        for hsub in range(2):
                            def _recip(e, p=pvb[hsub], sl=hsub):
                                with C.nc.allow_low_precision(reason="bf16 softmax recip operand"):
                                    return e.reciprocal(rr[0:1, sl * 512:(sl + 1) * 512], p[64:65, :])
                            trr = t.op("vector", _recip, deps=[tpv[hsub], C.war.get(("rr", hsub))], inc="ve")
                            pends.append(dict(trr=trr, pvb=pvb[hsub], bank=pvbank[hsub], slot=hsub,
                                              h=hg * 4 + 2 * pr + hsub, qg=qg))
                for p_ in pends:
                    emit_norm(p_)
                pends = []
                C.war[("wqk_sb", 0)] = t.tok("pe")
                C.war[("wv_sb", 0)] = t.tok("pe")
            aout_tok = t.tok("ve")
            C.war[("v1", 0)] = t.tok("pe")
            for og in range(2):
                wo_free = C.war.get(("wo_sb", 0))
                for hp in range(IC):
                    dma(C, "sync", "wo", wo_sb[:, hp * 512:(hp + 1) * 512],
                        d_wo16[hp * 128:(hp + 1) * 128, og * 512:(og + 1) * 512],
                        deps=[wo_free] + pp)
                wo_tok = t.tok("d_wo")
                for tt in range(TT):
                    bank = tt % 4
                    pb = C.P[bank]
                    bank_free = C.war.get(("pbank", bank))
                    tpe = None
                    for hp in range(IC):
                        tpe = t.op("tensor",
                                   lambda e, p=pb, hp=hp, tt=tt:
                                   e.matmul(p[:], aout[:, hp * L + tt * 128: hp * L + (tt + 1) * 128],
                                            wo_sb[:, hp * 512:(hp + 1) * 512],
                                            start=(hp == 0), stop=(hp == IC - 1)),
                                   deps=[aout_tok, wo_tok, bank_free] if hp == 0 else [],
                                   inc="pe" if hp == IC - 1 else None)
                    xslc = C.x_sb[:, tt * 1024 + og * 512: tt * 1024 + (og + 1) * 512]
                    tve = t.op("vector", lambda e, o=xslc, p=pb: e.tensor_add(o, o, p[:]),
                               deps=[tpe, C.x_tok[tt]], inc="ve")
                    C.war[("pbank", bank)] = tve
                    C.x_tok[tt] = tve
                C.war[("wo_sb", 0)] = t.tok("pe")
            C.war[("aoutw", 0)] = t.tok("pe")
            for k_ in (("lnfm", 0), ("lnfm", 1), ("pB1", 0), ("pB2", 0), ("pWE", 0)):
                C.war[k_] = t.tok("pe")
            store_x(C, d_xo, s)
            t.flush()
        t.op("sync", lambda e: e.wait_ge(C.t.sem["d_xs"], C.t.val["d_xs"]))
        t.flush()
    return nc



def build_l1_attn():
    H1 = 8
    nc = bass.Bass("TRN2", target_bir_lowering=False)
    d_x = nc.dram_tensor("x_in", [S, L, D], F32, kind="ExternalInput").ap()
    d_wq = nc.dram_tensor("wq", [512, D], F32, kind="ExternalInput").ap()
    d_wk = nc.dram_tensor("wk", [512, D], F32, kind="ExternalInput").ap()
    d_wv = nc.dram_tensor("wv", [8192, D], F32, kind="ExternalInput").ap()
    d_ident = nc.dram_tensor("ident", [128, 128], BF16, kind="ExternalInput").ap()
    d_xo = nc.dram_tensor("x_out", [S, L, D], F32, kind="ExternalOutput").ap()
    d_wqt = nc.dram_tensor("wqt", [4, 128, IC, 128], BF16, kind="Internal").ap()
    d_wkt = nc.dram_tensor("wkt", [4, 128, IC, 128], BF16, kind="Internal").ap()
    d_wvt = nc.dram_tensor("wvt", [D, 8192], BF16, kind="Internal").ap()

    with ExitStack() as es:
        C = new_ctx(nc, es, COMMON_STREAMS + ["qka", "qkb", "wvb"])
        alloc_common(C, es, nc)
        load_consts(C, d_ident)
        pp = prepass_transpose(C, es, nc, [
            dict(src=d_wq, dst=d_wqt, kind="panel", O=512, I=D),
            dict(src=d_wk, dst=d_wkt, kind="panel", O=512, I=D),
            dict(src=d_wv, dst=d_wvt, kind="plain", O=8192, I=D),
        ])
        with ExitStack() as ms:
            xn_fm = ms.enter_context(nc.sbuf_tensor("xn_fm", [128, IC * L], BF16))
            q_fm = ms.enter_context(nc.sbuf_tensor("q_fm", [128, 4 * L], BF16))
            k_fm = ms.enter_context(nc.sbuf_tensor("k_fm", [128, 4 * L], BF16))
            v_h = ms.enter_context(nc.sbuf_tensor("v_h", [128, TT * 1024], BF16))
            et = ms.enter_context(nc.sbuf_tensor("et", [128, TT * 512], BF16))
            wqkb = ms.enter_context(nc.sbuf_tensor("wqkb", [128, 2 * 1024], BF16))
            wvb = ms.enter_context(nc.sbuf_tensor("wvb", [128, IC * 512], BF16))
            ones = ms.enter_context(nc.sbuf_tensor("ones", [128, 1], BF16))
            rc = ms.enter_context(nc.sbuf_tensor("rc", [128, 1], F32))
            t = C.t
            ones_tok = t.op("vector", lambda e: e.memset(ones[:], 1.0), inc="ve")

            for s in range(S):
                load_x(C, d_x, s)
                xn_tok = ln_to_fm(C, list(range(TT)), xn_fm, L, lambda tt: tt * 128)
                for which, dscr, dstb in (("q", d_wqt, q_fm), ("k", d_wkt, k_fm)):
                    for oc in range(4):
                        slot = oc % 2
                        w_free = C.war.get(("wqkb", slot))
                        dma(C, "sync", "qk%s" % "ab"[slot],
                            wqkb[:, slot * 1024:(slot + 1) * 1024].rearrange("p (c o) -> p c o", o=128),
                            dscr[oc], deps=[w_free, C.war.get(("pWE", 0))] + pp)
                        twl = t.tok("d_qk%s" % "ab"[slot])
                        for ng in range(4):
                            bank = (oc * 4 + ng) % 2
                            pb = C.P[bank]
                            bank_free = C.war.get(("pbank", bank))
                            tpe = None
                            for ic in range(IC):
                                tpe = t.op("tensor",
                                           lambda e, p=pb, sl=slot, ic=ic, ng=ng:
                                           e.matmul(p[:], wqkb[:, sl * 1024 + ic * 128: sl * 1024 + (ic + 1) * 128],
                                                    xn_fm[:, ic * L + ng * 512: ic * L + (ng + 1) * 512],
                                                    start=(ic == 0), stop=(ic == IC - 1)),
                                           deps=[twl, bank_free, xn_tok] if ic == 0 else [],
                                           inc="pe" if ic == IC - 1 else None)
                            tev = t.op("vector",
                                       lambda e, p=pb, dd=dstb, oc=oc, ng=ng:
                                       e.tensor_copy(dd[:, oc * L + ng * 512: oc * L + (ng + 1) * 512], p[:]),
                                       deps=[tpe, C.war.get((which + "_fm", 0)), C.war.get(("pB2", 0))], inc="ve")
                            C.war[("pbank", bank)] = tev
                        C.war[("wqkb", slot)] = t.tok("pe")
                qk_ready = t.tok("ve")
                for h in range(H1):
                    hp, hsub = h // 2, h % 2
                    for dvg in range(2):
                        wv_free = C.war.get(("wvb", 0))
                        dma(C, "sync", "wvb", wvb[:].rearrange("p (c o) -> p c o", o=512),
                            d_wvt[:, h * 1024 + dvg * 512: h * 1024 + (dvg + 1) * 512]
                            .rearrange("(c p) o -> p c o", p=128),
                            deps=[wv_free, C.war.get(("pWE", 0))] + pp)
                        twv = t.tok("d_wvb")
                        for tt in range(TT):
                            bank = (2, 3, 6, 7)[tt % 4]
                            pb = C.P[bank]
                            bank_free = C.war.get(("pbank", bank))
                            tpe = None
                            for ic in range(IC):
                                tpe = t.op("tensor",
                                           lambda e, p=pb, ic=ic, tt=tt:
                                           e.matmul(p[:], xn_fm[:, ic * L + tt * 128: ic * L + (tt + 1) * 128],
                                                    wvb[:, ic * 512:(ic + 1) * 512],
                                                    start=(ic == 0), stop=(ic == IC - 1)),
                                           deps=[twv, bank_free, xn_tok] if ic == 0 else [],
                                           inc="pe" if ic == IC - 1 else None)
                            tev = t.op("vector",
                                       lambda e, p=pb, tt=tt, dvg=dvg:
                                       e.tensor_copy(v_h[:, tt * 1024 + dvg * 512: tt * 1024 + (dvg + 1) * 512], p[:]),
                                       deps=[tpe, C.war.get(("v_h", 0)), C.war.get(("pB1", 0))], inc="ve")
                            C.war[("pbank", bank)] = tev
                        C.war[("wvb", 0)] = t.tok("pe")
                    vh_ready = t.tok("ve")
                    for ng in range(4):
                        et_rd = C.war.get(("et", 0))
                        exp_toks = []
                        for mt in range(TT):
                            bank = mt % 4
                            pb = C.P[bank]
                            bank_free = C.war.get(("pbank", bank))
                            tps = t.op("tensor",
                                       lambda e, p=pb, hp=hp, hsub=hsub, mt=mt, ng=ng:
                                       e.matmul(p[:], k_fm[hsub * 64:(hsub + 1) * 64, hp * L + mt * 128: hp * L + (mt + 1) * 128],
                                                q_fm[hsub * 64:(hsub + 1) * 64, hp * L + ng * 512: hp * L + (ng + 1) * 512],
                                                start=True, stop=True),
                                       deps=[qk_ready, bank_free, et_rd], inc="pe")
                            tex = t.op("scalar",
                                       lambda e, p=pb, mt=mt: e.activation(et[:, mt * 512:(mt + 1) * 512], p[:], AF.Exp),
                                       deps=[tps, C.war.get(("pWE", 0))], inc="sc")
                            exp_toks.append(tex)
                            C.war[("pbank", bank)] = tex
                        for nl in range(4):
                            nt = ng * 4 + nl
                            sb_ = C.P[4]
                            s_free = C.war.get(("pbank", 4))
                            tsm = None
                            for mt in range(TT):
                                tsm = t.op("tensor",
                                           lambda e, p=sb_, mt=mt, nl=nl:
                                           e.matmul(p[:, 0:1], et[:, mt * 512 + nl * 128: mt * 512 + (nl + 1) * 128],
                                                    ones[:], start=(mt == 0), stop=(mt == TT - 1)),
                                           deps=([s_free, ones_tok] if mt == 0 else []) + [exp_toks[mt]],
                                           inc="pe" if mt == TT - 1 else None)
                            trc = t.op("vector", lambda e, p=sb_: e.reciprocal(rc[:], p[:, 0:1]),
                                       deps=[tsm, C.war.get(("rc", 0))], inc="ve")
                            C.war[("pbank", 4)] = trc
                            tve = None
                            for dg in range(2):
                                bank = 5 + dg
                                pb = C.P[bank]
                                bank_free = C.war.get(("pbank", bank))
                                tpv = None
                                for mt in range(TT):
                                    tpv = t.op("tensor",
                                               lambda e, p=pb, mt=mt, nl=nl, dg=dg:
                                               e.matmul(p[:], et[:, mt * 512 + nl * 128: mt * 512 + (nl + 1) * 128],
                                                        v_h[:, mt * 1024 + dg * 512: mt * 1024 + (dg + 1) * 512],
                                                        start=(mt == 0), stop=(mt == TT - 1)),
                                               deps=([vh_ready, bank_free] if mt == 0 else []) + [exp_toks[mt]],
                                               inc="pe" if mt == TT - 1 else None)
                                xslc = C.x_sb[:, nt * 1024 + dg * 512: nt * 1024 + (dg + 1) * 512]
                                tve = t.op("vector",
                                           lambda e, o=xslc, p=pb:
                                           e.scalar_tensor_tensor(o, p[:], rc[:, 0:1], o, ALU.mult, ALU.add),
                                           deps=[tpv, trc, C.x_tok[nt]], inc="ve")
                                C.war[("pbank", bank)] = tve
                                C.x_tok[nt] = tve
                            C.war[("rc", 0)] = tve
                        C.war[("et", 0)] = t.tok("pe")
                    C.war[("v_h", 0)] = t.tok("pe")
                for k_ in (("lnfm", 0), ("lnfm", 1), ("pB1", 0), ("pB2", 0), ("pWE", 0)):
                    C.war[k_] = t.tok("pe")
                store_x(C, d_xo, s)
                t.flush()
            t.op("sync", lambda e: e.wait_ge(C.t.sem["d_xs"], C.t.val["d_xs"]))
            t.flush()
    return nc



def _consts():
    ident = np.eye(128, dtype=np.float32).astype(BFNP)
    masks = np.zeros((4, 128, 512), np.float32)
    for j in range(4):
        for kp in range(128):
            masks[j, kp, j * 128 + kp:] = 1.0
    return ident, masks.astype(BFNP)


def _run(nc, per_core_maps):
    return run_bass_kernel_spmd(nc, per_core_maps, core_ids=list(range(NCORES)))


def _aff_cols(inp):
    f32 = np.float32
    qs, ks = np.asarray(inp["L0_attn_q_scale"], f32), np.asarray(inp["L0_attn_k_scale"], f32)
    qb, kb = np.asarray(inp["L0_attn_q_bias"], f32), np.asarray(inp["L0_attn_k_bias"], f32)
    return np.ascontiguousarray(np.stack(
        [np.concatenate([qs, qs]), np.concatenate([ks, ks]),
         np.concatenate([qb, qb]), np.concatenate([kb, kb])], axis=1).astype(f32))


def _pair_wqk(w):
    w = np.asarray(w, np.float32).reshape(1024, 4, 2, 2, 2, 64)
    return np.ascontiguousarray(w.transpose(0, 1, 2, 4, 3, 5).reshape(1024, 2048))


def kernel(**inp):
    ident, masks = _consts()
    f32 = np.float32
    x = np.ascontiguousarray(inp["L0_attn_inputs"], dtype=f32)
    xs = [np.ascontiguousarray(v) for v in np.split(x, NCORES, axis=0)]
    rs = lambda v, n: np.ascontiguousarray(np.asarray(v, f32).reshape(n, 128).T)

    nc = build_fused()
    shared = dict(wqk=_pair_wqk(inp["L0_attn_W_qk"]),
                  wv0=np.ascontiguousarray(inp["L0_attn_W_v"], f32),
                  wo=np.ascontiguousarray(inp["L0_attn_W_out"], f32),
                  aff=_aff_cols(inp), masks=masks, ident=ident,
                  w1=np.ascontiguousarray(inp["L0_ffn_w1_w"], f32),
                  w2=np.ascontiguousarray(inp["L0_ffn_w2_w"], f32),
                  h1=np.ascontiguousarray(inp["L0_ffn_h1_w"], f32),
                  h2=np.ascontiguousarray(inp["L0_ffn_h2_w"], f32),
                  w3=np.ascontiguousarray(inp["L0_ffn_w3_w"], f32),
                  s1rs=rs(inp["L0_ffn_w1_s"], 32), s2rs=rs(inp["L0_ffn_w2_s"], 32),
                  h1rs=rs(inp["L0_ffn_h1_s"], 32), h2rs=rs(inp["L0_ffn_h2_s"], 32),
                  w3rs=rs(inp["L0_ffn_w3_s"], 8),
                  wq=np.ascontiguousarray(inp["L1_attn_Wq"], f32),
                  wk=np.ascontiguousarray(inp["L1_attn_Wk"], f32),
                  wv1=np.ascontiguousarray(inp["L1_attn_Wv"], f32),
                  wl1f=np.ascontiguousarray(inp["L1_ffn_weight"], f32))
    maps = [dict(x_in=xs[c], **shared) for c in range(NCORES)]
    res = _run(nc, maps)
    return np.concatenate([np.asarray(r["x_out"]) for r in res.results], axis=0)
























def build_fused():
    H1 = 8
    nc = bass.Bass("TRN2", target_bir_lowering=False)
    d_x = nc.dram_tensor("x_in", [S, L, D], F32, kind="ExternalInput").ap()
    d_wqk = nc.dram_tensor("wqk", [D, 2048], F32, kind="ExternalInput").ap()
    d_wv0 = nc.dram_tensor("wv0", [D, D], F32, kind="ExternalInput").ap()
    d_wo = nc.dram_tensor("wo", [D, D], F32, kind="ExternalInput").ap()
    d_aff = nc.dram_tensor("aff", [128, 4], F32, kind="ExternalInput").ap()
    d_masks = nc.dram_tensor("masks", [4, 128, 512], BF16, kind="ExternalInput").ap()
    d_w1 = nc.dram_tensor("w1", [DH, D], F32, kind="ExternalInput").ap()
    d_w2 = nc.dram_tensor("w2", [DH, D], F32, kind="ExternalInput").ap()
    d_h1 = nc.dram_tensor("h1", [DH, DH], F32, kind="ExternalInput").ap()
    d_h2 = nc.dram_tensor("h2", [DH, DH], F32, kind="ExternalInput").ap()
    d_w3 = nc.dram_tensor("w3", [D, DH], F32, kind="ExternalInput").ap()
    d_s1 = nc.dram_tensor("s1rs", [128, 32], F32, kind="ExternalInput").ap()
    d_s2 = nc.dram_tensor("s2rs", [128, 32], F32, kind="ExternalInput").ap()
    d_h1s = nc.dram_tensor("h1rs", [128, 32], F32, kind="ExternalInput").ap()
    d_h2s = nc.dram_tensor("h2rs", [128, 32], F32, kind="ExternalInput").ap()
    d_w3s = nc.dram_tensor("w3rs", [128, 8], F32, kind="ExternalInput").ap()
    d_wq = nc.dram_tensor("wq", [512, D], F32, kind="ExternalInput").ap()
    d_wk = nc.dram_tensor("wk", [512, D], F32, kind="ExternalInput").ap()
    d_wv1 = nc.dram_tensor("wv1", [8192, D], F32, kind="ExternalInput").ap()
    d_wl1f = nc.dram_tensor("wl1f", [D, D], F32, kind="ExternalInput").ap()
    d_ident = nc.dram_tensor("ident", [128, 128], BF16, kind="ExternalInput").ap()
    d_xo = nc.dram_tensor("x_out", [S, L, D], F32, kind="ExternalOutput").ap()
    d_wqk16 = nc.dram_tensor("wqk16", [D, 2048], BF16, kind="Internal").ap()
    d_wv16 = nc.dram_tensor("wv16", [D, D], BF16, kind="Internal").ap()
    d_wo16 = nc.dram_tensor("wo16", [D, D], BF16, kind="Internal").ap()
    d_w1t = nc.dram_tensor("w1t", [DH // 128, 128, D // 128, 128], BF16, kind="Internal").ap()
    d_w2t = nc.dram_tensor("w2t", [DH // 128, 128, D // 128, 128], BF16, kind="Internal").ap()
    d_h1t = nc.dram_tensor("h1t", [DH // 128, 128, DH // 128, 128], BF16, kind="Internal").ap()
    d_h2t = nc.dram_tensor("h2t", [DH // 128, 128, DH // 128, 128], BF16, kind="Internal").ap()
    d_w3t = nc.dram_tensor("w3t", [DH, D], BF16, kind="Internal").ap()
    d_wqt = nc.dram_tensor("wqt", [4, 128, IC, 128], BF16, kind="Internal").ap()
    d_wkt = nc.dram_tensor("wkt", [4, 128, IC, 128], BF16, kind="Internal").ap()
    d_wvt = nc.dram_tensor("wvt", [D, 8192], BF16, kind="Internal").ap()
    d_swt = nc.dram_tensor("swt", [D, D], BF16, kind="Internal").ap()

    streams = COMMON_STREAMS + ["wqk", "wv", "wo", "w12a", "w12b", "ha", "hb",
                                "w3a", "w3b", "qka", "qkb", "wvb", "swt"]
    with ExitStack() as es:
        C = new_ctx(nc, es, streams)
        alloc_common(C, es, nc)
        masks = es.enter_context(nc.sbuf_tensor("maskssb", [128, 4 * 512], BF16))
        affc = es.enter_context(nc.sbuf_tensor("affc", [128, 4], F32))
        c32 = es.enter_context(nc.sbuf_tensor("c32", [1, 64], BF16))
        scol = es.enter_context(nc.sbuf_tensor("scol", [128, 4 * 32 + 8], F32))
        s1c, s2c, h1c, h2c = (scol[:, i * 32:(i + 1) * 32] for i in range(4))
        w3c = scol[:, 128:136]
        t = C.t
        load_consts(C, d_ident,
                    extra=[(masks[:, j * 512:(j + 1) * 512], d_masks[j]) for j in range(4)]
                          + [(affc[:], d_aff), (s1c, d_s1), (s2c, d_s2),
                             (h1c, d_h1s), (h2c, d_h2s), (w3c, d_w3s)])
        const_tok = C.const_tok
        scol_tok = C.const_tok
        load_x(C, d_x, 0)
        pp = prepass_transpose(C, es, nc, [
            dict(src=d_wqk, dst=d_wqk16, kind="cast", O=D, I=2048),
            dict(src=d_wv0, dst=d_wv16, kind="cast", O=D, I=D),
            dict(src=d_wo, dst=d_wo16, kind="cast", O=D, I=D),
            dict(src=d_w1, dst=d_w1t, kind="panel", O=DH, I=D),
            dict(src=d_w2, dst=d_w2t, kind="panel", O=DH, I=D),
            dict(src=d_h1, dst=d_h1t, kind="panel", O=DH, I=DH),
            dict(src=d_h2, dst=d_h2t, kind="panel", O=DH, I=DH),
            dict(src=d_w3, dst=d_w3t, kind="plain", O=D, I=DH,
                 scale=w3c, scale_tok=scol_tok),
            dict(src=d_wq, dst=d_wqt, kind="panel", O=512, I=D),
            dict(src=d_wk, dst=d_wkt, kind="panel", O=512, I=D),
            dict(src=d_wv1, dst=d_wvt, kind="plain", O=8192, I=D),
            dict(src=d_wl1f, dst=d_swt, kind="plain", O=D, I=D, sign=True),
        ])
        XN = es.enter_context(nc.sbuf_tensor("poolXN", [128, 16384], BF16)).ap()
        B1 = es.enter_context(nc.sbuf_tensor("poolB1", [128, 16384], BF16)).ap()
        B2 = es.enter_context(nc.sbuf_tensor("poolB2", [128, 16384], BF16)).ap()
        WE = es.enter_context(nc.sbuf_tensor("poolWE", [128, 14336], BF16)).ap()
        rb = es.enter_context(nc.sbuf_tensor("rb", [128, 512], F32))
        rr = es.enter_context(nc.sbuf_tensor("rr", [1, 1024], BF16))
        hstat = es.enter_context(nc.sbuf_tensor("hstat", [128, 2 * 96], F32))
        rc = es.enter_context(nc.sbuf_tensor("rc", [128, 1], F32))
        ones = es.enter_context(nc.sbuf_tensor("ones", [128, 1], BF16))
        tsil = es.enter_context(nc.sbuf_tensor("tsil", [128, 2 * 512], BF16))
        t.op("vector", lambda e: e.memset(c32[:], 1.0 / 32.0))
        ones_tok = t.op("vector", lambda e: e.memset(ones[:], 1.0), inc="ve")
        w12b = WE[:, 0:4096]
        hbuf = WE[:, 4096:12288]
        def h_stage(src, dst, panel_scratch, scale_cols, name, src_tok, dst_war=None, dst_war2=None):
            tsc = None
            for oc in range(32):
                slot = oc % 2
                pb_free = C.war.get(("hbuf", slot))
                dma(C, "sync", "h%s" % "ab"[slot],
                    hbuf[:, slot * 4096:(slot + 1) * 4096].rearrange("p (c o) -> p c o", o=128),
                    panel_scratch[oc], deps=[pb_free, C.war.get(("pWE", 0))] + pp)
                tl_ = t.tok("d_h%s" % "ab"[slot])
                bank = oc % 4
                pb = C.P[bank]
                bank_free = C.war.get(("pbank", bank))
                tpe = None
                for ic in range(32):
                    tpe = t.op("tensor",
                               lambda e, p=pb, sl=slot, ic=ic, x=src:
                               e.matmul(p[:], hbuf[:, sl * 4096 + ic * 128: sl * 4096 + (ic + 1) * 128],
                                        x[:, ic * 512:(ic + 1) * 512],
                                        start=(ic == 0), stop=(ic == 31)),
                               deps=([tl_, bank_free, src_tok]) if ic == 0 else [],
                               inc="pe" if ic == 31 else None)
                C.war[("hbuf", slot)] = tpe
                tsc = t.op("scalar",
                           lambda e, p=pb, oc=oc, dd=dst, sc=scale_cols:
                           e.activation(dd[:, oc * 512:(oc + 1) * 512], p[:], AF.Gelu, scale=sc[:, oc:oc + 1]),
                           deps=[tpe, scol_tok, dst_war, dst_war2], inc="sc")
                C.war[("pbank", bank)] = tsc
            return t.tok("sc")


        for s in range(S):
            if s > 0:
                load_x(C, d_x, s)
            xn_fm = XN
            qk_fm = B2[:, 0:8192]
            v1 = B2[:, 8192:12352]
            qn0 = B2[:, 12352:14400]
            aout = B1
            wqk_sb = WE[:, 0:4096]
            wv_sb = WE[:, 4096:6144]
            wo_sb = WE[:, 6144:10240]
            et = WE[:, 10240:14336]
            c32_tok = t.op("vector", lambda e: e.memset(v1[:], 1.0), deps=[C.war.get(("pB2dma", 0))], inc="ve")
            xn_tok = ln_to_fm(C, list(range(TT)), xn_fm, L, lambda tt: tt * 128)
            v1_rd = C.war.get(("v1", 0))
            for hg in range(4):
                wqk_free = C.war.get(("wqk_sb", 0))
                for ic in range(IC):
                    dma(C, "sync", "wqk", wqk_sb[:, ic * 512:(ic + 1) * 512],
                        d_wqk16[ic * 128:(ic + 1) * 128, hg * 512:(hg + 1) * 512],
                        deps=[wqk_free] + pp)
                wqk_tok = t.tok("d_wqk")
                wv_free = C.war.get(("wv_sb", 0))
                for ic in range(IC):
                    dma(C, "sync", "wv", wv_sb[:, ic * 256:(ic + 1) * 256],
                        d_wv16[ic * 128:(ic + 1) * 128, hg * 256:(hg + 1) * 256],
                        deps=[wv_free] + pp)
                wv_tok = t.tok("d_wv")
                def emit_qtrans(p_):
                    tt_, q4_, tseg_ = p_
                    tb = 3 + tt_ % 2
                    pb2 = C.P[tb]
                    b2_free = C.war.get(("pbank", tb))
                    qn = qn0[:, q4_ * 512:(q4_ + 1) * 512]
                    tpt = None
                    for j in range(4):
                        tpt = t.op("tensor",
                                   lambda e, p=pb2, j=j, i=qn:
                                   e.matmul(p[:, j * 128:(j + 1) * 128], i[:, j * 128:(j + 1) * 128], C.ident[:],
                                            start=True, stop=True),
                                   deps=[tseg_, b2_free, C.ident_tok], inc="pe")
                    qv = qk_fm[:].rearrange("p (c n) -> p c n", n=L)
                    pv3 = pb2[:].rearrange("p (c n) -> p c n", n=128)
                    qfm_free = C.war.get(("pB2dma", 0))
                    tqe = None
                    for cix in range(4):
                        isq = (cix % 2 == 0)
                        tqe = t.op("scalar",
                                   lambda e, qv=qv, pv3=pv3, cix=cix, tt=tt_, isq=isq:
                                   e.activation(qv[:, cix, tt * 128:(tt + 1) * 128],
                                                pv3[:, cix, :], AF.Identity,
                                                scale=affc[:, 0:1] if isq else affc[:, 1:2],
                                                bias=affc[:, 2:3] if isq else affc[:, 3:4]),
                                   deps=[tpt, const_tok, qfm_free], inc="sc")
                    C.war[("pbank", tb)] = tqe
                    C.war[("qn0", q4_)] = tqe
                    return tqe

                pend_q = []
                pend2 = None

                def emit_segs(p2):
                    tt_, b_, q4_, pb_ = p2["tt"], p2["b"], p2["q4"], p2["pb"]
                    st_ = hstat[:, b_ * 96:(b_ + 1) * 96]
                    mean8_ = st_[:, 48:56]; var8_ = st_[:, 80:88]; r8_ = st_[:, 88:96]
                    trc = t.op("vector", lambda e, o=r8_, i=var8_: e.reciprocal(o, i), deps=[p2["tsd"]], inc="ve")
                    qn = qn0[:, q4_ * 512:(q4_ + 1) * 512]
                    qn_free = C.war.get(("qn0", q4_))
                    tseg = None
                    for sg in range(8):
                        tseg = t.op("vector",
                                    lambda e, o=qn, i=pb_, sg=sg, m=mean8_, r=r8_:
                                    e.tensor_scalar(o[:, sg * 64:(sg + 1) * 64], i[:, sg * 64:(sg + 1) * 64],
                                                    m[:, sg:sg + 1], r[:, sg:sg + 1], ALU.subtract, ALU.mult),
                                    deps=[trc, qn_free, C.war.get(("pB2dma", 0))] if sg == 0 else [],
                                    inc="ve" if sg == 7 else None)
                    C.war[("hstat", b_)] = tseg
                    C.war[("pbank", p2["b3"])] = tseg
                    pend_q.append((tt_, q4_, tseg))
                    if len(pend_q) > 2:
                        emit_qtrans(pend_q.pop(0))

                for tt in range(TT):
                    b3 = tt % 3
                    b = tt % 2
                    q4 = tt % 4
                    pb = C.P[b3]
                    bank_free = C.war.get(("pbank", b3))
                    tpe = None
                    for ic in range(IC):
                        tpe = t.op("tensor",
                                   lambda e, p=pb, ic=ic, tt=tt:
                                   e.matmul(p[:], xn_fm[:, ic * L + tt * 128: ic * L + (tt + 1) * 128],
                                            wqk_sb[:, ic * 512:(ic + 1) * 512],
                                            start=(ic == 0), stop=(ic == IC - 1)),
                                   deps=[xn_tok, wqk_tok, bank_free] if ic == 0 else [],
                                   inc="pe" if ic == IC - 1 else None)
                    st = hstat[:, b * 96:(b + 1) * 96]
                    v6 = st[:, 0:48].rearrange("p (c k) -> p k c", k=6)
                    mean8 = st[:, 48:56]; sM28 = st[:, 56:64]
                    t1 = st[:, 64:72]; t2 = st[:, 72:80]
                    var8 = st[:, 80:88]
                    st_free = C.war.get(("hstat", b))
                    for sg8 in range(8):
                        t.op("vector", lambda e, o=st, i=pb, sg8=sg8:
                             e.bn_stats(o[:, sg8 * 6: sg8 * 6 + 6], i[:, sg8 * 64:(sg8 + 1) * 64]),
                             deps=[tpe, st_free] if sg8 == 0 else [])
                    t.op("vector", lambda e, o=mean8, v=v6: e.tensor_add(o, v[:, 1:2, :], v[:, 4:5, :]))
                    t.op("vector", lambda e, o=mean8, i=mean8: e.tensor_scalar_mul(o, i, 0.5))
                    t.op("vector", lambda e, o=t1, v=v6: e.tensor_mul(o, v[:, 1:2, :], v[:, 1:2, :]))
                    t.op("vector", lambda e, o=t2, v=v6: e.tensor_mul(o, v[:, 4:5, :], v[:, 4:5, :]))
                    t.op("vector", lambda e, o=t1, i=t2: e.tensor_add(o, o, i))
                    t.op("vector", lambda e, o=t1, i=t1: e.tensor_scalar_mul(o, i, 0.5))
                    t.op("vector", lambda e, o=sM28, v=v6: e.tensor_add(o, v[:, 2:3, :], v[:, 5:6, :]))
                    t.op("vector", lambda e, o=sM28, i=sM28: e.tensor_scalar_mul(o, i, 1.0 / 64.0))
                    t.op("vector", lambda e, o=sM28, i=t1: e.tensor_add(o, o, i))
                    t.op("vector", lambda e, o=t2, i=mean8: e.tensor_mul(o, i, i))
                    tvv = t.op("vector", lambda e, o=var8, a=sM28, bb=t2: e.tensor_sub(o, a, bb), inc="ve")
                    tsd = t.op("scalar", lambda e, o=var8, i=var8:
                               e.activation(o, i, AF.Sqrt, bias=C.epsc[:, 1:2]),
                               deps=[tvv, C.eps_tok], inc="sc")
                    if pend2 is not None:
                        emit_segs(pend2)
                    pend2 = dict(tt=tt, b=b, q4=q4, pb=pb, b3=b3, tsd=tsd)
                    vbank = 5 + tt % 3
                    vpb = C.P[vbank]
                    vbank_free = C.war.get(("pbank", vbank))
                    tpeV = None
                    for ic in range(IC):
                        tpeV = t.op("tensor",
                                    lambda e, p=vpb, ic=ic, tt=tt:
                                    e.matmul(p[:, 0:256], xn_fm[:, ic * L + tt * 128: ic * L + (tt + 1) * 128],
                                             wv_sb[:, ic * 256:(ic + 1) * 256],
                                             start=(ic == 0), stop=(ic == IC - 1)),
                                    deps=[xn_tok, wv_tok, vbank_free, c32_tok, v1_rd] if ic == 0 else [],
                                    inc="pe" if ic == IC - 1 else None)
                    tve = t.op("scalar",
                               lambda e, p=vpb, tt=tt:
                               e.activation(v1[:, tt * 260: tt * 260 + 260].rearrange("p (h dd) -> p h dd", dd=65)[:, :, 0:64],
                                            p[:, 0:256].rearrange("p (h dd) -> p h dd", dd=64), AF.Identity, scale=1.0 / 32.0),
                               deps=[tpeV], inc="sc")
                    C.war[("pbank", vbank)] = tve
                emit_segs(pend2)
                pend2 = None
                for p_ in pend_q:
                    emit_qtrans(p_)
                pend_q = []
                qk_ready = t.tok("sc")
                v_ready = qk_ready
                def emit_norm(p_):
                    tbc = t.op("tensor", lambda e, p=C.P[4], sl=p_["slot"]: e.matmul(p[0:64, :], c32[:, 0:64],
                                                                                    rr[0:1, sl * 512:(sl + 1) * 512],
                                                                                    start=True, stop=True),
                               deps=[p_["trr"], C.war.get(("pbank", 4)), c32_tok], inc="pe")
                    C.war[("rr", p_["slot"])] = tbc
                    trb = t.op("scalar", lambda e, p=C.P[4]: e.copy(rb[0:64, :], p[0:64, :]),
                               deps=[tbc, C.war.get(("rb", 0))], inc="sc")
                    C.war[("pbank", 4)] = trb
                    h_, qg_ = p_["h"], p_["qg"]
                    hp_, hs_ = h_ // 2, h_ % 2
                    tao = t.op("vector",
                               lambda e, p=p_["pvb"], hp=hp_, hs2=hs_, qg=qg_:
                               e.tensor_mul(aout[hs2 * 64:(hs2 + 1) * 64, hp * L + qg * 512: hp * L + (qg + 1) * 512],
                                            p[0:64, :], rb[0:64, :]),
                               deps=[trb, C.war.get(("aoutw", 0))], inc="ve")
                    C.war[("pbank", p_["bank"])] = tao
                    C.war[("rb", 0)] = tao

                pends = []
                for pr in range(2):
                    qch, kch = 2 * pr, 2 * pr + 1
                    for qg in range(4):
                        pvbank = [6, 7]
                        pvb = [C.P[6], C.P[7]]
                        pv_free = None
                        n_kt = 4 * qg + 4
                        tpv = [None, None]
                        for b0 in range(0, n_kt, 4):
                            kts = range(b0, min(b0 + 4, n_kt))
                            etoks = {}
                            for kt in kts:
                                for hsub in range(2):
                                    sbank = (0, 1, 2, 3)[(2 * kt + hsub) % 4]
                                    psb = C.P[sbank]
                                    ps_free = C.war.get(("pbank", sbank))
                                    tps = t.op("tensor",
                                               lambda e, p=psb, kch=kch, qch=qch, hsub=hsub, kt=kt, qg=qg:
                                               e.matmul(p[:], qk_fm[hsub * 64:(hsub + 1) * 64, kch * L + kt * 128: kch * L + (kt + 1) * 128],
                                                        qk_fm[hsub * 64:(hsub + 1) * 64, qch * L + qg * 512: qch * L + (qg + 1) * 512],
                                                        start=True, stop=True),
                                               deps=[qk_ready, ps_free], inc="pe")
                                    eslot = (2 * kt + hsub) % 8
                                    e_free = C.war.get(("et", eslot))
                                    tex = t.op("scalar",
                                               lambda e, p=psb, es_=eslot:
                                               e.activation(et[:, es_ * 512:(es_ + 1) * 512], p[:], AF.Exp, scale=1.0 / 64.0),
                                               deps=[tps, e_free], inc="sc")
                                    C.war[("pbank", sbank)] = tex
                                    etok = tex
                                    if kt >= 4 * qg:
                                        j = kt - 4 * qg
                                        etok = t.op("vector",
                                                    lambda e, es_=eslot, j=j:
                                                    e.tensor_mul(et[:, es_ * 512:(es_ + 1) * 512],
                                                                 et[:, es_ * 512:(es_ + 1) * 512],
                                                                 masks[:, j * 512:(j + 1) * 512]),
                                                    deps=[tex, const_tok], inc="ve")
                                    etoks[(kt, hsub)] = etok
                            if pends:
                                for p_ in pends:
                                    emit_norm(p_)
                                pends = []
                            if pv_free is None:
                                pv_free = [C.war.get(("pbank", 6)), C.war.get(("pbank", 7))]
                            for kt in kts:
                                for hsub in range(2):
                                    hl = 2 * pr + hsub
                                    eslot = (2 * kt + hsub) % 8
                                    tpv[hsub] = t.op("tensor",
                                               lambda e, p=pvb[hsub], kt=kt, hl=hl, u=eslot, n_kt=n_kt:
                                               e.matmul(p[0:65, :], v1[:, kt * 260 + hl * 65: kt * 260 + hl * 65 + 65],
                                                        et[:, u * 512:(u + 1) * 512], start=(kt == 0), stop=(kt == n_kt - 1)),
                                               deps=[etoks[(kt, hsub)], v_ready] + ([pv_free[hsub]] if kt == 0 else []),
                                               inc="pe")
                                    C.war[("et", eslot)] = tpv[hsub]
                        for hsub in range(2):
                            def _recip(e, p=pvb[hsub], sl=hsub):
                                with C.nc.allow_low_precision(reason="bf16 softmax recip operand"):
                                    return e.reciprocal(rr[0:1, sl * 512:(sl + 1) * 512], p[64:65, :])
                            trr = t.op("vector", _recip, deps=[tpv[hsub], C.war.get(("rr", hsub))], inc="ve")
                            pends.append(dict(trr=trr, pvb=pvb[hsub], bank=pvbank[hsub], slot=hsub,
                                              h=hg * 4 + 2 * pr + hsub, qg=qg))
                for p_ in pends:
                    emit_norm(p_)
                pends = []
                C.war[("wqk_sb", 0)] = t.tok("pe")
                C.war[("wv_sb", 0)] = t.tok("pe")
            aout_tok = t.tok("ve")
            C.war[("v1", 0)] = t.tok("pe")
            for og in range(2):
                wo_free = C.war.get(("wo_sb", 0))
                for hp in range(IC):
                    dma(C, "sync", "wo", wo_sb[:, hp * 512:(hp + 1) * 512],
                        d_wo16[hp * 128:(hp + 1) * 128, og * 512:(og + 1) * 512],
                        deps=[wo_free] + pp)
                wo_tok = t.tok("d_wo")
                for tt in range(TT):
                    bank = tt % 4
                    pb = C.P[bank]
                    bank_free = C.war.get(("pbank", bank))
                    tpe = None
                    for hp in range(IC):
                        tpe = t.op("tensor",
                                   lambda e, p=pb, hp=hp, tt=tt:
                                   e.matmul(p[:], aout[:, hp * L + tt * 128: hp * L + (tt + 1) * 128],
                                            wo_sb[:, hp * 512:(hp + 1) * 512],
                                            start=(hp == 0), stop=(hp == IC - 1)),
                                   deps=[aout_tok, wo_tok, bank_free] if hp == 0 else [],
                                   inc="pe" if hp == IC - 1 else None)
                    xslc = C.x_sb[:, tt * 1024 + og * 512: tt * 1024 + (og + 1) * 512]
                    tve = t.op("vector", lambda e, o=xslc, p=pb: e.tensor_add(o, o, p[:]),
                               deps=[tpe, C.x_tok[tt]], inc="ve")
                    C.war[("pbank", bank)] = tve
                    C.x_tok[tt] = tve
                C.war[("wo_sb", 0)] = t.tok("pe")
            C.war[("aoutw", 0)] = t.tok("pe")
            for k_ in (("lnfm", 0), ("lnfm", 1), ("pB1", 0), ("pB2", 0), ("pWE", 0)):
                C.war[k_] = t.tok("pe")
            xn_g = XN[:, 0:4096]
            xn_gB = XN[:, 4096:8192]
            tA = B1
            tB = B2
            xgs = [xn_g, xn_gB]
            xn_next = ln_to_fm(C, [0, 1, 2, 3], xgs[0], 512, lambda tt: tt * 128, war_key=0)
            for tg in range(4):
                xg = xgs[tg % 2]
                xn_tok = xn_next
                for oc in range(32):
                    slot = oc % 2
                    w_free = C.war.get(("w12sb", slot))
                    dma(C, "sync", "w12%s" % "ab"[slot],
                        w12b[:, slot * 2048: slot * 2048 + 1024].rearrange("p (c o) -> p c o", o=128),
                        d_w1t[oc], deps=[w_free, C.war.get(("pWE", 0))] + pp)
                    dma(C, "sync", "w12%s" % "ab"[slot],
                        w12b[:, slot * 2048 + 1024: slot * 2048 + 2048].rearrange("p (c o) -> p c o", o=128),
                        d_w2t[oc])
                    tlw = t.tok("d_w12%s" % "ab"[slot])
                    ba, bb_ = (oc % 2) * 2, (oc % 2) * 2 + 1
                    pa, pb_ = C.P[ba], C.P[bb_]
                    fa, fb = C.war.get(("pbank", ba)), C.war.get(("pbank", bb_))
                    for ic in range(IC):
                        t.op("tensor",
                             lambda e, p=pa, sl=slot, ic=ic:
                             e.matmul(p[:], w12b[:, sl * 2048 + ic * 128: sl * 2048 + (ic + 1) * 128],
                                      xg[:, ic * 512:(ic + 1) * 512], start=(ic == 0), stop=(ic == IC - 1)),
                             deps=[tlw, fa, xn_tok] if ic == 0 else [],
                             inc="pe" if ic == IC - 1 else None)
                    tpa = t.tok("pe")
                    for ic in range(IC):
                        t.op("tensor",
                             lambda e, p=pb_, sl=slot, ic=ic:
                             e.matmul(p[:], w12b[:, sl * 2048 + 1024 + ic * 128: sl * 2048 + 1024 + (ic + 1) * 128],
                                      xg[:, ic * 512:(ic + 1) * 512], start=(ic == 0), stop=(ic == IC - 1)),
                             deps=[fb] if ic == 0 else [],
                             inc="pe" if ic == IC - 1 else None)
                    tpb = t.tok("pe")
                    C.war[("w12sb", slot)] = tpb
                    sl_ap = tsil[:, slot * 512:(slot + 1) * 512]
                    sl_free = C.war.get(("tsil", slot))
                    tss = t.op("scalar",
                               lambda e, o=sl_ap, p=pa, oc=oc:
                               e.activation(o, p[:], AF.Silu, scale=s1c[:, oc:oc + 1]),
                               deps=[tpa, scol_tok, sl_free], inc="sc")
                    C.war[("pbank", ba)] = tss
                    tvv = t.op("vector",
                               lambda e, o=tA, oc=oc, p=pb_, sl=sl_ap:
                               e.scalar_tensor_tensor(o[:, oc * 512:(oc + 1) * 512], p[:],
                                                      s2c[:, oc:oc + 1], sl, ALU.mult, ALU.mult),
                               deps=[tpb, tss, C.war.get(("tAkey", 0)), C.war.get(("pB1", 0))], inc="ve")
                    C.war[("pbank", bb_)] = tvv
                    C.war[("tsil", slot)] = tvv
                tA_tok = t.tok("ve")
                C.war[("lnfm", tg % 2)] = t.tok("pe")
                if tg < 3:
                    nts = list(range((tg + 1) * 4, (tg + 1) * 4 + 4))
                    xn_next = ln_to_fm(C, nts, xgs[(tg + 1) % 2], 512,
                                       lambda tt, tg=tg: (tt - (tg + 1) * 4) * 128,
                                       war_key=(tg + 1) % 2)
                g1_tok = h_stage(tA, tB, d_h1t, h1c, "h1", tA_tok,
                                 dst_war=C.war.get(("w3tb", 0)), dst_war2=C.war.get(("pB2", 0)))
                g2_tok = h_stage(tB, tA, d_h2t, h2c, "h2", g1_tok)
                w3read = None
                for og in range(2):
                    dma(C, "sync", "w3%s" % "ab"[og],
                        tB[:].rearrange("p (c o) -> p c o", o=512),
                        d_w3t[:, og * 512:(og + 1) * 512].rearrange("(c p) o -> p c o", p=128),
                        deps=[g2_tok, w3read] + pp)
                    tl3 = t.tok("d_w3%s" % "ab"[og])
                    for tl_i in range(4):
                        bank = og * 4 + tl_i
                        pb = C.P[bank]
                        bank_free = C.war.get(("pbank", bank))
                        tpe = None
                        for ic in range(32):
                            tpe = t.op("tensor",
                                       lambda e, p=pb, ic=ic, tl_i=tl_i:
                                       e.matmul(p[:], tA[:, ic * 512 + tl_i * 128: ic * 512 + (tl_i + 1) * 128],
                                                tB[:, ic * 512:(ic + 1) * 512],
                                                start=(ic == 0), stop=(ic == 31)),
                                       deps=[tl3, bank_free, g2_tok] if ic == 0 else [],
                                       inc="pe" if ic == 31 else None)
                        tt = tg * 4 + tl_i
                        xslc = C.x_sb[:, tt * 1024 + og * 512: tt * 1024 + (og + 1) * 512]
                        tve = t.op("vector", lambda e, o=xslc, p=pb: e.tensor_add(o, o, p[:]),
                                   deps=[tpe, C.x_tok[tt]], inc="ve")
                        C.war[("pbank", bank)] = tve
                        C.x_tok[tt] = tve
                    w3read = t.tok("pe")
                C.war[("tAkey", 0)] = t.tok("pe")
                C.war[("w3tb", 0)] = t.tok("pe")
                t.flush()
            for k_ in (("lnfm", 0), ("lnfm", 1), ("pB1", 0), ("pB2", 0), ("pWE", 0)):
                C.war[k_] = t.tok("pe")
                pass
            xn_fm = XN
            q_fm = B2[:, 0:8192]
            k_fm = B2[:, 8192:16384]
            v_h = B1
            et = WE[:, 0:8192]
            wqkb = WE[:, 8192:10240]
            wvb = WE[:, 10240:14336]
            xn_tok = ln_to_fm(C, list(range(TT)), xn_fm, L, lambda tt: tt * 128)
            for which, dscr, dstb in (("q", d_wqt, q_fm), ("k", d_wkt, k_fm)):
                for oc in range(4):
                    slot = oc % 2
                    w_free = C.war.get(("wqkb", slot))
                    dma(C, "sync", "qk%s" % "ab"[slot],
                        wqkb[:, slot * 1024:(slot + 1) * 1024].rearrange("p (c o) -> p c o", o=128),
                        dscr[oc], deps=[w_free, C.war.get(("pWE", 0))] + pp)
                    twl = t.tok("d_qk%s" % "ab"[slot])
                    for ng in range(4):
                        bank = (oc * 4 + ng) % 2
                        pb = C.P[bank]
                        bank_free = C.war.get(("pbank", bank))
                        tpe = None
                        for ic in range(IC):
                            tpe = t.op("tensor",
                                       lambda e, p=pb, sl=slot, ic=ic, ng=ng:
                                       e.matmul(p[:], wqkb[:, sl * 1024 + ic * 128: sl * 1024 + (ic + 1) * 128],
                                                xn_fm[:, ic * L + ng * 512: ic * L + (ng + 1) * 512],
                                                start=(ic == 0), stop=(ic == IC - 1)),
                                       deps=[twl, bank_free, xn_tok] if ic == 0 else [],
                                       inc="pe" if ic == IC - 1 else None)
                        tev = t.op("vector",
                                   lambda e, p=pb, dd=dstb, oc=oc, ng=ng:
                                   e.tensor_copy(dd[:, oc * L + ng * 512: oc * L + (ng + 1) * 512], p[:]),
                                   deps=[tpe, C.war.get((which + "_fm", 0)), C.war.get(("pB2", 0))], inc="ve")
                        C.war[("pbank", bank)] = tev
                    C.war[("wqkb", slot)] = t.tok("pe")
            qk_ready = t.tok("ve")
            for h in range(H1):
                hp, hsub = h // 2, h % 2
                for dvg in range(2):
                    wv_free = C.war.get(("wvb", 0))
                    dma(C, "sync", "wvb", wvb[:].rearrange("p (c o) -> p c o", o=512),
                        d_wvt[:, h * 1024 + dvg * 512: h * 1024 + (dvg + 1) * 512]
                        .rearrange("(c p) o -> p c o", p=128),
                        deps=[wv_free, C.war.get(("pWE", 0))] + pp)
                    twv = t.tok("d_wvb")
                    for tt in range(TT):
                        bank = (2, 3, 6, 7)[tt % 4]
                        pb = C.P[bank]
                        bank_free = C.war.get(("pbank", bank))
                        tpe = None
                        for ic in range(IC):
                            tpe = t.op("tensor",
                                       lambda e, p=pb, ic=ic, tt=tt:
                                       e.matmul(p[:], xn_fm[:, ic * L + tt * 128: ic * L + (tt + 1) * 128],
                                                wvb[:, ic * 512:(ic + 1) * 512],
                                                start=(ic == 0), stop=(ic == IC - 1)),
                                       deps=[twv, bank_free, xn_tok] if ic == 0 else [],
                                       inc="pe" if ic == IC - 1 else None)
                        tev = t.op("vector",
                                   lambda e, p=pb, tt=tt, dvg=dvg:
                                   e.tensor_copy(v_h[:, tt * 1024 + dvg * 512: tt * 1024 + (dvg + 1) * 512], p[:]),
                                   deps=[tpe, C.war.get(("v_h", 0)), C.war.get(("pB1", 0))], inc="ve")
                        C.war[("pbank", bank)] = tev
                    C.war[("wvb", 0)] = t.tok("pe")
                vh_ready = t.tok("ve")
                for ng in range(4):
                    et_rd = C.war.get(("et", 0))
                    exp_toks = []
                    for mt in range(TT):
                        bank = mt % 4
                        pb = C.P[bank]
                        bank_free = C.war.get(("pbank", bank))
                        tps = t.op("tensor",
                                   lambda e, p=pb, hp=hp, hsub=hsub, mt=mt, ng=ng:
                                   e.matmul(p[:], k_fm[hsub * 64:(hsub + 1) * 64, hp * L + mt * 128: hp * L + (mt + 1) * 128],
                                            q_fm[hsub * 64:(hsub + 1) * 64, hp * L + ng * 512: hp * L + (ng + 1) * 512],
                                            start=True, stop=True),
                                   deps=[qk_ready, bank_free, et_rd], inc="pe")
                        tex = t.op("scalar",
                                   lambda e, p=pb, mt=mt: e.activation(et[:, mt * 512:(mt + 1) * 512], p[:], AF.Exp),
                                   deps=[tps, C.war.get(("pWE", 0))], inc="sc")
                        exp_toks.append(tex)
                        C.war[("pbank", bank)] = tex
                    for nl in range(4):
                        nt = ng * 4 + nl
                        sb_ = C.P[4]
                        s_free = C.war.get(("pbank", 4))
                        tsm = None
                        for mt in range(TT):
                            tsm = t.op("tensor",
                                       lambda e, p=sb_, mt=mt, nl=nl:
                                       e.matmul(p[:, 0:1], et[:, mt * 512 + nl * 128: mt * 512 + (nl + 1) * 128],
                                                ones[:], start=(mt == 0), stop=(mt == TT - 1)),
                                       deps=([s_free, ones_tok] if mt == 0 else []) + [exp_toks[mt]],
                                       inc="pe" if mt == TT - 1 else None)
                        trc = t.op("vector", lambda e, p=sb_: e.reciprocal(rc[:], p[:, 0:1]),
                                   deps=[tsm, C.war.get(("rc", 0))], inc="ve")
                        C.war[("pbank", 4)] = trc
                        tve = None
                        for dg in range(2):
                            bank = 5 + dg
                            pb = C.P[bank]
                            bank_free = C.war.get(("pbank", bank))
                            tpv = None
                            for mt in range(TT):
                                tpv = t.op("tensor",
                                           lambda e, p=pb, mt=mt, nl=nl, dg=dg:
                                           e.matmul(p[:], et[:, mt * 512 + nl * 128: mt * 512 + (nl + 1) * 128],
                                                    v_h[:, mt * 1024 + dg * 512: mt * 1024 + (dg + 1) * 512],
                                                    start=(mt == 0), stop=(mt == TT - 1)),
                                           deps=([vh_ready, bank_free] if mt == 0 else []) + [exp_toks[mt]],
                                           inc="pe" if mt == TT - 1 else None)
                            xslc = C.x_sb[:, nt * 1024 + dg * 512: nt * 1024 + (dg + 1) * 512]
                            tve = t.op("vector",
                                       lambda e, o=xslc, p=pb:
                                       e.scalar_tensor_tensor(o, p[:], rc[:, 0:1], o, ALU.mult, ALU.add),
                                       deps=[tpv, trc, C.x_tok[nt]], inc="ve")
                            C.war[("pbank", bank)] = tve
                            C.x_tok[nt] = tve
                        C.war[("rc", 0)] = tve
                    C.war[("et", 0)] = t.tok("pe")
                C.war[("v_h", 0)] = t.tok("pe")
            for k_ in (("lnfm", 0), ("lnfm", 1), ("pB1", 0), ("pB2", 0), ("pWE", 0)):
                C.war[k_] = t.tok("pe")
            swt_sb = B2[:, 0:8192]
            xn_g = XN[:, 0:4096]
            xn_gB = XN[:, 4096:8192]
            dma(C, "sync", "swt", swt_sb[:].rearrange("p (c o) -> p c o", o=1024),
                d_swt.rearrange("(c p) o -> p c o", p=128), deps=pp + [C.war.get(("swtsb", 0)), C.war.get(("pB2", 0))])
            swt_ld = t.tok("d_swt")
            C.war[("pB2dma", 0)] = swt_ld
            xgs = [xn_g, xn_gB]
            xn_next = ln_to_fm(C, [0, 1, 2, 3], xgs[0], 512, lambda tt: tt * 128, war_key=0)
            for tg in range(4):
                xg = xgs[tg % 2]
                xn_tok = xn_next
                if tg < 3:
                    nts = list(range((tg + 1) * 4, (tg + 1) * 4 + 4))
                    xn_next = ln_to_fm(C, nts, xgs[(tg + 1) % 2], 512,
                                       lambda tt, tg=tg: (tt - (tg + 1) * 4) * 128,
                                       war_key=(tg + 1) % 2)
                for tl in range(4):
                    tt = tg * 4 + tl
                    for og in range(2):
                        bank = tl * 2 + og
                        pb = C.P[bank]
                        bank_free = C.war.get(("pbank", bank))
                        tpe = None
                        for ic in range(IC):
                            tpe = t.op("tensor",
                                       lambda e, p=pb, ic=ic, tl=tl, og=og, xg=xg, w=swt_sb:
                                       e.matmul(p[:], xg[:, ic * 512 + tl * 128: ic * 512 + (tl + 1) * 128],
                                                w[:, ic * 1024 + og * 512: ic * 1024 + (og + 1) * 512],
                                                start=(ic == 0), stop=(ic == IC - 1)),
                                       deps=[xn_tok, swt_ld, bank_free] if ic == 0 else [],
                                       inc="pe" if ic == IC - 1 else None)
                        xslc = C.x_sb[:, tt * 1024 + og * 512: tt * 1024 + (og + 1) * 512]
                        tve = t.op("vector", lambda e, o=xslc, p=pb: e.tensor_add(o, o, p[:]),
                                   deps=[tpe, C.x_tok[tt]], inc="ve")
                        C.war[("pbank", bank)] = tve
                        C.x_tok[tt] = tve
                C.war[("lnfm", tg % 2)] = t.tok("pe")
            for k_ in (("lnfm", 0), ("lnfm", 1), ("pB1", 0), ("pB2", 0), ("pWE", 0)):
                C.war[k_] = t.tok("pe")
            C.war[("swtsb", 0)] = t.tok("pe")
            store_x(C, d_xo, s)
            t.flush()
        t.op("sync", lambda e: e.wait_ge(C.t.sem["d_xs"], C.t.val["d_xs"]))
        t.flush()
    return nc
